# version 01
# baseline (speedup 1.0000x reference)
import contextlib
import numpy as np
import ml_dtypes
import concourse.bass as bass
import concourse.mybir as mybir
from concourse.bass_utils import run_bass_kernel_spmd

F32 = mybir.dt.float32
BF16 = mybir.dt.bfloat16
AF = mybir.ActivationFunctionType
ALU = mybir.AluOpType
AX = mybir.AxisListType

NCORES = 8
T = 2048
D = 2048
NK = 16
NTS = 4
TS = 512
NPAIR = 16
MAGIC = 12582912.0
EXPB = -18.0
EPS = 1e-5
C2048 = float(1.0 / np.sqrt(np.float64(2048.0)))
C8192 = float(1.0 / np.sqrt(np.float64(8192.0)))
NELW = 8192.0 * 2048.0

ENGS = ("tensor", "vector", "scalar", "gpsimd", "sync")
NO_CC = False
DBG_WS = False


def _hadamard(n):
    h = np.array([[1.0]])
    while h.shape[0] < n:
        h = np.block([[h, h], [h, -h]])
    return h


class KB:

    def __init__(self, upto=99, dbg=()):
        self.upto = upto
        self.dbg = list(dbg)
        nc = self.nc = bass.Bass("TRN2", target_bir_lowering=False,
                                 num_devices=NCORES)
        self.q = {e: [] for e in ENGS}
        self.es = contextlib.ExitStack()
        self.sems = {}
        for s in ("cc", "cc2", "stn", "svec", "sact", "sgps"):
            self.sems[s] = [self.es.enter_context(nc.semaphore(s)), 0]
        self.dbg_tiles = {}

    FENCE = True
    _ENG_SEM = {"vector": "svec", "scalar": "sact", "gpsimd": "sgps"}

    def E(self, eng, build, sem=None, k=1, nofence=False):
        fence = self.FENCE and k == 1 and eng in self._ENG_SEM and not nofence
        if fence and sem is None:
            sem = self._ENG_SEM[eng]
        v = None
        if sem is not None:
            s = self.sem_of(sem)
            s[1] += k
            v = s[1]
            h = s[0]

            def go(e, build=build, h=h, k=k):
                build(e).then_inc(h, k)
        else:
            def go(e, build=build):
                build(e)
        self.q[eng].append(go)
        if fence:
            self.W(eng, sem, v)
        return v

    def W(self, eng, sem, v):
        if v is None or v <= 0:
            return
        h = self.sem_of(sem)[0]
        self.q[eng].append(lambda e, h=h, v=v: e.wait_ge(h, v))

    def sem_of(self, name):
        if name not in self.sems:
            self.sems[name] = [self.es.enter_context(self.nc.semaphore(name)), 0]
        return self.sems[name]

    def dma(self, eng, stream, dst, src, **kw):
        self.sem_of(stream)
        v = self.E(eng, lambda e, d=dst, s=src, kw=kw: e.dma_start(d, s, **kw),
                   stream, 16)
        return (stream, v)

    def WD(self, eng, tok):
        if tok is None:
            return
        self.W(eng, tok[0], tok[1])

    def val(self, sem):
        return self.sems[sem][1]

    def barrier(self):
        vals = [(s, sv[1]) for s, sv in list(self.sems.items())]
        for eng in ENGS:
            for s, v in vals:
                self.W(eng, s, v)

    _uid = 0

    def sb(self, name, shape, dtype, scope=None):
        KB._uid += 1
        t = (scope or self.es).enter_context(
            self.nc.sbuf_tensor(f"{name}_u{KB._uid}", shape, dtype))
        return t

    def play(self):
        q = self.q
        with self.nc.Block() as blk:
            @blk.tensor
            def _(e):
                for f in q["tensor"]:
                    f(e)

            @blk.vector
            def _(e):
                for f in q["vector"]:
                    f(e)

            @blk.scalar
            def _(e):
                for f in q["scalar"]:
                    f(e)

            @blk.gpsimd
            def _(e):
                for f in q["gpsimd"]:
                    f(e)

            @blk.sync
            def _(e):
                for f in q["sync"]:
                    f(e)

    def declare(self):
        nc = self.nc
        dram = self.dram = {}
        for name, shape in [
                ("x_fm", [D, T]), ("wq_b", [16, 128, 2048]), ("wk_b", [8, 128, 2048]),
                ("wv_t", [2048, 512]), ("wo_b", [16, 128, 2048]),
                ("tg_b", [64, 128, 2048]), ("tu_b", [64, 128, 2048]),
                ("td_b", [16, 128, 8192]), ("wmod_sl", [2048, 1536]),
                ("c_t", [128, 128]), ("onehot", [8, 1]),
                ("wgs", [8, 128, 2048]), ("wus", [8, 128, 2048]), ("wds", [8, 128, 2048]),
                ("ident_f", [128, 128]), ("cosq", [128, T]), ("sinq", [128, T]),
                ("cosk", [128, T]), ("sink", [128, T]), ("ccols", [128, 8]),
                ("ones1f", [1, 128])]:
            dram[name] = nc.dram_tensor(name, shape, F32, kind="ExternalInput")
        for name, shape in [
                ("h128_b", [128, 128]), ("i8h16_b", [128, 128]), ("i2h64_b", [128, 128]),
                ("pswap_b", [128, 128]), ("masks", [128, 4 * TS]),
                ("onescolb", [128, 1])]:
            dram[name] = nc.dram_tensor(name, shape, BF16, kind="ExternalInput")
        dram["out_fm"] = nc.dram_tensor("out_fm", [D, T], F32, kind="ExternalOutput")
        if self.dbg:
            dram["dbg_d"] = nc.dram_tensor("dbg_d", [128, 65536], F32,
                                           kind="ExternalOutput")
        dram["xprime"] = nc.dram_tensor("xprime", [D, T], F32)
        dram["u1_d"] = nc.dram_tensor("u1_d", [128, 16, T], BF16)
        dram["u2_d"] = nc.dram_tensor("u2_d", [128, 64, T], BF16)
        dram["tdq_d"] = nc.dram_tensor("tdq_d", [16, 128, 8192], BF16)
        dram["opair_d"] = nc.dram_tensor("opair_d", [16, 128, T], BF16)
        dram["ag_in"] = nc.dram_tensor("ag_in", [8, 1536], F32)
        dram["ag_out"] = nc.dram_tensor("ag_out", [64, 1536], F32)
        dram["ar_in"] = nc.dram_tensor("ar_in", [1, 4], F32)
        dram["ar_out"] = nc.dram_tensor("ar_out", [1, 4], F32)

        sb = self.sb
        self.bigA1 = sb("bigA1", [128, 32768], BF16)
        self.sid_f = sb("sid_f", [128, 128], F32)
        self.sh128 = sb("sh128", [128, 128], BF16)
        self.si8h16 = sb("si8h16", [128, 128], BF16)
        self.si2h64 = sb("si2h64", [128, 128], BF16)
        self.spswap = sb("spswap", [128, 128], BF16)
        self.sccols = sb("sccols", [128, 8], F32)
        self.sones1 = sb("sones1", [1, 128], F32)
        self.sonescol = sb("sonescol", [128, 1], BF16)
        self.sonehot = sb("sonehot", [8, 1], F32)
        self.mod96 = sb("mod96", [128, 96], F32)
        self.onePca = sb("onePca", [128, 16], F32)
        self.onePcb = sb("onePcb", [128, 16], F32)
        self.gacols = sb("gacols", [128, 16], F32)
        self.gbcols = sb("gbcols", [128, 16], F32)
        self.wsrow = sb("wsrow", [1, 16], F32)
        self.wsg_i = sb("wsg_i", [128, 1], F32)
        self.wsu_i = sb("wsu_i", [128, 1], F32)
        self.wsd_i = sb("wsd_i", [128, 1], F32)
        self.srow_su = sb("srow_su", [1, T], F32)
        self.bc1 = sb("bc1", [128, T], F32)
        self.bc2 = sb("bc2", [128, T], F32)
        self.tmp1 = sb("tmp1", [128, TS], F32)
        self.tmp2 = sb("tmp2", [128, TS], F32)
        self.mcol = sb("mcol", [128, 16], F32)

        self.ps = [self.es.enter_context(self.nc.psum_tensor(f"ps{i}", [128, TS], F32))
                   for i in range(6)]
        self.ps_row = self.es.enter_context(self.nc.psum_tensor("ps_row", [1, TS], F32))
        self.ps_row2 = self.es.enter_context(self.nc.psum_tensor("ps_row2", [1, TS], F32))

    def hA(self, i):
        return self.bigA1[:, i * 2048:(i + 1) * 2048]

    def alloc_xstr(self, scope):
        self.xstr = [self.sb(f"xstr{i}", [128, T], F32, scope) for i in range(2)]

    def alloc_rows(self, scope, width=T):
        self.srow_a = self.sb("srow_a", [1, width], F32, scope)
        self.srow_b = self.sb("srow_b", [1, width], F32, scope)
        self.srow_c = self.sb("srow_c", [1, width], F32, scope)
        self.srow_d = self.sb("srow_d", [1, width], F32, scope)

    def p0_consts(self):
        d = self.dram
        pairs = [(self.sid_f, d["ident_f"]), (self.sh128, d["h128_b"]),
                 (self.si8h16, d["i8h16_b"]), (self.si2h64, d["i2h64_b"]),
                 (self.spswap, d["pswap_b"]), (self.sccols, d["ccols"]),
                 (self.sonehot, d["onehot"]), (self.sonescol, d["onescolb"]),
                 (self.sones1, d["ones1f"])]
        for dst, src in pairs:
            tok = self.dma("sync", "cst", dst[:, :], src.ap())
        self.CONSTS_D = tok

    def p1_mod_ws(self, scope):
        E, W, dma = self.E, self.W, self.dma
        d, ps = self.dram, self.ps
        siluc = self.sb("siluc", [128, 128], F32, scope)
        siluc_b = self.sb("siluc_b", [128, 128], BF16, scope)
        modsl = self.sb("modsl", [8, 1536], F32, scope)
        exsl = [self.sb(f"exsl{i}", [8, TS], F32, scope) for i in range(2)]
        rowbuf = [self.sb(f"rowbuf{i}", [1, TS], F32, scope) for i in range(2)]
        wsacc = self.sb("wsacc", [128, 4], F32, scope)
        wsred = self.sb("wsred", [128, 1], F32, scope)
        if DBG_WS:
            wsdbg = self.sb("wsdbg", [128, 24], F32, scope)
            self.tap("wsdbg", wsdbg, 128, 24)
        gtmp = self.sb("gtmp", [128, 64], F32, scope)
        wstr = [self.sb(f"p1w{i}", [128, 2048], F32, scope) for i in range(2)]
        wmf = [self.sb(f"p1m{i}", [128, 1536], F32, scope) for i in range(2)]
        wmb = [self.sb(f"p1mb{i}", [128, 1536], BF16, scope) for i in range(2)]

        cdone = self.dma("sync", "cst", siluc[:, :], d["c_t"].ap())
        self.WD("scalar", cdone)
        sa = E("scalar", lambda e: e.activation(siluc[:, :], siluc[:, :], AF.Silu),
               "sact")
        W("gpsimd", "sact", sa)
        sg = E("gpsimd", lambda e: e.tensor_copy(siluc_b[:, :], siluc[:, :]), "sgps")

        slab_d, slab_v = [], []
        for t24 in range(24):
            m, i = divmod(t24, 8)
            src = (d["wgs"], d["wus"], d["wds"])[m]
            if t24 >= 2:
                W("sync", "svec", slab_v[t24 - 2])
            slab_d.append(dma("sync", f"sw{t24 % 2}", wstr[t24 % 2][:, :],
                              src.ap()[i, :, :]))
            self.WD("vector", slab_d[t24])
            rdv = E("vector", lambda e, b=t24 % 2: e.tensor_reduce(
                wsred[:, :], wstr[b][:, :], AX.X, ALU.add,
                apply_absolute_value=True), "svec")
            W("vector", "svec", rdv)
            if DBG_WS:
                E("vector", lambda e, t24=t24: e.tensor_copy(
                    wsdbg[:, t24:t24 + 1], wsred[:, :]))
            if i == 0:
                slab_v.append(E("vector", lambda e, m=m: e.tensor_copy(
                    wsacc[:, m:m + 1], wsred[:, :]), "svec"))
            else:
                slab_v.append(E("vector", lambda e, m=m: e.tensor_tensor(
                    wsacc[:, m:m + 1], wsacc[:, m:m + 1], wsred[:, :], ALU.add), "svec"))
        self.WD("tensor", self.CONSTS_D)
        W("tensor", "svec", slab_v[-1])
        wst = E("tensor", lambda e: e.matmul(self.ps_row[0:1, 0:4], self.sccols[:, 2:3],
                                             wsacc[:, :], start=True, stop=True), "stn")
        W("vector", "stn", wst)
        wsv = E("vector", lambda e: e.tensor_copy(self.wsrow[0:1, 0:4],
                                                  self.ps_row[0:1, 0:4]), "svec")

        W("tensor", "sgps", sg)
        wm_d, wm_v, wm_t = [], [], []
        for k in range(16):
            if k >= 2:
                W("sync", "svec", wm_v[k - 2])
            wm_d.append(dma("sync", f"wm{k % 2}", wmf[k % 2][:, :],
                            d["wmod_sl"].ap()[k * 128:(k + 1) * 128, :]))
            self.WD("vector", wm_d[k])
            if k >= 2:
                W("vector", "stn", wm_t[k - 2])
            wm_v.append(E("vector", lambda e, b=k % 2: e.tensor_copy(
                wmb[b][:, :], wmf[b][:, :]), "svec"))
            W("tensor", "svec", wm_v[k])
            for n3 in range(3):
                v = E("tensor", lambda e, k=k, n3=n3: e.matmul(
                    ps[n3][0:8, :], siluc_b[:, k * 8:(k + 1) * 8],
                    wmb[k % 2][:, n3 * 512:(n3 + 1) * 512],
                    start=(k == 0), stop=(k == 15)), "stn")
            wm_t.append(v)
        W("vector", "stn", wm_t[-1])
        for n3 in range(3):
            mev = E("vector", lambda e, n3=n3: e.tensor_copy(
                modsl[0:8, n3 * 512:(n3 + 1) * 512], ps[n3][0:8, :]), "svec")

        W("gpsimd", "svec", mev)
        W("gpsimd", "svec", wsv)
        dma("gpsimd", "ccb", d["ag_in"].ap(), modsl[0:8, :])
        dma("gpsimd", "ccb", d["ar_in"].ap()[0:1, :], self.wsrow[0:1, 0:4])
        W("gpsimd", "ccb", self.val("ccb"))
        if NO_CC:
            for rr in range(8):
                self.dma("gpsimd", "ccf", d["ag_out"].ap()[rr * 8:rr * 8 + 8, :],
                         d["ag_in"].ap())
            self.dma("gpsimd", "ccf", d["ar_out"].ap(), d["ar_in"].ap())
        else:
            cch, cc2h = self.sems["cc"][0], self.sems["cc2"][0]
            E("gpsimd", lambda e: e.collective_compute(
                "AllGather", ALU.bypass, replica_groups=[list(range(NCORES))],
                ins=[d["ag_in"].ap().opt()],
                outs=[d["ag_out"].ap().opt()]).then_inc(cch), nofence=True)
            E("gpsimd", lambda e: e.collective_compute(
                "AllReduce", ALU.add, replica_groups=[list(range(NCORES))],
                ins=[d["ar_in"].ap().opt()],
                outs=[d["ar_out"].ap().opt()]).then_inc(cc2h), nofence=True)
            self.sems["cc"][1] = 1
            self.sems["cc2"][1] = 1
            W("gpsimd", "cc", 1)
            W("gpsimd", "cc2", 1)
        if NO_CC:
            W("gpsimd", "ccf", self.val("ccf"))
        ccd = dma("gpsimd", "ccb", self.wsrow[0:1, 4:8], d["ar_out"].ap()[0:1, :])

        self.WD("sync", ccd)
        ex_d, ex_t, ex_v, tp_t = [], [], [], []
        for n in range(24):
            if n >= 2:
                W("sync", "stn", ex_t[n - 2])
            r0, js = n // 3, (n % 3) * 512
            ex_d.append(dma("sync", f"ex{n % 2}", exsl[n % 2][0:8, :],
                            d["ag_out"].ap()[r0 * 8:r0 * 8 + 8, js:js + 512]))
            self.WD("tensor", ex_d[n])
            if n >= 2:
                W("tensor", "svec", ex_v[n - 2])
            ex_t.append(E("tensor", lambda e, n=n: e.matmul(
                ps[n % 2][0:1, :], self.sonehot[0:8, 0:1],
                exsl[n % 2][0:8, :], start=True, stop=True), "stn"))
            W("vector", "stn", ex_t[n])
            if n >= 2:
                W("vector", "stn", tp_t[n - 2])
            ex_v.append(E("vector", lambda e, n=n: e.tensor_copy(
                rowbuf[n % 2][0:1, :], ps[n % 2][0:1, :]), "svec"))
            W("tensor", "svec", ex_v[n])
            for j4 in range(4):
                tp = E("tensor", lambda e, n=n, j4=j4: e.transpose(
                    ps[5][:, n * 4 + j4:n * 4 + j4 + 1],
                    rowbuf[n % 2][0:1, j4 * 128:(j4 + 1) * 128],
                    self.sones1[0:1, 0:1]), "stn")
            tp_t.append(tp)
        W("vector", "stn", tp_t[-1])
        E("vector", lambda e: e.tensor_copy(self.mod96[:, :], ps[5][:, 0:96]))
        E("vector", lambda e: e.tensor_scalar(self.onePca[:, :], self.mod96[:, 16:32],
                                              1.0, None, ALU.add))
        E("vector", lambda e: e.tensor_scalar(self.onePcb[:, :], self.mod96[:, 64:80],
                                              1.0, None, ALU.add))
        for dst, off, lo in ((self.gacols, 32, 0), (self.gbcols, 80, 16)):
            E("vector", lambda e, off=off, lo=lo: e.tensor_scalar(
                gtmp[:, lo:lo + 16], self.mod96[:, off:off + 16], 0.0, None, ALU.is_ge))
            E("vector", lambda e, lo=lo: e.tensor_scalar(
                gtmp[:, lo:lo + 16], gtmp[:, lo:lo + 16], 2.0, -1.0, ALU.mult, ALU.add))
            E("vector", lambda e, off=off, lo=lo: e.tensor_scalar(
                gtmp[:, 32 + lo:48 + lo], self.mod96[:, off:off + 16], -1.0, None,
                ALU.mult))
            E("vector", lambda e, off=off, lo=lo: e.tensor_tensor(
                gtmp[:, 32 + lo:48 + lo], gtmp[:, 32 + lo:48 + lo],
                self.mod96[:, off:off + 16], ALU.max))
            E("vector", lambda e, lo=lo: e.tensor_scalar(
                gtmp[:, 32 + lo:48 + lo], gtmp[:, 32 + lo:48 + lo], 0.25, None, ALU.max))
            E("vector", lambda e, dst=dst, lo=lo: e.tensor_tensor(
                dst[:, :], gtmp[:, 32 + lo:48 + lo], gtmp[:, lo:lo + 16], ALU.mult))
        E("vector", lambda e: e.tensor_scalar(self.wsrow[0:1, 8:11], self.wsrow[0:1, 4:7],
                                              1.0 / NELW, EPS, ALU.mult, ALU.max))
        mv = E("vector", lambda e: e.reciprocal(self.wsrow[0:1, 12:15],
                                                self.wsrow[0:1, 8:11]), "svec")
        W("vector", "svec", mv)
        W("tensor", "svec", mv)
        for i in range(3):
            wb = E("tensor", lambda e, i=i: e.matmul(
                ps[4][:, i:i + 1], self.sones1[0:1, :], self.wsrow[0:1, 12 + i:13 + i],
                start=True, stop=True), "stn")
        W("vector", "stn", wb)
        E("vector", lambda e: e.tensor_copy(self.wsg_i[:, :], ps[4][:, 0:1]))
        E("vector", lambda e: e.tensor_copy(self.wsu_i[:, :], ps[4][:, 1:2]))
        E("vector", lambda e: e.tensor_copy(self.wsd_i[:, :], ps[4][:, 2:3]), "svec")
        self.tap("mod96", self.mod96, 128, 96)
        self.tap("wsrow", self.wsrow, 1, 16)
        self.tap("wsacc", wsacc, 128, 4)

    def ln_stats(self, src_dram, xbf, xsq):
        srow_a, srow_b, srow_c, srow_d = (self.srow_a, self.srow_b,
                                          self.srow_c, self.srow_d)
        xstr = self.xstr
        E, W, dma = self.E, self.W, self.dma
        ps = self.ps
        sq_ps = [ps[4][0:1, :], ps[5][0:1, :], self.ps_row[0:1, :], self.ps_row2[0:1, :]]
        cast_v, sq_a, mm_t = [], [], []
        for c in range(NK):
            if c >= 2:
                W("sync", "svec", cast_v[c - 2])
            xd = dma("sync", f"xs{c % 2}", xstr[c % 2][:, :],
                     src_dram.ap()[c * 128:(c + 1) * 128, :])
            self.WD("vector", xd)
            if c >= 2:
                W("vector", "stn", mm_t[c - 2])
            cast_v.append(E("vector", lambda e, b=c % 2: e.tensor_copy(
                xbf[b][:, :], xstr[b][:, :]), "svec"))
            W("scalar", "svec", cast_v[c])
            if c >= 2:
                W("scalar", "stn", mm_t[c - 2])
            sq_a.append(E("scalar", lambda e, b=c % 2: e.activation(
                xsq[b][:, :], xbf[b][:, :], AF.Square), "sact"))
            W("tensor", "sact", sq_a[c])
            for t4 in range(NTS):
                E("tensor", lambda e, c=c, t4=t4: e.matmul(
                    ps[t4][0:1, :], self.sonescol[:, :],
                    xbf[c % 2][:, t4 * TS:(t4 + 1) * TS],
                    start=(c == 0), stop=(c == NK - 1)))
                mm = E("tensor", lambda e, c=c, t4=t4, tgt=sq_ps[t4]: e.matmul(
                    tgt, self.sonescol[:, :], xsq[c % 2][:, t4 * TS:(t4 + 1) * TS],
                    start=(c == 0), stop=(c == NK - 1)), "stn")
            mm_t.append(mm)
        W("vector", "stn", mm_t[-1])
        for t4 in range(NTS):
            E("vector", lambda e, t4=t4: e.tensor_copy(
                srow_a[0:1, t4 * TS:(t4 + 1) * TS], ps[t4][0:1, :]))
            E("vector", lambda e, t4=t4, s=sq_ps[t4]: e.tensor_copy(
                srow_b[0:1, t4 * TS:(t4 + 1) * TS], s))
        E("vector", lambda e: e.tensor_scalar(srow_a[0:1, :], srow_a[0:1, :],
                                              1.0 / D, None, ALU.mult))
        E("vector", lambda e: e.tensor_tensor(srow_c[0:1, :], srow_a[0:1, :],
                                              srow_a[0:1, :], ALU.mult))
        rv = E("vector", lambda e: e.scalar_tensor_tensor(
            srow_b[0:1, :], srow_b[0:1, :], 1.0 / D, srow_c[0:1, :],
            ALU.mult, ALU.subtract), "svec")
        W("scalar", "svec", rv)
        ra = E("scalar", lambda e: e.activation(
            srow_d[0:1, :], srow_b[0:1, :], AF.Sqrt,
            bias=self.sccols[0:1, 1:2]), "sact")
        W("vector", "sact", ra)
        rv2 = E("vector", lambda e: e.reciprocal(srow_b[0:1, :],
                                                 srow_d[0:1, :]), "svec")
        W("vector", "svec", rv2)
        self.bcast_rows([(srow_a, self.bc1, T), (srow_b, self.bc2, T)], rv2)

    def bcast_rows(self, jobs, ready_v):
        E, W = self.E, self.W
        ps = self.ps
        W("tensor", "svec", ready_v)
        i = 0
        last_ev = None
        for row, dstt, width in jobs:
            for t4 in range(width // TS):
                bank = i % 4
                if last_ev is not None and i >= 4:
                    W("tensor", "svec", evs[i - 4])
                mm = E("tensor", lambda e, row=row, t4=t4, bank=bank: e.matmul(
                    ps[bank][:, :], self.sones1[0:1, :],
                    row[0:1, t4 * TS:(t4 + 1) * TS], start=True, stop=True), "stn")
                W("vector", "stn", mm)
                ev = E("vector", lambda e, dstt=dstt, t4=t4, bank=bank: e.tensor_copy(
                    dstt[:, t4 * TS:(t4 + 1) * TS], ps[bank][:, :]), "svec")
                if i == 0:
                    evs = {}
                evs[i] = ev
                last_ev = ev
                i += 1
        return last_ev

    def p4_h(self, src_dram, scale_tile, bias_off, dst_fn, xln):
        xstr = self.xstr
        E, W, dma = self.E, self.W, self.dma
        sub_v, act_a = [], []
        for c in range(NK):
            if c >= 2:
                W("sync", "svec", sub_v[c - 2])
            xd = dma("sync", f"xs{c % 2}", xstr[c % 2][:, :],
                     src_dram.ap()[c * 128:(c + 1) * 128, :])
            self.WD("vector", xd)
            if c >= 1:
                W("vector", "sact", act_a[c - 1])
            E("vector", lambda e, b=c % 2: e.tensor_tensor(
                xln[:, :], xstr[b][:, :], self.bc1[:, :], ALU.subtract))
            sub_v.append(E("vector", lambda e: e.tensor_tensor(
                xln[:, :], xln[:, :], self.bc2[:, :], ALU.mult), "svec"))
            W("scalar", "svec", sub_v[c])
            act_a.append(E("scalar", lambda e, c=c, st=scale_tile, bo=bias_off:
                           e.activation(dst_fn(c)[:, :], xln[:, :], AF.Identity,
                                        scale=st[:, c:c + 1],
                                        bias=self.mod96[:, bo + c:bo + c + 1]), "sact"))
        return act_a[-1]

    def p5_kv(self, scope):
        E, W, dma = self.E, self.W, self.dma
        d, ps = self.dram, self.ps
        self.kfm = [self.sb(f"kfm{i}", [128, T], BF16, scope) for i in range(8)]
        self.vaug = [self.sb(f"vaug{i}", [128, 520], BF16, scope) for i in range(16)]
        self.qpair = self.sb("qpair", [128, T], BF16, scope)
        self.qraw = self.sb("qraw", [128, TS], BF16, scope)
        wv_all = self.sb("wv_all", [128, 8192], BF16, scope)
        sc5 = contextlib.ExitStack()
        rope_ck = self.bc1
        rope_sk = self.bc2
        wstr = [self.sb("p5w0", [128, 2048], F32, sc5)]
        wstr.append(wstr[0])
        wstrb = [self.sb("p5wb0", [128, 2048], BF16, sc5)]
        wstrb.append(wstrb[0])

        ckd = dma("sync", "cst", rope_ck[:, :], d["cosk"].ap())
        skd = dma("sync", "cst", rope_sk[:, :], d["sink"].ap())

        wk_d, wk_g, mm_last = [], [], []
        rope_done = []
        for o in range(8):
            if o >= 1:
                W("sync", "sgps", wk_g[o - 1])
            wk_d.append(dma("sync", "wk", wstr[o % 2][:, :], d["wk_b"].ap()[o, :, :]))
            self.WD("gpsimd", wk_d[o])
            if o >= 1:
                W("gpsimd", "stn", mm_last[o - 1])
            wk_g.append(E("gpsimd", lambda e, b=o % 2: e.tensor_copy(
                wstrb[b][:, :], wstr[b][:, :]), "sgps"))
            W("tensor", "sgps", wk_g[o])
            if o == 0:
                self.WD("tensor", skd)
                self.WD("vector", skd)
                swv_prev = None
            for t4 in range(4):
                if t4 >= 2 or o > 0:
                    W("tensor", "svec", rope_done[-2] if len(rope_done) >= 2 else None)
                for k in range(16):
                    mmv = E("tensor", lambda e, o=o, t4=t4, k=k: e.matmul(
                        ps[t4 % 2][:, :], wstrb[o % 2][:, k * 128:(k + 1) * 128],
                        self.hA(k)[:, t4 * TS:(t4 + 1) * TS],
                        start=(k == 0), stop=(k == 15)), "stn")
                W("scalar", "stn", mmv)
                if swv_prev is not None:
                    W("scalar", "stn", swv_prev)
                arv = E("scalar", lambda e, t4=t4: e.activation(
                    self.qraw[:, :], ps[t4 % 2][:, :], AF.Copy), "sact")
                W("tensor", "sact", arv)
                swv_prev = E("tensor", lambda e, t4=t4: e.matmul(
                    ps[2 + t4 % 2][:, :], self.spswap[:, :], self.qraw[:, :],
                    start=True, stop=True), "stn")
                W("vector", "stn", swv_prev)
                E("vector", lambda e, t4=t4: e.tensor_tensor(
                    self.tmp1[:, 0:TS], ps[t4 % 2][:, :],
                    rope_ck[:, t4 * TS:(t4 + 1) * TS], ALU.mult))
                E("vector", lambda e, t4=t4: e.tensor_tensor(
                    self.tmp2[:, 0:TS], ps[2 + t4 % 2][:, :],
                    rope_sk[:, t4 * TS:(t4 + 1) * TS], ALU.mult))
                rope_done.append(E("vector", lambda e, o=o, t4=t4: e.tensor_tensor(
                    self.kfm[o][:, t4 * TS:(t4 + 1) * TS], self.tmp1[:, 0:TS],
                    self.tmp2[:, 0:TS], ALU.add), "svec"))
            mm_last.append(self.val("stn"))
        self.K_DONE = rope_done[-1]

        for i in range(16):
            for h8 in range(8):
                E("gpsimd", lambda e, i=i, h8=h8: e.memset(
                    self.vaug[i][:, h8 * 65 + 64:h8 * 65 + 65], 1.0))
        vones = E("gpsimd", lambda e, i=15: e.tensor_copy(
            self.vaug[15][:, 519:520], self.vaug[15][:, 519:520]), "sgps")
        W("sync", "sgps", wk_g[-1])
        for qv4 in range(4):
            if qv4 >= 1:
                W("sync", "sgps", wvg)
            wvd = dma("sync", "wv", wstr[0].ap().rearrange("p (a v) -> p a v", a=4),
                      d["wv_t"].ap()[qv4 * 512:(qv4 + 1) * 512, :]
                      .rearrange("(a p) v -> p a v", p=128))
            self.WD("gpsimd", wvd)
            wvg = E("gpsimd", lambda e, qv4=qv4: e.tensor_copy(
                wv_all[:, qv4 * 2048:(qv4 + 1) * 2048], wstr[0][:, :]), "sgps")
        W("tensor", "sgps", wvg)
        v_ev = []
        for i in range(16):
            bank = 4 + i % 2
            if i >= 2:
                W("tensor", "svec", v_ev[i - 2])
            for k in range(16):
                vmm = E("tensor", lambda e, i=i, k=k, bank=bank: e.matmul(
                    ps[bank][:, :], self.hA(k)[:, i * 128:(i + 1) * 128],
                    wv_all[:, k * 512:(k + 1) * 512],
                    start=(k == 0), stop=(k == 15)), "stn")
            W("vector", "stn", vmm)
            if i == 0:
                W("vector", "sgps", vones)
            for h8 in range(8):
                vv = E("vector", lambda e, i=i, h8=h8, bank=bank: e.tensor_copy(
                    self.vaug[i][:, h8 * 65:h8 * 65 + 64],
                    ps[bank][:, h8 * 64:(h8 + 1) * 64]), "svec")
            v_ev.append(vv)
        self.V_DONE = v_ev[-1]
        self.barrier()
        sc5.close()

    def p6_attention(self, scope):
        E, W, dma = self.E, self.W, self.dma
        d, ps = self.dram, self.ps
        ptile = [self.sb(f"ptile{i}", [128, TS], BF16, scope) for i in range(4)]
        opevA = self.sb("opevA", [64, T], BF16, scope)
        opevB = self.sb("opevB", [64, T], BF16, scope)
        recrow = self.sb("recrow", [1, TS], F32, scope)
        recrow2 = self.sb("recrow2", [1, TS], F32, scope)
        bcsbA = self.sb("bcsbA", [64, TS], F32, scope)
        bcsbB = self.sb("bcsbB", [64, TS], F32, scope)
        rope_c = self.bc1
        rope_s = self.bc2
        self.smasks = self.sb("smasks", [128, 4 * TS], BF16, scope)
        wstr = [self.sb("p6w0", [128, 2048], F32, scope)]
        wstr.append(wstr[0])
        wstrb = [self.sb("p6wb0", [128, 2048], BF16, scope)]
        wstrb.append(wstrb[0])

        W("sync", "svec", self.K_DONE)
        dma("sync", "cst", self.smasks[:, :], d["masks"].ap())
        cqd = dma("sync", "cst", rope_c[:, :], d["cosq"].ap())
        sqd = dma("sync", "cst", rope_s[:, :], d["sinq"].ap())
        self.WD("tensor", sqd)
        self.WD("vector", sqd)
        W("tensor", "svec", self.K_DONE)
        W("tensor", "svec", self.V_DONE)

        wq_g, q_mm_done, op_dma = [], [], []
        prev_o_last_smm = None
        prev_ov = None
        swv_prev = None
        prev_pv_last = None
        for o in range(NPAIR):
            if o >= 1:
                W("sync", "sgps", wq_g[o - 1])
            wqd = dma("sync", "wq", wstr[o % 2][:, :], d["wq_b"].ap()[o, :, :])
            self.WD("gpsimd", wqd)
            if o >= 1:
                W("gpsimd", "stn", q_mm_done[o - 1])
            wq_g.append(E("gpsimd", lambda e, b=o % 2: e.tensor_copy(
                wstrb[b][:, :], wstr[b][:, :]), "sgps"))
            W("tensor", "sgps", wq_g[o])
            rope_v = []
            for t4 in range(4):
                if prev_ov is not None and t4 == 0:
                    W("tensor", "svec", prev_ov)
                if t4 >= 2:
                    W("tensor", "svec", rope_v[t4 - 2])
                for k in range(16):
                    mmv = E("tensor", lambda e, o=o, t4=t4, k=k: e.matmul(
                        ps[t4 % 2][:, :], wstrb[o % 2][:, k * 128:(k + 1) * 128],
                        self.hA(k)[:, t4 * TS:(t4 + 1) * TS],
                        start=(k == 0), stop=(k == 15)), "stn")
                W("scalar", "stn", mmv)
                if swv_prev is not None:
                    W("scalar", "stn", swv_prev)
                arv = E("scalar", lambda e, t4=t4: e.activation(
                    self.qraw[:, :], ps[t4 % 2][:, :], AF.Copy), "sact")
                W("tensor", "sact", arv)
                swv_prev = E("tensor", lambda e, t4=t4: e.matmul(
                    ps[2 + t4 % 2][:, :], self.spswap[:, :], self.qraw[:, :],
                    start=True, stop=True), "stn")
                W("vector", "stn", swv_prev)
                if prev_o_last_smm is not None and t4 == 0:
                    W("vector", "stn", prev_o_last_smm)
                E("vector", lambda e, t4=t4: e.tensor_tensor(
                    self.tmp1[:, 0:TS], ps[t4 % 2][:, :],
                    rope_c[:, t4 * TS:(t4 + 1) * TS], ALU.mult))
                E("vector", lambda e, t4=t4: e.tensor_tensor(
                    self.tmp2[:, 0:TS], ps[2 + t4 % 2][:, :],
                    rope_s[:, t4 * TS:(t4 + 1) * TS], ALU.mult))
                rope_v.append(E("vector", lambda e, t4=t4: e.tensor_tensor(
                    self.qpair[:, t4 * TS:(t4 + 1) * TS], self.tmp1[:, 0:TS],
                    self.tmp2[:, 0:TS], ALU.add), "svec"))
            q_mm_done.append(self.val("stn"))
            kvch = o // 2
            vgrp = o // 2
            W("tensor", "svec", rope_v[-1])
            ov = None
            for qc in range(NTS):
                njt = 4 * (qc + 1)
                exp_a, pv_t, mk_v = {}, {}, {}
                if ov is not None:
                    W("tensor", "svec", ov)
                if prev_pv_last is not None:
                    W("scalar", "stn", prev_pv_last)
                for loc in range(2):
                    for j in range(njt):
                        sbank = (2 if j % 2 == 0 else 4) + loc
                        pti = 2 * (j % 2) + loc
                        if (loc, j - 2) in exp_a:
                            W("tensor", "sact", exp_a[(loc, j - 2)])
                        smm = E("tensor", lambda e, j=j, qc=qc, loc=loc, sbank=sbank,
                                kvch=kvch: e.matmul(
                                    ps[sbank][:, :],
                                    self.kfm[kvch][loc * 64:loc * 64 + 64,
                                                   j * 128:(j + 1) * 128],
                                    self.qpair[loc * 64:loc * 64 + 64,
                                               qc * TS:(qc + 1) * TS],
                                    start=True, stop=True), "stn")
                        W("scalar", "stn", smm)
                        if (loc, j - 2) in pv_t:
                            W("scalar", "stn", pv_t[(loc, j - 2)])
                        if (loc, j - 2) in mk_v:
                            W("scalar", "svec", mk_v[(loc, j - 2)])
                        ea = E("scalar", lambda e, sbank=sbank, pti=pti: e.activation(
                            ptile[pti][:, :], ps[sbank][:, :], AF.Exp,
                            bias=self.sccols[:, 0:1]), "sact")
                        exp_a[(loc, j)] = ea
                        dmask = j - 4 * qc
                        if dmask >= 0:
                            W("vector", "sact", ea)
                            mv = E("vector", lambda e, pti=pti, dm=dmask: e.tensor_tensor(
                                ptile[pti][:, :], ptile[pti][:, :],
                                self.smasks[:, dm * TS:(dm + 1) * TS], ALU.mult), "svec")
                            mk_v[(loc, j)] = mv
                            W("tensor", "svec", mv)
                        else:
                            W("tensor", "sact", ea)
                        pv = E("tensor", lambda e, j=j, loc=loc, pti=pti, vgrp=vgrp,
                               njt=njt: e.matmul(
                                   ps[loc][0:65, :],
                                   self.vaug[j][:, vgrp * 65:vgrp * 65 + 65],
                                   ptile[pti][:, :],
                                   start=(j == 0), stop=(j == njt - 1)), "stn")
                        pv_t[(loc, j)] = pv
                prev_o_last_smm = self.val("stn")
                prev_pv_last = pv_t[(1, njt - 1)]
                W("vector", "stn", pv_t[(1, njt - 1)])
                E("vector", lambda e: e.reciprocal(recrow[0:1, :], ps[0][64:65, :]))
                rv = E("vector", lambda e: e.reciprocal(recrow2[0:1, :],
                                                        ps[1][64:65, :]), "svec")
                W("tensor", "svec", rv)
                E("tensor", lambda e: e.matmul(
                    ps[2][0:64, :], self.sones1[0:1, 0:64], recrow[0:1, :],
                    start=True, stop=True), "stn")
                b2 = E("tensor", lambda e: e.matmul(
                    ps[3][0:64, :], self.sones1[0:1, 0:64], recrow2[0:1, :],
                    start=True, stop=True), "stn")
                W("vector", "stn", b2)
                if op_dma:
                    self.WD("vector", op_dma[-1])
                E("vector", lambda e: e.tensor_copy(bcsbA[:, :], ps[2][0:64, :]))
                E("vector", lambda e: e.tensor_copy(bcsbB[:, :], ps[3][0:64, :]))
                E("vector", lambda e, qc=qc: e.tensor_tensor(
                    opevA[:, qc * TS:(qc + 1) * TS], ps[0][0:64, :], bcsbA[:, :],
                    ALU.mult))
                ov = E("vector", lambda e, qc=qc: e.tensor_tensor(
                    opevB[:, qc * TS:(qc + 1) * TS], ps[1][0:64, :], bcsbB[:, :],
                    ALU.mult), "svec")
            prev_ov = ov
            W("gpsimd", "svec", ov)
            dma("gpsimd", "opo", d["opair_d"].ap()[o, 0:64, :], opevA[:, :])
            op_dma.append(dma("gpsimd", "opo", d["opair_d"].ap()[o, 64:128, :],
                              opevB[:, :]))
        self.ATT_DMA = op_dma[-1]
        self.tap("qpair", self.qpair, 128, 2048)
        self.tap("pt3", ptile[3], 128, 512)
        self.tap("rec2", recrow2, 1, 512)
        self.tap("bcsbB", bcsbB, 64, 512)
        self.tap("opevA", opevA, 64, 2048)
        self.tap("opevB", opevB, 64, 2048)

    def p7_outproj(self, scope):
        xstr = self.xstr
        E, W, dma = self.E, self.W, self.dma
        d, ps = self.dram, self.ps
        wstr = [self.sb(f"p7w{i}", [128, 2048], F32, scope) for i in range(2)]
        wstrb = [self.sb(f"p7wb{i}", [128, 2048], BF16, scope) for i in range(2)]
        xnout = [self.sb(f"xnout{i}", [128, T], F32, scope) for i in range(2)]
        xln = self.sb("p7xln", [128, T], F32, scope)

        self.WD("sync", self.ATT_DMA)
        o_in = []
        for i in range(16):
            o_in.append(dma("sync", "oin", self.hA(i)[:, :], d["opair_d"].ap()[i, :, :]))
        wo_g, xp_dma = [], []
        mm_done = []
        ep_a, ep_v = {}, {}
        for o2 in range(16):
            if o2 >= 2:
                W("sync", "sgps", wo_g[o2 - 2])
            wod = dma("sync", f"wo{o2 % 2}", wstr[o2 % 2][:, :], d["wo_b"].ap()[o2, :, :])
            if o2 >= 2:
                W("sync", "svec", ep_v[(o2 - 2, 3)])
            xrd = dma("sync", f"xs{o2 % 2}", xstr[o2 % 2][:, :],
                      d["x_fm"].ap()[o2 * 128:(o2 + 1) * 128, :])
            self.WD("gpsimd", wod)
            if o2 >= 2:
                W("gpsimd", "stn", mm_done[o2 - 2])
            wo_g.append(E("gpsimd", lambda e, b=o2 % 2: e.tensor_copy(
                wstrb[b][:, :], wstr[b][:, :]), "sgps"))
            W("tensor", "sgps", wo_g[o2])
            if o2 == 0:
                self.WD("tensor", o_in[-1])
            for t4 in range(4):
                if (o2, t4 - 2) in ep_a:
                    W("tensor", "sact", ep_a[(o2, t4 - 2)])
                elif o2 >= 1:
                    W("tensor", "sact", ep_a[(o2 - 1, t4 + 2)])
                for k in range(16):
                    mmv = E("tensor", lambda e, o2=o2, t4=t4, k=k: e.matmul(
                        ps[t4 % 2][:, :], wstrb[o2 % 2][:, k * 128:(k + 1) * 128],
                        self.hA(k)[:, t4 * TS:(t4 + 1) * TS],
                        start=(k == 0), stop=(k == 15)), "stn")
                W("scalar", "stn", mmv)
                if (o2 - 1, t4) in ep_v:
                    W("scalar", "svec", ep_v[(o2 - 1, t4)])
                av = E("scalar", lambda e, o2=o2, t4=t4: e.activation(
                    xln[:, t4 * TS:(t4 + 1) * TS], ps[t4 % 2][:, :],
                    AF.Identity, scale=self.gacols[:, o2:o2 + 1]), "sact")
                ep_a[(o2, t4)] = av
                W("vector", "sact", av)
                if t4 == 0:
                    self.WD("vector", xrd)
                    if o2 >= 2:
                        self.WD("vector", xp_dma[o2 - 2])
                ep_v[(o2, t4)] = E("vector", lambda e, o2=o2, t4=t4: e.tensor_tensor(
                    xnout[o2 % 2][:, t4 * TS:(t4 + 1) * TS],
                    xln[:, t4 * TS:(t4 + 1) * TS],
                    xstr[o2 % 2][:, t4 * TS:(t4 + 1) * TS], ALU.add), "svec")
            mm_done.append(self.val("stn"))
            W("sync", "svec", ep_v[(o2, 3)])
            xp_dma.append(dma("sync", f"xp{o2 % 2}",
                              d["xprime"].ap()[o2 * 128:(o2 + 1) * 128, :],
                              xnout[o2 % 2][:, :]))
        self.XP_DMA = xp_dma[-1]
        self.tap("xn15", xnout[1], 128, 2048)
        self.tap("gacols", self.gacols, 128, 16)

    def p9_h2_fwht(self, scope):
        srow_a, srow_b, srow_c, srow_d = (self.srow_a, self.srow_b,
                                          self.srow_c, self.srow_d)
        xstr = self.xstr
        E, W, dma = self.E, self.W, self.dma
        d, ps = self.dram, self.ps
        xbf = [self.sb(f"p9b{i}", [128, 2048], BF16, scope) for i in range(2)]
        u1ev = [self.sb(f"u1ev{i}", [128, TS], BF16, scope) for i in range(2)]
        xln = self.sb("p9xln", [128, T], F32, scope)
        mxp = self.sb("mxp9", [128, T], F32, scope)
        mnp = self.sb("mnp9", [128, T], F32, scope)
        zq = self.sb("zq9", [128, TS], F32, scope)
        srow_sg = self.sb("srow_sg", [1, T], F32, scope)
        self.srow_sg = srow_sg

        sub_v, act_a, st1_t, u1_dma, uev_a = [], [], {}, [], []
        for c in range(NK):
            if c >= 2:
                W("sync", "svec", sub_v[c - 2])
            xd = dma("sync", f"xs{c % 2}", xstr[c % 2][:, :],
                     d["xprime"].ap()[c * 128:(c + 1) * 128, :])
            self.WD("vector", xd)
            if c >= 1:
                W("vector", "sact", act_a[c - 1])
            E("vector", lambda e, b=c % 2: e.tensor_tensor(
                xln[:, :], xstr[b][:, :], self.bc1[:, :], ALU.subtract))
            sub_v.append(E("vector", lambda e: e.tensor_tensor(
                xln[:, :], xln[:, :], self.bc2[:, :], ALU.mult), "svec"))
            W("scalar", "svec", sub_v[c])
            if c >= 2:
                W("scalar", "stn", st1_t[(c - 2, 3)])
            act_a.append(E("scalar", lambda e, c=c: e.activation(
                xbf[c % 2][:, :], xln[:, :], AF.Identity,
                scale=self.onePcb[:, c:c + 1],
                bias=self.mod96[:, 48 + c:49 + c]), "sact"))
            W("tensor", "sact", act_a[c])
            for t4 in range(4):
                it = c * 4 + t4
                if it >= 2:
                    W("tensor", "sact", uev_a[it - 2])
                stv = E("tensor", lambda e, c=c, t4=t4: e.matmul(
                    ps[t4 % 2][:, :], self.sh128[:, :],
                    xbf[c % 2][:, t4 * TS:(t4 + 1) * TS], start=True, stop=True), "stn")
                st1_t[(c, t4)] = stv
                W("scalar", "stn", stv)
                if it >= 2:
                    self.WD("scalar", u1_dma[it - 2])
                ua = E("scalar", lambda e, t4=t4: e.activation(
                    u1ev[t4 % 2][:, :], ps[t4 % 2][:, :], AF.Copy), "sact")
                uev_a.append(ua)
                W("gpsimd", "sact", ua)
                u1_dma.append(dma("gpsimd", f"ub{t4 % 2}",
                                  d["u1_d"].ap()[:, c, t4 * TS:(t4 + 1) * TS],
                                  u1ev[t4 % 2][:, :]))
        W("gpsimd", "ub0", self.val("ub0"))
        W("gpsimd", "ub1", self.val("ub1"))
        v1_in = []
        for g in range(16):
            v1_in.append(dma("gpsimd", "vin", self.hA(g)[:, :],
                             d["u1_d"].ap()[8 * g:8 * g + 8, :, :]
                             .rearrange("e c t -> (e c) t")))
        self.WD("tensor", v1_in[-1])
        mx_v = {}
        for g in range(16):
            for t4 in range(4):
                it = g * 4 + t4
                bank = 2 + it % 2
                if it >= 2:
                    W("tensor", "svec", mx_v[it - 2])
                pa = E("tensor", lambda e, g=g, t4=t4, bank=bank: e.matmul(
                    ps[bank][:, :], self.si8h16[:, :],
                    self.hA(g)[:, t4 * TS:(t4 + 1) * TS], start=True, stop=True), "stn")
                W("vector", "stn", pa)
                if g == 0:
                    E("vector", lambda e, t4=t4, bank=bank: e.tensor_copy(
                        mxp[:, t4 * TS:(t4 + 1) * TS], ps[bank][:, :]))
                    v = E("vector", lambda e, t4=t4, bank=bank: e.tensor_copy(
                        mnp[:, t4 * TS:(t4 + 1) * TS], ps[bank][:, :]), "svec")
                else:
                    E("vector", lambda e, t4=t4, bank=bank: e.tensor_tensor(
                        mxp[:, t4 * TS:(t4 + 1) * TS], mxp[:, t4 * TS:(t4 + 1) * TS],
                        ps[bank][:, :], ALU.max))
                    v = E("vector", lambda e, t4=t4, bank=bank: e.tensor_tensor(
                        mnp[:, t4 * TS:(t4 + 1) * TS], mnp[:, t4 * TS:(t4 + 1) * TS],
                        ps[bank][:, :], ALU.min), "svec")
                mx_v[it] = v
        E("vector", lambda e: e.tensor_scalar(mnp[:, :], mnp[:, :], -1.0, None, ALU.mult))
        mab = E("vector", lambda e: e.tensor_tensor(mxp[:, :], mxp[:, :], mnp[:, :],
                                                    ALU.max), "svec")
        mrow = self.xpart_max(mxp, T, mab, srow_c)
        E("vector", lambda e: e.tensor_scalar(srow_d[0:1, :], srow_c[0:1, :],
                                              C2048, EPS, ALU.mult, ALU.max))
        rcv = E("vector", lambda e: e.reciprocal(srow_a[0:1, :], srow_d[0:1, :]),
                "svec")
        W("vector", "svec", rcv)
        E("vector", lambda e: e.tensor_scalar(srow_a[0:1, :], srow_a[0:1, :],
                                              127.0 * C2048, None, ALU.mult))
        E("vector", lambda e: e.tensor_scalar(
            srow_sg[0:1, :], srow_d[0:1, :], self.wsrow[0:1, 8:9],
            1.0 / 127.0, ALU.mult, ALU.mult))
        srv = E("vector", lambda e: e.tensor_scalar(
            self.srow_su[0:1, :], srow_d[0:1, :], self.wsrow[0:1, 9:10],
            1.0 / 127.0, ALU.mult, ALU.mult), "svec")
        bcv = self.bcast_rows([(srow_a, self.bc2, T), (srow_sg, self.bc1, T)],
                              srv)
        W("tensor", "svec", bcv)
        rq = {}
        for g in range(16):
            for t4 in range(4):
                it = g * 4 + t4
                bank = 2 + it % 2
                if it >= 2:
                    W("tensor", "svec", rq[it - 2])
                pb = E("tensor", lambda e, g=g, t4=t4, bank=bank: e.matmul(
                    ps[bank][:, :], self.si8h16[:, :],
                    self.hA(g)[:, t4 * TS:(t4 + 1) * TS], start=True, stop=True), "stn")
                W("vector", "stn", pb)
                E("vector", lambda e, t4=t4, bank=bank: e.tensor_tensor(
                    zq[:, :], ps[bank][:, :], self.bc2[:, t4 * TS:(t4 + 1) * TS],
                    ALU.mult))
                rq[it] = E("vector", lambda e, g=g, t4=t4: e.tensor_scalar(
                    self.hA(g)[:, t4 * TS:(t4 + 1) * TS], zq[:, :],
                    MAGIC, -MAGIC, ALU.add, ALU.add), "svec")
        self.R_DONE = rq[63]
        self.tap("R0", self.hA(0), 128, 2048)

    def xpart_max(self, mtile, width, ready_v, out_row):
        E, W = self.E, self.W
        ps = self.ps
        W("tensor", "svec", ready_v)
        ns = width // 128
        tc = []
        for s in range(ns):
            bank = 2 + s % 2
            if s >= 2:
                W("tensor", "svec", tc[s - 2])
            tv = E("tensor", lambda e, s=s, bank=bank: e.transpose(
                ps[bank][:, 0:128], mtile[:, s * 128:(s + 1) * 128],
                self.sid_f[:, :]), "stn")
            W("vector", "stn", tv)
            tc.append(E("vector", lambda e, s=s, bank=bank: e.tensor_reduce(
                self.mcol[:, s:s + 1], ps[bank][:, 0:128], AX.X, ALU.max), "svec"))
        W("tensor", "svec", tc[-1])
        rv = []
        for s in range(ns):
            pr = self.ps_row if s % 2 == 0 else self.ps_row2
            if s >= 2:
                W("tensor", "svec", rv[s - 2])
            tv = E("tensor", lambda e, s=s, pr=pr: e.transpose(
                pr[0:1, 0:128], self.mcol[:, s:s + 1], self.sid_f[:, :]), "stn")
            W("vector", "stn", tv)
            rv.append(E("vector", lambda e, s=s, pr=pr, out_row=out_row: e.tensor_copy(
                out_row[0:1, s * 128:(s + 1) * 128], pr[0:1, 0:128]), "svec"))
        return rv[-1]

    def p10_tdq(self, scope):
        E, W, dma = self.E, self.W, self.dma
        d = self.dram
        wstr = [self.sb(f"pAw{i}", [128, 2048], F32, scope) for i in range(2)]
        wstrb = [self.sb(f"pAwb{i}", [128, 2048], BF16, scope) for i in range(2)]
        qt = self.sb("qt10", [128, 2048], F32, scope)
        tq, tdma = [], []
        for i in range(64):
            o2, kq = divmod(i, 4)
            if i >= 2:
                W("sync", "sgps", tq[i - 2])
            dd = dma("sync", f"tf{i % 2}", wstr[i % 2][:, :],
                     d["td_b"].ap()[o2, :, kq * 2048:(kq + 1) * 2048])
            self.WD("gpsimd", dd)
            if i >= 2:
                self.WD("gpsimd", tdma[i - 2])
            E("gpsimd", lambda e, b=i % 2: e.tensor_scalar(
                qt[:, :], wstr[b][:, :], self.wsd_i[:, :], None, ALU.mult))
            E("gpsimd", lambda e: e.tensor_scalar(
                qt[:, :], qt[:, :], 1.0, -1.0, ALU.min, ALU.max))
            tq.append(E("gpsimd", lambda e, b=i % 2: e.tensor_scalar(
                wstrb[b][:, :], qt[:, :], MAGIC, -MAGIC, ALU.add, ALU.add), "sgps"))
            tdma.append(dma("gpsimd", f"tq{i % 2}",
                            d["tdq_d"].ap()[o2, :, kq * 2048:(kq + 1) * 2048],
                            wstrb[i % 2][:, :]))
        self.TDQ_TOT = [("tq0", self.val("tq0")), ("tq1", self.val("tq1"))]

    def p11_gateup(self, scope):
        E, W, dma = self.E, self.W, self.dma
        d, ps = self.dram, self.ps
        wstr = [self.sb(f"pBg{i}", [128, 2048], F32, scope) for i in range(2)]
        wstrb = [self.sb(f"pBgb{i}", [128, 2048], BF16, scope) for i in range(2)]
        wstr2 = [self.sb(f"pBu{i}", [128, 2048], F32, scope) for i in range(2)]
        wstr2b = [self.sb(f"pBub{i}", [128, 2048], BF16, scope) for i in range(2)]
        qt = self.sb("qt11", [128, 2048], F32, scope)
        zt1 = [self.sb(f"zt1_{i}", [128, TS], F32, scope) for i in range(2)]
        zt2 = [self.sb(f"zt2_{i}", [128, TS], F32, scope) for i in range(2)]
        zbf = [self.sb(f"zbf{i}", [128, TS], BF16, scope) for i in range(2)]
        u2ev = [self.sb(f"u2ev{i}", [128, TS], BF16, scope) for i in range(2)]

        gq, u2_dma = [], []
        z_v, u_a, s1_t = {}, {}, {}
        mm_done = []
        for o in range(64):
            if o >= 2:
                W("sync", "sgps", gq[o - 2])
            d1 = dma("sync", f"tg{o % 2}", wstr[o % 2][:, :], d["tg_b"].ap()[o, :, :])
            d2 = dma("sync", f"tu{o % 2}", wstr2[o % 2][:, :], d["tu_b"].ap()[o, :, :])
            self.WD("gpsimd", d1)
            self.WD("gpsimd", d2)
            if o >= 2:
                W("gpsimd", "stn", mm_done[o - 2])
            for (src, dst, wsc) in ((wstr[o % 2], wstrb[o % 2], self.wsg_i),
                                    (wstr2[o % 2], wstr2b[o % 2], self.wsu_i)):
                E("gpsimd", lambda e, src=src, wsc=wsc: e.tensor_scalar(
                    qt[:, :], src[:, :], wsc[:, :], None, ALU.mult))
                E("gpsimd", lambda e: e.tensor_scalar(
                    qt[:, :], qt[:, :], 1.0, -1.0, ALU.min, ALU.max))
                qv = E("gpsimd", lambda e, dst=dst: e.tensor_scalar(
                    dst[:, :], qt[:, :], MAGIC, -MAGIC, ALU.add, ALU.add), "sgps")
            gq.append(qv)
            W("tensor", "sgps", qv)
            for t4 in range(4):
                it = o * 4 + t4
                if it >= 2:
                    W("tensor", "svec", z_v[it - 2])
                    W("tensor", "sact", u_a[it - 2])
                for k in range(16):
                    E("tensor", lambda e, o=o, t4=t4, k=k: e.matmul(
                        ps[t4 % 2][:, :], wstrb[o % 2][:, k * 128:(k + 1) * 128],
                        self.hA(k)[:, t4 * TS:(t4 + 1) * TS],
                        start=(k == 0), stop=(k == 15)), "stn")
                for k in range(16):
                    gu = E("tensor", lambda e, o=o, t4=t4, k=k: e.matmul(
                        ps[2 + t4 % 2][:, :], wstr2b[o % 2][:, k * 128:(k + 1) * 128],
                        self.hA(k)[:, t4 * TS:(t4 + 1) * TS],
                        start=(k == 0), stop=(k == 15)), "stn")
                W("vector", "stn", gu)
                if it >= 2:
                    W("vector", "sact", z_a_prev[it - 2])
                E("vector", lambda e, t4=t4: e.tensor_tensor(
                    zt1[t4 % 2][:, :], ps[t4 % 2][:, :],
                    self.bc1[:, t4 * TS:(t4 + 1) * TS], ALU.mult), "svec")
                ztv = self.val("svec")
                W("scalar", "svec", ztv)
                if it >= 2:
                    W("scalar", "svec", z_v[it - 2])
                za = E("scalar", lambda e, t4=t4: e.activation(
                    zt2[t4 % 2][:, :], zt1[t4 % 2][:, :], AF.Silu), "sact")
                if it == 0:
                    z_a_prev = {}
                z_a_prev[it] = za
                W("vector", "sact", za)
                if it >= 2:
                    W("vector", "stn", s1_t[it - 2])
                z_v[it] = E("vector", lambda e, t4=t4: e.tensor_tensor(
                    zbf[t4 % 2][:, :], zt2[t4 % 2][:, :], ps[2 + t4 % 2][:, :],
                    ALU.mult), "svec")
                W("tensor", "svec", z_v[it])
                s1_t[it] = E("tensor", lambda e, t4=t4: e.matmul(
                    ps[4 + t4 % 2][:, :], self.sh128[:, :], zbf[t4 % 2][:, :],
                    start=True, stop=True), "stn")
                W("scalar", "stn", s1_t[it])
                if it >= 2:
                    self.WD("scalar", u2_dma[it - 2])
                u_a[it] = E("scalar", lambda e, t4=t4: e.activation(
                    u2ev[t4 % 2][:, :], ps[4 + t4 % 2][:, :], AF.Copy), "sact")
                W("gpsimd", "sact", u_a[it])
                u2_dma.append(dma("gpsimd", f"ub{t4 % 2}",
                                  d["u2_d"].ap()[:, o, t4 * TS:(t4 + 1) * TS],
                                  u2ev[t4 % 2][:, :]))
            mm_done.append(self.val("stn"))
        self.GU_TOT = [("ub0", self.val("ub0")), ("ub1", self.val("ub1"))]

    def p12_down(self, scope):
        srow_a, srow_b, srow_c, srow_d = (self.srow_a, self.srow_b,
                                          self.srow_c, self.srow_d)
        xstr = self.xstr
        E, W, dma = self.E, self.W, self.dma
        d, ps = self.dram, self.ps
        tds = [self.sb(f"tds{i}", [128, 8192], BF16, scope) for i in range(2)]
        outsb = [self.sb(f"outsb{i}", [128, TS], F32, scope) for i in range(2)]
        mxp = self.sb("mxp12", [128, TS], F32, scope)
        mnp = self.sb("mnp12", [128, TS], F32, scope)
        zq = self.sb("zq12", [128, TS], F32, scope)

        def v2A(g):
            return self.bigA1[:, g * 512:(g + 1) * 512]

        out_dmas = []
        prev_down_done = None
        for tok in self.GU_TOT:
            self.WD("gpsimd", tok)
        for q4 in range(4):
            HB = q4 * TS
            if prev_down_done is not None:
                W("gpsimd", "stn", prev_down_done)
            v2_in = []
            for g in range(64):
                v2_in.append(dma("gpsimd", "vin", v2A(g)[:, :],
                                 d["u2_d"].ap()[2 * g:2 * g + 2, :, HB:HB + TS]
                                 .rearrange("e b t -> (e b) t")))
            self.WD("tensor", v2_in[-1])
            mx_v = {}
            for g in range(64):
                bank = 2 + g % 2
                if g >= 2:
                    W("tensor", "svec", mx_v[g - 2])
                pa = E("tensor", lambda e, g=g, bank=bank: e.matmul(
                    ps[bank][:, :], self.si2h64[:, :], v2A(g)[:, :],
                    start=True, stop=True), "stn")
                W("vector", "stn", pa)
                if g == 0:
                    E("vector", lambda e, bank=bank: e.tensor_copy(
                        mxp[:, :], ps[bank][:, :]))
                    v = E("vector", lambda e, bank=bank: e.tensor_copy(
                        mnp[:, :], ps[bank][:, :]), "svec")
                else:
                    E("vector", lambda e, bank=bank: e.tensor_tensor(
                        mxp[:, :], mxp[:, :], ps[bank][:, :], ALU.max))
                    v = E("vector", lambda e, bank=bank: e.tensor_tensor(
                        mnp[:, :], mnp[:, :], ps[bank][:, :], ALU.min), "svec")
                mx_v[g] = v
            E("vector", lambda e: e.tensor_scalar(mnp[:, :], mnp[:, :], -1.0, None,
                                                  ALU.mult))
            mab = E("vector", lambda e: e.tensor_tensor(mxp[:, :], mxp[:, :], mnp[:, :],
                                                        ALU.max), "svec")
            self.xpart_max(mxp, TS, mab, srow_c)
            E("vector", lambda e, HB=HB: e.tensor_tensor(
                srow_c[0:1, 0:TS], srow_c[0:1, 0:TS],
                self.srow_su[0:1, HB:HB + TS], ALU.mult))
            E("vector", lambda e: e.tensor_scalar(
                srow_d[0:1, 0:TS], srow_c[0:1, 0:TS], C8192, EPS,
                ALU.mult, ALU.max))
            rcv = E("vector", lambda e: e.reciprocal(srow_a[0:1, 0:TS],
                                                     srow_d[0:1, 0:TS]), "svec")
            W("vector", "svec", rcv)
            E("vector", lambda e: e.tensor_scalar(
                srow_a[0:1, 0:TS], srow_a[0:1, 0:TS], 127.0 * C8192,
                None, ALU.mult))
            E("vector", lambda e, HB=HB: e.tensor_tensor(
                srow_a[0:1, 0:TS], srow_a[0:1, 0:TS],
                self.srow_su[0:1, HB:HB + TS], ALU.mult))
            s2v = E("vector", lambda e: e.tensor_scalar(
                srow_b[0:1, 0:TS], srow_d[0:1, 0:TS],
                self.wsrow[0:1, 10:11], 1.0 / 127.0, ALU.mult, ALU.mult), "svec")
            bcv = self.bcast_rows([(srow_a, self.bc2, TS),
                                   (srow_b, self.bc1, TS)], s2v)
            W("tensor", "svec", bcv)
            rq = {}
            for g in range(64):
                bank = 2 + g % 2
                if g >= 2:
                    W("tensor", "svec", rq[g - 2])
                pb = E("tensor", lambda e, g=g, bank=bank: e.matmul(
                    ps[bank][:, :], self.si2h64[:, :], v2A(g)[:, :],
                    start=True, stop=True), "stn")
                W("vector", "stn", pb)
                E("vector", lambda e, bank=bank: e.tensor_tensor(
                    zq[:, :], ps[bank][:, :], self.bc2[:, 0:TS], ALU.mult))
                rq[g] = E("vector", lambda e, g=g: e.tensor_scalar(
                    v2A(g)[:, :], zq[:, :], MAGIC, -MAGIC, ALU.add, ALU.add), "svec")
            W("tensor", "svec", rq[63])
            td_in, ep_a, ep_v, mmd = [], {}, {}, {}
            for o2 in range(16):
                if o2 >= 2:
                    W("sync", "stn", mmd[o2 - 2])
                if q4 == 0 and o2 == 0:
                    for tok in self.TDQ_TOT:
                        self.WD("sync", tok)
                td_in.append(dma("sync", f"ts{o2 % 2}", tds[o2 % 2][:, :],
                                 d["tdq_d"].ap()[o2, :, :]))
                if o2 >= 2:
                    W("sync", "svec", ep_v[o2 - 2])
                xrd = dma("sync", f"xs{o2 % 2}", xstr[o2 % 2][:, 0:TS],
                          d["xprime"].ap()[o2 * 128:(o2 + 1) * 128, HB:HB + TS])
                self.WD("tensor", td_in[o2])
                bank = 4 + o2 % 2
                if o2 >= 2:
                    W("tensor", "sact", ep_a[o2 - 2])
                for g in range(64):
                    dmm = E("tensor", lambda e, o2=o2, g=g, bank=bank: e.matmul(
                        ps[bank][:, :], tds[o2 % 2][:, g * 128:(g + 1) * 128],
                        v2A(g)[:, :], start=(g == 0), stop=(g == 63)), "stn")
                mmd[o2] = self.val("stn")
                W("scalar", "stn", dmm)
                if o2 >= 1:
                    W("scalar", "svec", ep_v[o2 - 1])
                da = E("scalar", lambda e, o2=o2, bank=bank: e.activation(
                    self.tmp2[:, 0:TS], ps[bank][:, :], AF.Identity,
                    scale=self.gbcols[:, o2:o2 + 1]), "sact")
                ep_a[o2] = da
                W("vector", "sact", da)
                self.WD("vector", xrd)
                if len(out_dmas) >= 2:
                    self.WD("vector", out_dmas[-2])
                E("vector", lambda e: e.tensor_tensor(
                    self.tmp2[:, 0:TS], self.tmp2[:, 0:TS], self.bc1[:, 0:TS],
                    ALU.mult))
                ep_v[o2] = E("vector", lambda e, o2=o2: e.tensor_tensor(
                    outsb[o2 % 2][:, :], self.tmp2[:, 0:TS],
                    xstr[o2 % 2][:, 0:TS], ALU.add), "svec")
                W("sync", "svec", ep_v[o2])
                out_dmas.append(dma("sync", f"ou{o2 % 2}",
                                    d["out_fm"].ap()[o2 * 128:(o2 + 1) * 128,
                                                     HB:HB + TS],
                                    outsb[o2 % 2][:, :]))
            prev_down_done = self.val("stn")
        self.W("sync", "ou0", self.val("ou0"))
        self.W("sync", "ou1", self.val("ou1"))

    def tap(self, key, tile, rows, cols):
        self.dbg_tiles[key] = (tile, rows, cols)

    def emit_taps(self):
        if not self.dbg:
            return
        self.barrier()
        off = 0
        for key, r, c in self.dbg:
            tile, tr, tc = self.dbg_tiles[key]
            assert r == tr and c == tc, (key, r, c, tr, tc)
            self.dma("gpsimd", "dbg", self.dram["dbg_d"].ap()[0:r, off:off + c],
                     tile[0:r, 0:c])
            off += c
        self.W("gpsimd", "dbg", self.val("dbg"))

    def build(self):
        self.declare()
        self.p0_consts()
        with contextlib.ExitStack() as sc:
            self.p1_mod_ws(sc)
            self.barrier()
        if self.upto >= 3:
            with contextlib.ExitStack() as sc:
                xbf = [self.sb(f"p3b{i}", [128, 2048], BF16, sc) for i in range(2)]
                xsq = [self.sb(f"p3q{i}", [128, 2048], BF16, sc) for i in range(2)]
                xln = self.sb("p3xln", [128, T], F32, sc)
                self.alloc_rows(sc)
                self.alloc_xstr(sc)
                self.ln_stats(self.dram["x_fm"], xbf, xsq)
                self.barrier()
                self.p4_h(self.dram["x_fm"], self.onePca, 0, self.hA, xln)
                self.barrier()
            self.tap("h0", self.hA(0), 128, 2048)
            self.tap("h15", self.hA(15), 128, 2048)
        if self.upto >= 5:
            with contextlib.ExitStack() as sc:
                self.p5_kv(sc)
                self.barrier()
                if self.upto >= 6:
                    self.p6_attention(sc)
                    self.barrier()
                for _ki in range(8):
                    self.tap(f"k{_ki}", self.kfm[_ki], 128, 2048)
                self.tap("v0", self.vaug[0], 128, 520)
        if self.upto >= 7:
            with contextlib.ExitStack() as sc:
                self.alloc_xstr(sc)
                self.p7_outproj(sc)
                self.barrier()
        if self.upto >= 8:
            with contextlib.ExitStack() as sc:
                xbf = [self.sb(f"p8b{i}", [128, 2048], BF16, sc) for i in range(2)]
                xsq = [self.sb(f"p8q{i}", [128, 2048], BF16, sc) for i in range(2)]
                self.alloc_rows(sc)
                self.alloc_xstr(sc)
                self.ln_stats(self.dram["xprime"], xbf, xsq)
                self.barrier()
        if self.upto >= 9:
            with contextlib.ExitStack() as sc:
                self.alloc_rows(sc)
                self.alloc_xstr(sc)
                self.p9_h2_fwht(sc)
                self.barrier()
        if self.upto >= 10:
            with contextlib.ExitStack() as sc:
                self.p10_tdq(sc)
                self.barrier()
        if self.upto >= 11:
            with contextlib.ExitStack() as sc:
                self.p11_gateup(sc)
                self.barrier()
        if self.upto >= 12:
            with contextlib.ExitStack() as sc:
                self.alloc_rows(sc, TS)
                self.alloc_xstr(sc)
                self.p12_down(sc)
        self.emit_taps()
        self.play()
        self.es.close()
        return self.nc



def _blk4(wT):
    K, M = wT.shape
    ko, mo = K // 128, M // 128
    out = np.ascontiguousarray(
        wT.reshape(ko, 128, mo, 128).transpose(2, 1, 0, 3))
    return np.ascontiguousarray(out.reshape(mo, 128, ko * 128))


def host_prep(inputs):
    x = inputs["L0_attn_x"]
    wq, wk, wv, wo = (inputs["L0_attn_wq"], inputs["L0_attn_wk"],
                      inputs["L0_attn_wv"], inputs["L0_attn_wo"])
    wg, wu, wd = (inputs["L0_ffn_w_gate"], inputs["L0_ffn_w_up"],
                  inputs["L0_ffn_w_down"])
    cc_in, wmod = inputs["L0_c"], inputs["L0_wmod"]

    sh = {}
    sh["wq_b"] = _blk4(np.ascontiguousarray(np.asarray(wq).T))
    wkT = np.ascontiguousarray(np.asarray(wk).T)
    wk_dup = np.concatenate([
        np.concatenate([wkT[:, h * 64:(h + 1) * 64]] * 2, axis=1)
        for h in range(8)], axis=1)
    sh["wk_b"] = _blk4(wk_dup)
    sh["wv_t"] = np.ascontiguousarray(np.asarray(wv).T)
    sh["wo_b"] = _blk4(np.ascontiguousarray(np.asarray(wo).T))
    qq = np.arange(128)
    kk16 = np.arange(16)
    frow = (qq[None, :] % 16) * 128 + 8 * kk16[:, None] + qq[None, :] // 16
    og = np.arange(64)
    cg = np.arange(128)
    row_idx = cg[None, None, None, :] * 64 + og[:, None, None, None]
    col_idx = frow.T[None, :, :, None]
    wg_a, wu_a, wd_a = np.asarray(wg), np.asarray(wu), np.asarray(wd)
    tg4 = wg_a[row_idx, col_idx]
    tu4 = wu_a[row_idx, col_idx]
    sh["tg_b"] = np.ascontiguousarray(tg4.reshape(64, 128, 2048).astype(np.float32))
    sh["tu_b"] = np.ascontiguousarray(tu4.reshape(64, 128, 2048).astype(np.float32))
    sh["td_b"] = _blk4(np.ascontiguousarray(wd_a.T))
    sh["ident_f"] = np.eye(128, dtype=np.float32)
    H128 = _hadamard(128)
    sh["h128_b"] = H128.astype(ml_dtypes.bfloat16)
    sh["i8h16_b"] = np.kron(np.eye(8), _hadamard(16)).astype(ml_dtypes.bfloat16)
    sh["i2h64_b"] = np.kron(np.eye(2), _hadamard(64)).astype(ml_dtypes.bfloat16)
    psw = np.zeros((128, 128), np.float32)
    for i in range(128):
        psw[i, i ^ 1] = 1.0
    sh["pswap_b"] = psw.astype(ml_dtypes.bfloat16)
    i_idx = (np.arange(128) % 64) // 2
    omega = 1.0 / 10000.0 ** (np.arange(0, 64, 2, dtype=np.float64) / 64.0)
    ang = omega[i_idx][:, None] * np.arange(T, dtype=np.float64)[None, :]
    cos_t = np.cos(ang)
    sin_t = np.sin(ang)
    sign = np.where(np.arange(128) % 2 == 0, -1.0, 1.0)[:, None]
    sh["cosq"] = (0.125 * cos_t).astype(np.float32)
    sh["sinq"] = (0.125 * sin_t * sign).astype(np.float32)
    sh["cosk"] = cos_t.astype(np.float32)
    sh["sink"] = (sin_t * sign).astype(np.float32)
    masks = np.zeros((4, 128, TS), np.float32)
    rr = np.arange(128)[:, None]
    ss = np.arange(TS)[None, :]
    for dd in range(4):
        masks[dd] = (ss >= rr + 128 * dd).astype(np.float32)
    sh["masks"] = np.ascontiguousarray(
        masks.transpose(1, 0, 2).reshape(128, 4 * TS)).astype(ml_dtypes.bfloat16)
    ccols = np.zeros((128, 8), np.float32)
    ccols[:, 0] = EXPB
    ccols[:, 1] = EPS
    ccols[:, 2] = 1.0
    ccols[:, 3] = 0.25
    sh["ccols"] = ccols
    sh["ones1f"] = np.ones((1, 128), np.float32)
    sh["onescolb"] = np.ones((128, 1), ml_dtypes.bfloat16)
    c_np = np.asarray(cc_in)
    c_t = np.zeros((128, 128), np.float32)
    for ch in range(16):
        for b in range(8):
            c_t[:, ch * 8 + b] = c_np[b, ch * 128:(ch + 1) * 128]
    sh["c_t"] = c_t

    wmod_np = np.asarray(wmod)
    x_np = np.asarray(x)
    per_core = []
    for i in range(NCORES):
        dd = dict(sh)
        dd["x_fm"] = np.ascontiguousarray(x_np[i].T.astype(np.float32))
        oh = np.zeros((8, 1), np.float32)
        oh[i, 0] = 1.0
        dd["onehot"] = oh
        dd["wmod_sl"] = np.ascontiguousarray(wmod_np[:, i * 1536:(i + 1) * 1536])
        dd["wgs"] = np.ascontiguousarray(sh["tg_b"][8 * i:8 * i + 8])
        dd["wus"] = np.ascontiguousarray(sh["tu_b"][8 * i:8 * i + 8])
        dd["wds"] = np.ascontiguousarray(
            sh["td_b"][2 * i:2 * i + 2].reshape(2, 128, 4, 2048)
            .transpose(0, 2, 1, 3).reshape(8, 128, 2048))
        per_core.append(dd)
    return per_core


def kernel(**inputs):
    in_maps = host_prep(inputs)
    nc = KB().build()
    res = run_bass_kernel_spmd(nc, in_maps, core_ids=list(range(NCORES)))
    out = np.stack([np.asarray(res.results[i]["out_fm"]).T for i in range(NCORES)])
    return np.ascontiguousarray(out).astype(np.float32)


# version 02
# speedup vs baseline: 1.8356x; 1.8356x over previous
import contextlib
import numpy as np
import ml_dtypes
import concourse.bass as bass
import concourse.mybir as mybir
from concourse.bass_utils import run_bass_kernel_spmd

F32 = mybir.dt.float32
BF16 = mybir.dt.bfloat16
AF = mybir.ActivationFunctionType
ALU = mybir.AluOpType
AX = mybir.AxisListType

NCORES = 8
T = 2048
D = 2048
NK = 16
NTS = 4
TS = 512
NPAIR = 16
MAGIC = 12582912.0
EXPB = -18.0
EPS = 1e-5
C2048 = float(1.0 / np.sqrt(np.float64(2048.0)))
C8192 = float(1.0 / np.sqrt(np.float64(8192.0)))
NELW = 8192.0 * 2048.0

ENGS = ("tensor", "vector", "scalar", "gpsimd", "sync")
NO_CC = False
DBG_WS = False


def _hadamard(n):
    h = np.array([[1.0]])
    while h.shape[0] < n:
        h = np.block([[h, h], [h, -h]])
    return h


class KB:

    def __init__(self, upto=99, dbg=()):
        self.upto = upto
        self.dbg = list(dbg)
        nc = self.nc = bass.Bass("TRN2", target_bir_lowering=False,
                                 num_devices=NCORES)
        self.q = {e: [] for e in ENGS}
        self.es = contextlib.ExitStack()
        self.sems = {}
        for s in ("cc", "cc2", "stn", "svec", "sact", "sgps"):
            self.sems[s] = [self.es.enter_context(nc.semaphore(s)), 0]
        self.dbg_tiles = {}

    FENCE = True
    _ENG_SEM = {"vector": "svec", "scalar": "sact", "gpsimd": "sgps"}

    def E(self, eng, build, sem=None, k=1, nofence=False):
        fence = self.FENCE and k == 1 and eng in self._ENG_SEM and not nofence
        if fence and sem is None:
            sem = self._ENG_SEM[eng]
        v = None
        if sem is not None:
            s = self.sem_of(sem)
            s[1] += k
            v = s[1]
            h = s[0]

            def go(e, build=build, h=h, k=k):
                build(e).then_inc(h, k)
        else:
            def go(e, build=build):
                build(e)
        self.q[eng].append(go)
        if fence:
            self.W(eng, sem, v)
        return v

    def W(self, eng, sem, v):
        if v is None or v <= 0:
            return
        h = self.sem_of(sem)[0]
        self.q[eng].append(lambda e, h=h, v=v: e.wait_ge(h, v))

    def sem_of(self, name):
        if name not in self.sems:
            self.sems[name] = [self.es.enter_context(self.nc.semaphore(name)), 0]
        return self.sems[name]

    def dma(self, eng, stream, dst, src, **kw):
        self.sem_of(stream)
        v = self.E(eng, lambda e, d=dst, s=src, kw=kw: e.dma_start(d, s, **kw),
                   stream, 16)
        return (stream, v)

    def WD(self, eng, tok):
        if tok is None:
            return
        self.W(eng, tok[0], tok[1])

    def val(self, sem):
        return self.sems[sem][1]

    def barrier(self):
        vals = [(s, sv[1]) for s, sv in list(self.sems.items())]
        for eng in ENGS:
            for s, v in vals:
                self.W(eng, s, v)

    _uid = 0

    def sb(self, name, shape, dtype, scope=None):
        KB._uid += 1
        t = (scope or self.es).enter_context(
            self.nc.sbuf_tensor(f"{name}_u{KB._uid}", shape, dtype))
        return t

    def play(self):
        q = self.q
        with self.nc.Block() as blk:
            @blk.tensor
            def _(e):
                for f in q["tensor"]:
                    f(e)

            @blk.vector
            def _(e):
                for f in q["vector"]:
                    f(e)

            @blk.scalar
            def _(e):
                for f in q["scalar"]:
                    f(e)

            @blk.gpsimd
            def _(e):
                for f in q["gpsimd"]:
                    f(e)

            @blk.sync
            def _(e):
                for f in q["sync"]:
                    f(e)

    def declare(self):
        nc = self.nc
        dram = self.dram = {}
        for name, shape in [
                ("x_fm", [D, T]), ("wq_b", [16, 128, 2048]), ("wk_b", [8, 128, 2048]),
                ("wv_t", [2048, 512]), ("wo_b", [16, 128, 2048]),
                ("tg_b", [64, 128, 2048]), ("tu_b", [64, 128, 2048]),
                ("td_b", [16, 128, 8192]), ("wmod_sl", [2048, 1536]),
                ("c_t", [128, 128]), ("onehot", [8, 1]),
                ("wgs", [8, 128, 2048]), ("wus", [8, 128, 2048]), ("wds", [8, 128, 2048]),
                ("ident_f", [128, 128]), ("cosq", [128, T]), ("sinq", [128, T]),
                ("cosk", [128, T]), ("sink", [128, T]), ("ccols", [128, 8]),
                ("ones1f", [1, 128])]:
            dram[name] = nc.dram_tensor(name, shape, F32, kind="ExternalInput")
        for name, shape in [
                ("h128_b", [128, 128]), ("i8h16_b", [128, 128]), ("i2h64_b", [128, 128]),
                ("pswap_b", [128, 128]), ("masks", [128, 4 * TS]),
                ("onescolb", [128, 1])]:
            dram[name] = nc.dram_tensor(name, shape, BF16, kind="ExternalInput")
        dram["out_fm"] = nc.dram_tensor("out_fm", [D, T], F32, kind="ExternalOutput")
        if self.dbg:
            dram["dbg_d"] = nc.dram_tensor("dbg_d", [128, 65536], F32,
                                           kind="ExternalOutput")
        dram["xprime"] = nc.dram_tensor("xprime", [D, T], F32)
        dram["u1_d"] = nc.dram_tensor("u1_d", [128, 16, T], BF16)
        dram["u2_d"] = nc.dram_tensor("u2_d", [128, 64, T], BF16)
        dram["tdq_d"] = nc.dram_tensor("tdq_d", [16, 128, 8192], BF16)
        dram["opair_d"] = nc.dram_tensor("opair_d", [16, 128, T], BF16)
        dram["ag_in"] = nc.dram_tensor("ag_in", [8, 1536], F32)
        dram["ag_out"] = nc.dram_tensor("ag_out", [64, 1536], F32)
        dram["ar_in"] = nc.dram_tensor("ar_in", [1, 4], F32)
        dram["ar_out"] = nc.dram_tensor("ar_out", [1, 4], F32)

        sb = self.sb
        self.bigA1 = sb("bigA1", [128, 32768], BF16)
        self.sid_f = sb("sid_f", [128, 128], F32)
        self.sh128 = sb("sh128", [128, 128], BF16)
        self.si8h16 = sb("si8h16", [128, 128], BF16)
        self.si2h64 = sb("si2h64", [128, 128], BF16)
        self.spswap = sb("spswap", [128, 128], BF16)
        self.sccols = sb("sccols", [128, 8], F32)
        self.sones1 = sb("sones1", [1, 128], F32)
        self.sonescol = sb("sonescol", [128, 1], BF16)
        self.sonehot = sb("sonehot", [8, 1], F32)
        self.mod96 = sb("mod96", [128, 96], F32)
        self.onePca = sb("onePca", [128, 16], F32)
        self.onePcb = sb("onePcb", [128, 16], F32)
        self.gacols = sb("gacols", [128, 16], F32)
        self.gbcols = sb("gbcols", [128, 16], F32)
        self.wsrow = sb("wsrow", [1, 16], F32)
        self.wsg_i = sb("wsg_i", [128, 1], F32)
        self.wsu_i = sb("wsu_i", [128, 1], F32)
        self.wsd_i = sb("wsd_i", [128, 1], F32)
        self.srow_su = sb("srow_su", [1, T], F32)
        self.bc1 = sb("bc1", [128, T], F32)
        self.bc2 = sb("bc2", [128, T], F32)
        self.tmp1 = sb("tmp1", [128, TS], F32)
        self.tmp2 = sb("tmp2", [128, TS], F32)
        self.mcol = sb("mcol", [128, 16], F32)

        self.ps = [self.es.enter_context(self.nc.psum_tensor(f"ps{i}", [128, TS], F32))
                   for i in range(6)]
        self.ps_row = self.es.enter_context(self.nc.psum_tensor("ps_row", [1, TS], F32))
        self.ps_row2 = self.es.enter_context(self.nc.psum_tensor("ps_row2", [1, TS], F32))

    def hA(self, i):
        return self.bigA1[:, i * 2048:(i + 1) * 2048]

    def alloc_xstr(self, scope):
        self.xstr = [self.sb(f"xstr{i}", [128, T], F32, scope) for i in range(2)]

    def alloc_rows(self, scope, width=T):
        self.srow_a = self.sb("srow_a", [1, width], F32, scope)
        self.srow_b = self.sb("srow_b", [1, width], F32, scope)
        self.srow_c = self.sb("srow_c", [1, width], F32, scope)
        self.srow_d = self.sb("srow_d", [1, width], F32, scope)

    def p0_consts(self):
        d = self.dram
        pairs = [(self.sid_f, d["ident_f"]), (self.sh128, d["h128_b"]),
                 (self.si8h16, d["i8h16_b"]), (self.si2h64, d["i2h64_b"]),
                 (self.spswap, d["pswap_b"]), (self.sccols, d["ccols"]),
                 (self.sonehot, d["onehot"]), (self.sonescol, d["onescolb"]),
                 (self.sones1, d["ones1f"])]
        for dst, src in pairs:
            tok = self.dma("sync", "cst", dst[:, :], src.ap())
        self.CONSTS_D = tok

    def p1_mod_ws(self, scope):
        E, W, dma = self.E, self.W, self.dma
        d, ps = self.dram, self.ps
        siluc = self.sb("siluc", [128, 128], F32, scope)
        siluc_b = self.sb("siluc_b", [128, 128], BF16, scope)
        modsl = self.sb("modsl", [8, 1536], F32, scope)
        exsl = [self.sb(f"exsl{i}", [8, TS], F32, scope) for i in range(2)]
        rowbuf = [self.sb(f"rowbuf{i}", [1, TS], F32, scope) for i in range(2)]
        wsacc = self.sb("wsacc", [128, 4], F32, scope)
        wsred = self.sb("wsred", [128, 1], F32, scope)
        if DBG_WS:
            wsdbg = self.sb("wsdbg", [128, 24], F32, scope)
            self.tap("wsdbg", wsdbg, 128, 24)
        gtmp = self.sb("gtmp", [128, 64], F32, scope)
        wstr = [self.sb(f"p1w{i}", [128, 2048], F32, scope) for i in range(2)]
        wmf = [self.sb(f"p1m{i}", [128, 1536], F32, scope) for i in range(2)]
        wmb = [self.sb(f"p1mb{i}", [128, 1536], BF16, scope) for i in range(2)]

        cdone = self.dma("sync", "cst", siluc[:, :], d["c_t"].ap())
        self.WD("scalar", cdone)
        sa = E("scalar", lambda e: e.activation(siluc[:, :], siluc[:, :], AF.Silu),
               "sact")
        W("vector", "sact", sa)
        sg = E("vector", lambda e: e.tensor_copy(siluc_b[:, :], siluc[:, :]), "svec")

        slab_d, slab_v = [], []
        for t24 in range(24):
            m, i = divmod(t24, 8)
            src = (d["wgs"], d["wus"], d["wds"])[m]
            if t24 >= 2:
                W("sync", "svec", slab_v[t24 - 2])
            slab_d.append(dma("sync", f"sw{t24 % 2}", wstr[t24 % 2][:, :],
                              src.ap()[i, :, :]))
            self.WD("vector", slab_d[t24])
            rdv = E("vector", lambda e, b=t24 % 2: e.tensor_reduce(
                wsred[:, :], wstr[b][:, :], AX.X, ALU.add,
                apply_absolute_value=True), "svec")
            W("vector", "svec", rdv)
            if DBG_WS:
                E("vector", lambda e, t24=t24: e.tensor_copy(
                    wsdbg[:, t24:t24 + 1], wsred[:, :]))
            if i == 0:
                slab_v.append(E("vector", lambda e, m=m: e.tensor_copy(
                    wsacc[:, m:m + 1], wsred[:, :]), "svec"))
            else:
                slab_v.append(E("vector", lambda e, m=m: e.tensor_tensor(
                    wsacc[:, m:m + 1], wsacc[:, m:m + 1], wsred[:, :], ALU.add), "svec"))
        self.WD("tensor", self.CONSTS_D)
        W("tensor", "svec", slab_v[-1])
        wst = E("tensor", lambda e: e.matmul(self.ps_row[0:1, 0:4], self.sccols[:, 2:3],
                                             wsacc[:, :], start=True, stop=True), "stn")
        W("vector", "stn", wst)
        wsv = E("vector", lambda e: e.tensor_copy(self.wsrow[0:1, 0:4],
                                                  self.ps_row[0:1, 0:4]), "svec")

        W("tensor", "svec", sg)
        wm_d, wm_v, wm_t = [], [], []
        for k in range(16):
            if k >= 2:
                W("sync", "svec", wm_v[k - 2])
            wm_d.append(dma("sync", f"wm{k % 2}", wmf[k % 2][:, :],
                            d["wmod_sl"].ap()[k * 128:(k + 1) * 128, :]))
            self.WD("vector", wm_d[k])
            if k >= 2:
                W("vector", "stn", wm_t[k - 2])
            wm_v.append(E("vector", lambda e, b=k % 2: e.tensor_copy(
                wmb[b][:, :], wmf[b][:, :]), "svec"))
            W("tensor", "svec", wm_v[k])
            for n3 in range(3):
                v = E("tensor", lambda e, k=k, n3=n3: e.matmul(
                    ps[n3][0:8, :], siluc_b[:, k * 8:(k + 1) * 8],
                    wmb[k % 2][:, n3 * 512:(n3 + 1) * 512],
                    start=(k == 0), stop=(k == 15)), "stn")
            wm_t.append(v)
        W("vector", "stn", wm_t[-1])
        for n3 in range(3):
            mev = E("vector", lambda e, n3=n3: e.tensor_copy(
                modsl[0:8, n3 * 512:(n3 + 1) * 512], ps[n3][0:8, :]), "svec")

        W("gpsimd", "svec", mev)
        W("gpsimd", "svec", wsv)
        dma("gpsimd", "ccb", d["ag_in"].ap(), modsl[0:8, :])
        dma("gpsimd", "ccb", d["ar_in"].ap()[0:1, :], self.wsrow[0:1, 0:4])
        W("gpsimd", "ccb", self.val("ccb"))
        if NO_CC:
            for rr in range(8):
                self.dma("gpsimd", "ccf", d["ag_out"].ap()[rr * 8:rr * 8 + 8, :],
                         d["ag_in"].ap())
            self.dma("gpsimd", "ccf", d["ar_out"].ap(), d["ar_in"].ap())
        else:
            cch, cc2h = self.sems["cc"][0], self.sems["cc2"][0]
            E("gpsimd", lambda e: e.collective_compute(
                "AllGather", ALU.bypass, replica_groups=[list(range(NCORES))],
                ins=[d["ag_in"].ap().opt()],
                outs=[d["ag_out"].ap().opt()]).then_inc(cch), nofence=True)
            E("gpsimd", lambda e: e.collective_compute(
                "AllReduce", ALU.add, replica_groups=[list(range(NCORES))],
                ins=[d["ar_in"].ap().opt()],
                outs=[d["ar_out"].ap().opt()]).then_inc(cc2h), nofence=True)
            self.sems["cc"][1] = 1
            self.sems["cc2"][1] = 1
            W("gpsimd", "cc", 1)
            W("gpsimd", "cc2", 1)
        if NO_CC:
            W("gpsimd", "ccf", self.val("ccf"))
        ccd = dma("gpsimd", "ccb", self.wsrow[0:1, 4:8], d["ar_out"].ap()[0:1, :])

        self.WD("sync", ccd)
        ex_d, ex_t, ex_v, tp_t = [], [], [], []
        for n in range(24):
            if n >= 2:
                W("sync", "stn", ex_t[n - 2])
            r0, js = n // 3, (n % 3) * 512
            ex_d.append(dma("sync", f"ex{n % 2}", exsl[n % 2][0:8, :],
                            d["ag_out"].ap()[r0 * 8:r0 * 8 + 8, js:js + 512]))
            self.WD("tensor", ex_d[n])
            if n >= 2:
                W("tensor", "svec", ex_v[n - 2])
            ex_t.append(E("tensor", lambda e, n=n: e.matmul(
                ps[n % 2][0:1, :], self.sonehot[0:8, 0:1],
                exsl[n % 2][0:8, :], start=True, stop=True), "stn"))
            W("vector", "stn", ex_t[n])
            if n >= 2:
                W("vector", "stn", tp_t[n - 2])
            ex_v.append(E("vector", lambda e, n=n: e.tensor_copy(
                rowbuf[n % 2][0:1, :], ps[n % 2][0:1, :]), "svec"))
            W("tensor", "svec", ex_v[n])
            for j4 in range(4):
                tp = E("tensor", lambda e, n=n, j4=j4: e.transpose(
                    ps[5][:, n * 4 + j4:n * 4 + j4 + 1],
                    rowbuf[n % 2][0:1, j4 * 128:(j4 + 1) * 128],
                    self.sones1[0:1, 0:1]), "stn")
            tp_t.append(tp)
        W("vector", "stn", tp_t[-1])
        E("vector", lambda e: e.tensor_copy(self.mod96[:, :], ps[5][:, 0:96]))
        E("vector", lambda e: e.tensor_scalar(self.onePca[:, :], self.mod96[:, 16:32],
                                              1.0, None, ALU.add))
        E("vector", lambda e: e.tensor_scalar(self.onePcb[:, :], self.mod96[:, 64:80],
                                              1.0, None, ALU.add))
        for dst, off, lo in ((self.gacols, 32, 0), (self.gbcols, 80, 16)):
            E("vector", lambda e, off=off, lo=lo: e.tensor_scalar(
                gtmp[:, lo:lo + 16], self.mod96[:, off:off + 16], 0.0, None, ALU.is_ge))
            E("vector", lambda e, lo=lo: e.tensor_scalar(
                gtmp[:, lo:lo + 16], gtmp[:, lo:lo + 16], 2.0, -1.0, ALU.mult, ALU.add))
            E("vector", lambda e, off=off, lo=lo: e.tensor_scalar(
                gtmp[:, 32 + lo:48 + lo], self.mod96[:, off:off + 16], -1.0, None,
                ALU.mult))
            E("vector", lambda e, off=off, lo=lo: e.tensor_tensor(
                gtmp[:, 32 + lo:48 + lo], gtmp[:, 32 + lo:48 + lo],
                self.mod96[:, off:off + 16], ALU.max))
            E("vector", lambda e, lo=lo: e.tensor_scalar(
                gtmp[:, 32 + lo:48 + lo], gtmp[:, 32 + lo:48 + lo], 0.25, None, ALU.max))
            E("vector", lambda e, dst=dst, lo=lo: e.tensor_tensor(
                dst[:, :], gtmp[:, 32 + lo:48 + lo], gtmp[:, lo:lo + 16], ALU.mult))
        E("vector", lambda e: e.tensor_scalar(self.wsrow[0:1, 8:11], self.wsrow[0:1, 4:7],
                                              1.0 / NELW, EPS, ALU.mult, ALU.max))
        mv = E("vector", lambda e: e.reciprocal(self.wsrow[0:1, 12:15],
                                                self.wsrow[0:1, 8:11]), "svec")
        W("vector", "svec", mv)
        W("tensor", "svec", mv)
        for i in range(3):
            wb = E("tensor", lambda e, i=i: e.matmul(
                ps[4][:, i:i + 1], self.sones1[0:1, :], self.wsrow[0:1, 12 + i:13 + i],
                start=True, stop=True), "stn")
        W("vector", "stn", wb)
        E("vector", lambda e: e.tensor_copy(self.wsg_i[:, :], ps[4][:, 0:1]))
        E("vector", lambda e: e.tensor_copy(self.wsu_i[:, :], ps[4][:, 1:2]))
        E("vector", lambda e: e.tensor_copy(self.wsd_i[:, :], ps[4][:, 2:3]), "svec")
        self.tap("mod96", self.mod96, 128, 96)
        self.tap("wsrow", self.wsrow, 1, 16)
        self.tap("wsacc", wsacc, 128, 4)

    def ln_stats(self, src_dram, xbf, xsq):
        srow_a, srow_b, srow_c, srow_d = (self.srow_a, self.srow_b,
                                          self.srow_c, self.srow_d)
        xstr = self.xstr
        E, W, dma = self.E, self.W, self.dma
        ps = self.ps
        sq_ps = [ps[4][0:1, :], ps[5][0:1, :], self.ps_row[0:1, :], self.ps_row2[0:1, :]]
        cast_v, sq_a, mm_t = [], [], []
        for c in range(NK):
            if c >= 2:
                W("sync", "svec", cast_v[c - 2])
            xd = dma("sync", f"xs{c % 2}", xstr[c % 2][:, :],
                     src_dram.ap()[c * 128:(c + 1) * 128, :])
            self.WD("vector", xd)
            if c >= 2:
                W("vector", "stn", mm_t[c - 2])
            cast_v.append(E("vector", lambda e, b=c % 2: e.tensor_copy(
                xbf[b][:, :], xstr[b][:, :]), "svec"))
            W("scalar", "svec", cast_v[c])
            if c >= 2:
                W("scalar", "stn", mm_t[c - 2])
            sq_a.append(E("scalar", lambda e, b=c % 2: e.activation(
                xsq[b][:, :], xbf[b][:, :], AF.Square), "sact"))
            W("tensor", "sact", sq_a[c])
            for t4 in range(NTS):
                E("tensor", lambda e, c=c, t4=t4: e.matmul(
                    ps[t4][0:1, :], self.sonescol[:, :],
                    xbf[c % 2][:, t4 * TS:(t4 + 1) * TS],
                    start=(c == 0), stop=(c == NK - 1)))
                mm = E("tensor", lambda e, c=c, t4=t4, tgt=sq_ps[t4]: e.matmul(
                    tgt, self.sonescol[:, :], xsq[c % 2][:, t4 * TS:(t4 + 1) * TS],
                    start=(c == 0), stop=(c == NK - 1)), "stn")
            mm_t.append(mm)
        W("vector", "stn", mm_t[-1])
        for t4 in range(NTS):
            E("vector", lambda e, t4=t4: e.tensor_copy(
                srow_a[0:1, t4 * TS:(t4 + 1) * TS], ps[t4][0:1, :]))
            E("vector", lambda e, t4=t4, s=sq_ps[t4]: e.tensor_copy(
                srow_b[0:1, t4 * TS:(t4 + 1) * TS], s))
        E("vector", lambda e: e.tensor_scalar(srow_a[0:1, :], srow_a[0:1, :],
                                              1.0 / D, None, ALU.mult))
        E("vector", lambda e: e.tensor_tensor(srow_c[0:1, :], srow_a[0:1, :],
                                              srow_a[0:1, :], ALU.mult))
        rv = E("vector", lambda e: e.scalar_tensor_tensor(
            srow_b[0:1, :], srow_b[0:1, :], 1.0 / D, srow_c[0:1, :],
            ALU.mult, ALU.subtract), "svec")
        W("scalar", "svec", rv)
        ra = E("scalar", lambda e: e.activation(
            srow_d[0:1, :], srow_b[0:1, :], AF.Sqrt,
            bias=self.sccols[0:1, 1:2]), "sact")
        W("vector", "sact", ra)
        rv2 = E("vector", lambda e: e.reciprocal(srow_b[0:1, :],
                                                 srow_d[0:1, :]), "svec")
        W("vector", "svec", rv2)
        self.bcast_rows([(srow_a, self.bc1, T), (srow_b, self.bc2, T)], rv2)

    def bcast_rows(self, jobs, ready_v):
        E, W = self.E, self.W
        ps = self.ps
        W("tensor", "svec", ready_v)
        i = 0
        last_ev = None
        for row, dstt, width in jobs:
            for t4 in range(width // TS):
                bank = i % 4
                if last_ev is not None and i >= 4:
                    W("tensor", "svec", evs[i - 4])
                mm = E("tensor", lambda e, row=row, t4=t4, bank=bank: e.matmul(
                    ps[bank][:, :], self.sones1[0:1, :],
                    row[0:1, t4 * TS:(t4 + 1) * TS], start=True, stop=True), "stn")
                W("vector", "stn", mm)
                ev = E("vector", lambda e, dstt=dstt, t4=t4, bank=bank: e.tensor_copy(
                    dstt[:, t4 * TS:(t4 + 1) * TS], ps[bank][:, :]), "svec")
                if i == 0:
                    evs = {}
                evs[i] = ev
                last_ev = ev
                i += 1
        return last_ev

    def p4_h(self, src_dram, scale_tile, bias_off, dst_fn, xln):
        xstr = self.xstr
        E, W, dma = self.E, self.W, self.dma
        sub_v, act_a = [], []
        for c in range(NK):
            if c >= 2:
                W("sync", "svec", sub_v[c - 2])
            xd = dma("sync", f"xs{c % 2}", xstr[c % 2][:, :],
                     src_dram.ap()[c * 128:(c + 1) * 128, :])
            self.WD("vector", xd)
            if c >= 1:
                W("vector", "sact", act_a[c - 1])
            E("vector", lambda e, b=c % 2: e.tensor_tensor(
                xln[:, :], xstr[b][:, :], self.bc1[:, :], ALU.subtract))
            sub_v.append(E("vector", lambda e: e.tensor_tensor(
                xln[:, :], xln[:, :], self.bc2[:, :], ALU.mult), "svec"))
            W("scalar", "svec", sub_v[c])
            act_a.append(E("scalar", lambda e, c=c, st=scale_tile, bo=bias_off:
                           e.activation(dst_fn(c)[:, :], xln[:, :], AF.Identity,
                                        scale=st[:, c:c + 1],
                                        bias=self.mod96[:, bo + c:bo + c + 1]), "sact"))
        return act_a[-1]

    def p5_kv(self, scope):
        E, W, dma = self.E, self.W, self.dma
        d, ps = self.dram, self.ps
        self.kfm = [self.sb(f"kfm{i}", [128, T], BF16, scope) for i in range(8)]
        self.vaug = [self.sb(f"vaug{i}", [128, 520], BF16, scope) for i in range(16)]
        self.qpair = self.sb("qpair", [128, T], BF16, scope)
        self.qraw = self.sb("qraw", [128, TS], BF16, scope)
        wv_all = self.sb("wv_all", [128, 8192], BF16, scope)
        sc5 = contextlib.ExitStack()
        rope_ck = self.bc1
        rope_sk = self.bc2
        wstr = [self.sb("p5w0", [128, 2048], F32, sc5)]
        wstr.append(wstr[0])
        wstrb = [self.sb("p5wb0", [128, 2048], BF16, sc5)]
        wstrb.append(wstrb[0])

        ckd = dma("sync", "cst", rope_ck[:, :], d["cosk"].ap())
        skd = dma("sync", "cst", rope_sk[:, :], d["sink"].ap())

        wk_d, wk_g, mm_last = [], [], []
        rope_done = []
        for o in range(8):
            if o >= 1:
                W("sync", "sact", wk_g[o - 1])
            wk_d.append(dma("sync", "wk", wstr[o % 2][:, :], d["wk_b"].ap()[o, :, :]))
            self.WD("scalar", wk_d[o])
            if o >= 1:
                W("scalar", "stn", mm_last[o - 1])
            wk_g.append(E("scalar", lambda e, b=o % 2: e.copy(
                wstrb[b][:, :], wstr[b][:, :]), "sact"))
            W("tensor", "sact", wk_g[o])
            if o == 0:
                self.WD("tensor", skd)
                self.WD("vector", skd)
                swv_prev = None
            for t4 in range(4):
                if t4 >= 2 or o > 0:
                    W("tensor", "svec", rope_done[-2] if len(rope_done) >= 2 else None)
                for k in range(16):
                    mmv = E("tensor", lambda e, o=o, t4=t4, k=k: e.matmul(
                        ps[t4 % 2][:, :], wstrb[o % 2][:, k * 128:(k + 1) * 128],
                        self.hA(k)[:, t4 * TS:(t4 + 1) * TS],
                        start=(k == 0), stop=(k == 15)), "stn")
                W("scalar", "stn", mmv)
                if swv_prev is not None:
                    W("scalar", "stn", swv_prev)
                arv = E("scalar", lambda e, t4=t4: e.activation(
                    self.qraw[:, :], ps[t4 % 2][:, :], AF.Copy), "sact")
                W("tensor", "sact", arv)
                swv_prev = E("tensor", lambda e, t4=t4: e.matmul(
                    ps[2 + t4 % 2][:, :], self.spswap[:, :], self.qraw[:, :],
                    start=True, stop=True), "stn")
                W("vector", "stn", swv_prev)
                E("vector", lambda e, t4=t4: e.tensor_tensor(
                    self.tmp1[:, 0:TS], ps[t4 % 2][:, :],
                    rope_ck[:, t4 * TS:(t4 + 1) * TS], ALU.mult))
                E("vector", lambda e, t4=t4: e.tensor_tensor(
                    self.tmp2[:, 0:TS], ps[2 + t4 % 2][:, :],
                    rope_sk[:, t4 * TS:(t4 + 1) * TS], ALU.mult))
                rope_done.append(E("vector", lambda e, o=o, t4=t4: e.tensor_tensor(
                    self.kfm[o][:, t4 * TS:(t4 + 1) * TS], self.tmp1[:, 0:TS],
                    self.tmp2[:, 0:TS], ALU.add), "svec"))
            mm_last.append(self.val("stn"))
        self.K_DONE = rope_done[-1]

        for i in range(16):
            for h8 in range(8):
                E("gpsimd", lambda e, i=i, h8=h8: e.memset(
                    self.vaug[i][:, h8 * 65 + 64:h8 * 65 + 65], 1.0))
        vones = E("gpsimd", lambda e, i=15: e.tensor_copy(
            self.vaug[15][:, 519:520], self.vaug[15][:, 519:520]), "sgps")
        W("sync", "sact", wk_g[-1])
        for qv4 in range(4):
            if qv4 >= 1:
                W("sync", "sact", wvg)
            wvd = dma("sync", "wv", wstr[0].ap().rearrange("p (a v) -> p a v", a=4),
                      d["wv_t"].ap()[qv4 * 512:(qv4 + 1) * 512, :]
                      .rearrange("(a p) v -> p a v", p=128))
            self.WD("scalar", wvd)
            wvg = E("scalar", lambda e, qv4=qv4: e.copy(
                wv_all[:, qv4 * 2048:(qv4 + 1) * 2048], wstr[0][:, :]), "sact")
        W("tensor", "sact", wvg)
        v_ev = []
        for i in range(16):
            bank = 4 + i % 2
            if i >= 2:
                W("tensor", "svec", v_ev[i - 2])
            for k in range(16):
                vmm = E("tensor", lambda e, i=i, k=k, bank=bank: e.matmul(
                    ps[bank][:, :], self.hA(k)[:, i * 128:(i + 1) * 128],
                    wv_all[:, k * 512:(k + 1) * 512],
                    start=(k == 0), stop=(k == 15)), "stn")
            W("vector", "stn", vmm)
            if i == 0:
                W("vector", "sgps", vones)
            for h8 in range(8):
                vv = E("vector", lambda e, i=i, h8=h8, bank=bank: e.tensor_copy(
                    self.vaug[i][:, h8 * 65:h8 * 65 + 64],
                    ps[bank][:, h8 * 64:(h8 + 1) * 64]), "svec")
            v_ev.append(vv)
        self.V_DONE = v_ev[-1]
        self.barrier()
        sc5.close()

    def p6_attention(self, scope):
        E, W, dma = self.E, self.W, self.dma
        d, ps = self.dram, self.ps
        ptile = [self.sb(f"ptile{i}", [128, TS], BF16, scope) for i in range(4)]
        opevA = self.sb("opevA", [64, T], BF16, scope)
        opevB = self.sb("opevB", [64, T], BF16, scope)
        recrow = self.sb("recrow", [1, TS], F32, scope)
        recrow2 = self.sb("recrow2", [1, TS], F32, scope)
        bcsbA = self.sb("bcsbA", [64, TS], F32, scope)
        bcsbB = self.sb("bcsbB", [64, TS], F32, scope)
        rope_c = self.bc1
        rope_s = self.bc2
        self.smasks = self.sb("smasks", [128, 4 * TS], BF16, scope)
        wstr = [self.sb("p6w0", [128, 2048], F32, scope)]
        wstr.append(wstr[0])
        wstrb = [self.sb("p6wb0", [128, 2048], BF16, scope)]
        wstrb.append(wstrb[0])

        W("sync", "svec", self.K_DONE)
        dma("sync", "cst", self.smasks[:, :], d["masks"].ap())
        cqd = dma("sync", "cst", rope_c[:, :], d["cosq"].ap())
        sqd = dma("sync", "cst", rope_s[:, :], d["sinq"].ap())
        self.WD("tensor", sqd)
        self.WD("vector", sqd)
        W("tensor", "svec", self.K_DONE)
        W("tensor", "svec", self.V_DONE)

        wq_g, q_mm_done, op_dma = [], [], []
        prev_o_last_smm = None
        prev_ov = None
        swv_prev = None
        prev_pv_last = None
        for o in range(NPAIR):
            if o >= 1:
                W("sync", "svec", wq_g[o - 1])
            wqd = dma("sync", "wq", wstr[o % 2][:, :], d["wq_b"].ap()[o, :, :])
            self.WD("vector", wqd)
            if o >= 1:
                W("vector", "stn", q_mm_done[o - 1])
            wq_g.append(E("vector", lambda e, b=o % 2: e.tensor_copy(
                wstrb[b][:, :], wstr[b][:, :]), "svec"))
            W("tensor", "svec", wq_g[o])
            rope_v = []
            for t4 in range(4):
                if prev_ov is not None and t4 == 0:
                    W("tensor", "svec", prev_ov)
                if t4 >= 2:
                    W("tensor", "svec", rope_v[t4 - 2])
                for k in range(16):
                    mmv = E("tensor", lambda e, o=o, t4=t4, k=k: e.matmul(
                        ps[t4 % 2][:, :], wstrb[o % 2][:, k * 128:(k + 1) * 128],
                        self.hA(k)[:, t4 * TS:(t4 + 1) * TS],
                        start=(k == 0), stop=(k == 15)), "stn")
                W("scalar", "stn", mmv)
                if swv_prev is not None:
                    W("scalar", "stn", swv_prev)
                arv = E("scalar", lambda e, t4=t4: e.activation(
                    self.qraw[:, :], ps[t4 % 2][:, :], AF.Copy), "sact")
                W("tensor", "sact", arv)
                swv_prev = E("tensor", lambda e, t4=t4: e.matmul(
                    ps[2 + t4 % 2][:, :], self.spswap[:, :], self.qraw[:, :],
                    start=True, stop=True), "stn")
                W("vector", "stn", swv_prev)
                if prev_o_last_smm is not None and t4 == 0:
                    W("vector", "stn", prev_o_last_smm)
                E("vector", lambda e, t4=t4: e.tensor_tensor(
                    self.tmp1[:, 0:TS], ps[t4 % 2][:, :],
                    rope_c[:, t4 * TS:(t4 + 1) * TS], ALU.mult))
                E("vector", lambda e, t4=t4: e.tensor_tensor(
                    self.tmp2[:, 0:TS], ps[2 + t4 % 2][:, :],
                    rope_s[:, t4 * TS:(t4 + 1) * TS], ALU.mult))
                rope_v.append(E("vector", lambda e, t4=t4: e.tensor_tensor(
                    self.qpair[:, t4 * TS:(t4 + 1) * TS], self.tmp1[:, 0:TS],
                    self.tmp2[:, 0:TS], ALU.add), "svec"))
            q_mm_done.append(self.val("stn"))
            kvch = o // 2
            vgrp = o // 2
            W("tensor", "svec", rope_v[-1])
            ov = None
            for qc in range(NTS):
                njt = 4 * (qc + 1)
                exp_a, pv_t, mk_v = {}, {}, {}
                if ov is not None:
                    W("tensor", "svec", ov)
                if prev_pv_last is not None:
                    W("scalar", "stn", prev_pv_last)
                for loc in range(2):
                    for j in range(njt):
                        sbank = (2 if j % 2 == 0 else 4) + loc
                        pti = 2 * (j % 2) + loc
                        if (loc, j - 2) in exp_a:
                            W("tensor", "sact", exp_a[(loc, j - 2)])
                        smm = E("tensor", lambda e, j=j, qc=qc, loc=loc, sbank=sbank,
                                kvch=kvch: e.matmul(
                                    ps[sbank][:, :],
                                    self.kfm[kvch][loc * 64:loc * 64 + 64,
                                                   j * 128:(j + 1) * 128],
                                    self.qpair[loc * 64:loc * 64 + 64,
                                               qc * TS:(qc + 1) * TS],
                                    start=True, stop=True), "stn")
                        W("scalar", "stn", smm)
                        if (loc, j - 2) in pv_t:
                            W("scalar", "stn", pv_t[(loc, j - 2)])
                        if (loc, j - 2) in mk_v:
                            W("scalar", "svec", mk_v[(loc, j - 2)])
                        ea = E("scalar", lambda e, sbank=sbank, pti=pti: e.activation(
                            ptile[pti][:, :], ps[sbank][:, :], AF.Exp,
                            bias=self.sccols[:, 0:1]), "sact")
                        exp_a[(loc, j)] = ea
                        dmask = j - 4 * qc
                        if dmask >= 0:
                            W("vector", "sact", ea)
                            mv = E("vector", lambda e, pti=pti, dm=dmask: e.tensor_tensor(
                                ptile[pti][:, :], ptile[pti][:, :],
                                self.smasks[:, dm * TS:(dm + 1) * TS], ALU.mult), "svec")
                            mk_v[(loc, j)] = mv
                            W("tensor", "svec", mv)
                        else:
                            W("tensor", "sact", ea)
                        pv = E("tensor", lambda e, j=j, loc=loc, pti=pti, vgrp=vgrp,
                               njt=njt: e.matmul(
                                   ps[loc][0:65, :],
                                   self.vaug[j][:, vgrp * 65:vgrp * 65 + 65],
                                   ptile[pti][:, :],
                                   start=(j == 0), stop=(j == njt - 1)), "stn")
                        pv_t[(loc, j)] = pv
                prev_o_last_smm = self.val("stn")
                prev_pv_last = pv_t[(1, njt - 1)]
                W("vector", "stn", pv_t[(1, njt - 1)])
                E("vector", lambda e: e.reciprocal(recrow[0:1, :], ps[0][64:65, :]))
                rv = E("vector", lambda e: e.reciprocal(recrow2[0:1, :],
                                                        ps[1][64:65, :]), "svec")
                W("tensor", "svec", rv)
                E("tensor", lambda e: e.matmul(
                    ps[2][0:64, :], self.sones1[0:1, 0:64], recrow[0:1, :],
                    start=True, stop=True), "stn")
                b2 = E("tensor", lambda e: e.matmul(
                    ps[3][0:64, :], self.sones1[0:1, 0:64], recrow2[0:1, :],
                    start=True, stop=True), "stn")
                W("vector", "stn", b2)
                if op_dma:
                    self.WD("vector", op_dma[-1])
                E("vector", lambda e: e.tensor_copy(bcsbA[:, :], ps[2][0:64, :]))
                E("vector", lambda e: e.tensor_copy(bcsbB[:, :], ps[3][0:64, :]))
                E("vector", lambda e, qc=qc: e.tensor_tensor(
                    opevA[:, qc * TS:(qc + 1) * TS], ps[0][0:64, :], bcsbA[:, :],
                    ALU.mult))
                ov = E("vector", lambda e, qc=qc: e.tensor_tensor(
                    opevB[:, qc * TS:(qc + 1) * TS], ps[1][0:64, :], bcsbB[:, :],
                    ALU.mult), "svec")
            prev_ov = ov
            W("gpsimd", "svec", ov)
            dma("gpsimd", "opo", d["opair_d"].ap()[o, 0:64, :], opevA[:, :])
            op_dma.append(dma("gpsimd", "opo", d["opair_d"].ap()[o, 64:128, :],
                              opevB[:, :]))
        self.ATT_DMA = op_dma[-1]
        self.tap("qpair", self.qpair, 128, 2048)
        self.tap("pt3", ptile[3], 128, 512)
        self.tap("rec2", recrow2, 1, 512)
        self.tap("bcsbB", bcsbB, 64, 512)
        self.tap("opevA", opevA, 64, 2048)
        self.tap("opevB", opevB, 64, 2048)

    def p7_outproj(self, scope):
        xstr = self.xstr
        E, W, dma = self.E, self.W, self.dma
        d, ps = self.dram, self.ps
        wstr = [self.sb(f"p7w{i}", [128, 2048], F32, scope) for i in range(2)]
        wstrb = [self.sb(f"p7wb{i}", [128, 2048], BF16, scope) for i in range(2)]
        xnout = [self.sb(f"xnout{i}", [128, T], F32, scope) for i in range(2)]
        xln = self.sb("p7xln", [128, T], F32, scope)

        self.WD("sync", self.ATT_DMA)
        o_in = []
        for i in range(16):
            o_in.append(dma("sync", "oin", self.hA(i)[:, :], d["opair_d"].ap()[i, :, :]))
        wo_g, xp_dma = [], []
        mm_done = []
        ep_a, ep_v = {}, {}
        for o2 in range(16):
            if o2 >= 2:
                W("sync", "sact", wo_g[o2 - 2])
            wod = dma("sync", f"wo{o2 % 2}", wstr[o2 % 2][:, :], d["wo_b"].ap()[o2, :, :])
            if o2 >= 2:
                W("sync", "svec", ep_v[(o2 - 2, 3)])
            xrd = dma("sync", f"xs{o2 % 2}", xstr[o2 % 2][:, :],
                      d["x_fm"].ap()[o2 * 128:(o2 + 1) * 128, :])
            self.WD("scalar", wod)
            if o2 >= 2:
                W("scalar", "stn", mm_done[o2 - 2])
            wo_g.append(E("scalar", lambda e, b=o2 % 2: e.copy(
                wstrb[b][:, :], wstr[b][:, :]), "sact"))
            W("tensor", "sact", wo_g[o2])
            if o2 == 0:
                self.WD("tensor", o_in[-1])
            for t4 in range(4):
                if (o2, t4 - 2) in ep_a:
                    W("tensor", "sact", ep_a[(o2, t4 - 2)])
                elif o2 >= 1:
                    W("tensor", "sact", ep_a[(o2 - 1, t4 + 2)])
                for k in range(16):
                    mmv = E("tensor", lambda e, o2=o2, t4=t4, k=k: e.matmul(
                        ps[t4 % 2][:, :], wstrb[o2 % 2][:, k * 128:(k + 1) * 128],
                        self.hA(k)[:, t4 * TS:(t4 + 1) * TS],
                        start=(k == 0), stop=(k == 15)), "stn")
                W("scalar", "stn", mmv)
                if (o2 - 1, t4) in ep_v:
                    W("scalar", "svec", ep_v[(o2 - 1, t4)])
                av = E("scalar", lambda e, o2=o2, t4=t4: e.activation(
                    xln[:, t4 * TS:(t4 + 1) * TS], ps[t4 % 2][:, :],
                    AF.Identity, scale=self.gacols[:, o2:o2 + 1]), "sact")
                ep_a[(o2, t4)] = av
                W("vector", "sact", av)
                if t4 == 0:
                    self.WD("vector", xrd)
                    if o2 >= 2:
                        self.WD("vector", xp_dma[o2 - 2])
                ep_v[(o2, t4)] = E("vector", lambda e, o2=o2, t4=t4: e.tensor_tensor(
                    xnout[o2 % 2][:, t4 * TS:(t4 + 1) * TS],
                    xln[:, t4 * TS:(t4 + 1) * TS],
                    xstr[o2 % 2][:, t4 * TS:(t4 + 1) * TS], ALU.add), "svec")
            mm_done.append(self.val("stn"))
            W("sync", "svec", ep_v[(o2, 3)])
            xp_dma.append(dma("sync", f"xp{o2 % 2}",
                              d["xprime"].ap()[o2 * 128:(o2 + 1) * 128, :],
                              xnout[o2 % 2][:, :]))
        self.XP_DMA = xp_dma[-1]
        self.tap("xn15", xnout[1], 128, 2048)
        self.tap("gacols", self.gacols, 128, 16)

    def p9_h2_fwht(self, scope):
        srow_a, srow_b, srow_c, srow_d = (self.srow_a, self.srow_b,
                                          self.srow_c, self.srow_d)
        xstr = self.xstr
        E, W, dma = self.E, self.W, self.dma
        d, ps = self.dram, self.ps
        xbf = [self.sb(f"p9b{i}", [128, 2048], BF16, scope) for i in range(2)]
        u1ev = [self.sb(f"u1ev{i}", [128, TS], BF16, scope) for i in range(2)]
        xln = self.sb("p9xln", [128, T], F32, scope)
        mxp = self.sb("mxp9", [128, T], F32, scope)
        mnp = self.sb("mnp9", [128, T], F32, scope)
        zq = self.sb("zq9", [128, TS], F32, scope)
        srow_sg = self.sb("srow_sg", [1, T], F32, scope)
        self.srow_sg = srow_sg

        sub_v, act_a, st1_t, u1_dma, uev_a = [], [], {}, [], []
        for c in range(NK):
            if c >= 2:
                W("sync", "svec", sub_v[c - 2])
            xd = dma("sync", f"xs{c % 2}", xstr[c % 2][:, :],
                     d["xprime"].ap()[c * 128:(c + 1) * 128, :])
            self.WD("vector", xd)
            if c >= 1:
                W("vector", "sact", act_a[c - 1])
            E("vector", lambda e, b=c % 2: e.tensor_tensor(
                xln[:, :], xstr[b][:, :], self.bc1[:, :], ALU.subtract))
            sub_v.append(E("vector", lambda e: e.tensor_tensor(
                xln[:, :], xln[:, :], self.bc2[:, :], ALU.mult), "svec"))
            W("scalar", "svec", sub_v[c])
            if c >= 2:
                W("scalar", "stn", st1_t[(c - 2, 3)])
            act_a.append(E("scalar", lambda e, c=c: e.activation(
                xbf[c % 2][:, :], xln[:, :], AF.Identity,
                scale=self.onePcb[:, c:c + 1],
                bias=self.mod96[:, 48 + c:49 + c]), "sact"))
            W("tensor", "sact", act_a[c])
            for t4 in range(4):
                it = c * 4 + t4
                if it >= 2:
                    W("tensor", "sact", uev_a[it - 2])
                stv = E("tensor", lambda e, c=c, t4=t4: e.matmul(
                    ps[t4 % 2][:, :], self.sh128[:, :],
                    xbf[c % 2][:, t4 * TS:(t4 + 1) * TS], start=True, stop=True), "stn")
                st1_t[(c, t4)] = stv
                W("scalar", "stn", stv)
                if it >= 2:
                    self.WD("scalar", u1_dma[it - 2])
                ua = E("scalar", lambda e, t4=t4: e.activation(
                    u1ev[t4 % 2][:, :], ps[t4 % 2][:, :], AF.Copy), "sact")
                uev_a.append(ua)
                W("gpsimd", "sact", ua)
                u1_dma.append(dma("gpsimd", f"ub{t4 % 2}",
                                  d["u1_d"].ap()[:, c, t4 * TS:(t4 + 1) * TS],
                                  u1ev[t4 % 2][:, :]))
        W("gpsimd", "ub0", self.val("ub0"))
        W("gpsimd", "ub1", self.val("ub1"))
        v1_in = []
        for g in range(16):
            v1_in.append(dma("gpsimd", "vin", self.hA(g)[:, :],
                             d["u1_d"].ap()[8 * g:8 * g + 8, :, :]
                             .rearrange("e c t -> (e c) t")))
        self.WD("tensor", v1_in[-1])
        mx_v = {}
        for g in range(16):
            for t4 in range(4):
                it = g * 4 + t4
                bank = 2 + it % 2
                if it >= 2:
                    W("tensor", "svec", mx_v[it - 2])
                pa = E("tensor", lambda e, g=g, t4=t4, bank=bank: e.matmul(
                    ps[bank][:, :], self.si8h16[:, :],
                    self.hA(g)[:, t4 * TS:(t4 + 1) * TS], start=True, stop=True), "stn")
                W("vector", "stn", pa)
                if g == 0:
                    E("vector", lambda e, t4=t4, bank=bank: e.tensor_copy(
                        mxp[:, t4 * TS:(t4 + 1) * TS], ps[bank][:, :]))
                    v = E("vector", lambda e, t4=t4, bank=bank: e.tensor_copy(
                        mnp[:, t4 * TS:(t4 + 1) * TS], ps[bank][:, :]), "svec")
                else:
                    E("vector", lambda e, t4=t4, bank=bank: e.tensor_tensor(
                        mxp[:, t4 * TS:(t4 + 1) * TS], mxp[:, t4 * TS:(t4 + 1) * TS],
                        ps[bank][:, :], ALU.max))
                    v = E("vector", lambda e, t4=t4, bank=bank: e.tensor_tensor(
                        mnp[:, t4 * TS:(t4 + 1) * TS], mnp[:, t4 * TS:(t4 + 1) * TS],
                        ps[bank][:, :], ALU.min), "svec")
                mx_v[it] = v
        E("vector", lambda e: e.tensor_scalar(mnp[:, :], mnp[:, :], -1.0, None, ALU.mult))
        mab = E("vector", lambda e: e.tensor_tensor(mxp[:, :], mxp[:, :], mnp[:, :],
                                                    ALU.max), "svec")
        mrow = self.xpart_max(mxp, T, mab, srow_c)
        E("vector", lambda e: e.tensor_scalar(srow_d[0:1, :], srow_c[0:1, :],
                                              C2048, EPS, ALU.mult, ALU.max))
        rcv = E("vector", lambda e: e.reciprocal(srow_a[0:1, :], srow_d[0:1, :]),
                "svec")
        W("vector", "svec", rcv)
        E("vector", lambda e: e.tensor_scalar(srow_a[0:1, :], srow_a[0:1, :],
                                              127.0 * C2048, None, ALU.mult))
        E("vector", lambda e: e.tensor_scalar(
            srow_sg[0:1, :], srow_d[0:1, :], self.wsrow[0:1, 8:9],
            1.0 / 127.0, ALU.mult, ALU.mult))
        srv = E("vector", lambda e: e.tensor_scalar(
            self.srow_su[0:1, :], srow_d[0:1, :], self.wsrow[0:1, 9:10],
            1.0 / 127.0, ALU.mult, ALU.mult), "svec")
        bcv = self.bcast_rows([(srow_a, self.bc2, T), (srow_sg, self.bc1, T)],
                              srv)
        W("tensor", "svec", bcv)
        rq = {}
        for g in range(16):
            for t4 in range(4):
                it = g * 4 + t4
                bank = 2 + it % 2
                if it >= 2:
                    W("tensor", "svec", rq[it - 2])
                pb = E("tensor", lambda e, g=g, t4=t4, bank=bank: e.matmul(
                    ps[bank][:, :], self.si8h16[:, :],
                    self.hA(g)[:, t4 * TS:(t4 + 1) * TS], start=True, stop=True), "stn")
                W("vector", "stn", pb)
                E("vector", lambda e, t4=t4, bank=bank: e.tensor_tensor(
                    zq[:, :], ps[bank][:, :], self.bc2[:, t4 * TS:(t4 + 1) * TS],
                    ALU.mult))
                rq[it] = E("vector", lambda e, g=g, t4=t4: e.tensor_scalar(
                    self.hA(g)[:, t4 * TS:(t4 + 1) * TS], zq[:, :],
                    MAGIC, -MAGIC, ALU.add, ALU.add), "svec")
        self.R_DONE = rq[63]
        self.tap("R0", self.hA(0), 128, 2048)

    def xpart_max(self, mtile, width, ready_v, out_row):
        E, W = self.E, self.W
        ps = self.ps
        W("tensor", "svec", ready_v)
        ns = width // 128
        tc = []
        for s in range(ns):
            bank = 2 + s % 2
            if s >= 2:
                W("tensor", "svec", tc[s - 2])
            tv = E("tensor", lambda e, s=s, bank=bank: e.transpose(
                ps[bank][:, 0:128], mtile[:, s * 128:(s + 1) * 128],
                self.sid_f[:, :]), "stn")
            W("vector", "stn", tv)
            tc.append(E("vector", lambda e, s=s, bank=bank: e.tensor_reduce(
                self.mcol[:, s:s + 1], ps[bank][:, 0:128], AX.X, ALU.max), "svec"))
        W("tensor", "svec", tc[-1])
        rv = []
        for s in range(ns):
            pr = self.ps_row if s % 2 == 0 else self.ps_row2
            if s >= 2:
                W("tensor", "svec", rv[s - 2])
            tv = E("tensor", lambda e, s=s, pr=pr: e.transpose(
                pr[0:1, 0:128], self.mcol[:, s:s + 1], self.sid_f[:, :]), "stn")
            W("vector", "stn", tv)
            rv.append(E("vector", lambda e, s=s, pr=pr, out_row=out_row: e.tensor_copy(
                out_row[0:1, s * 128:(s + 1) * 128], pr[0:1, 0:128]), "svec"))
        return rv[-1]

    def p10_tdq(self, scope):
        E, W, dma = self.E, self.W, self.dma
        d = self.dram
        wstr = [self.sb(f"pAw{i}", [128, 2048], F32, scope) for i in range(2)]
        wstrb = [self.sb(f"pAwb{i}", [128, 2048], BF16, scope) for i in range(2)]
        qt = self.sb("qt10", [128, 2048], F32, scope)
        tq, tdma = [], []
        for i in range(64):
            o2, kq = divmod(i, 4)
            if i >= 2:
                W("sync", "svec", tq[i - 2])
            dd = dma("sync", f"tf{i % 2}", wstr[i % 2][:, :],
                     d["td_b"].ap()[o2, :, kq * 2048:(kq + 1) * 2048])
            self.WD("vector", dd)
            if i >= 2:
                self.WD("vector", tdma[i - 2])
            E("vector", lambda e, b=i % 2: e.tensor_scalar(
                qt[:, :], wstr[b][:, :], self.wsd_i[:, :], None, ALU.mult))
            E("vector", lambda e: e.tensor_scalar(
                qt[:, :], qt[:, :], 1.0, -1.0, ALU.min, ALU.max))
            tq.append(E("vector", lambda e, b=i % 2: e.tensor_scalar(
                wstrb[b][:, :], qt[:, :], MAGIC, -MAGIC, ALU.add, ALU.add), "svec"))
            W("gpsimd", "svec", tq[-1])
            tdma.append(dma("gpsimd", f"tq{i % 2}",
                            d["tdq_d"].ap()[o2, :, kq * 2048:(kq + 1) * 2048],
                            wstrb[i % 2][:, :]))
        self.TDQ_TOT = [("tq0", self.val("tq0")), ("tq1", self.val("tq1"))]

    def p11_gateup(self, scope):
        E, W, dma = self.E, self.W, self.dma
        d, ps = self.dram, self.ps
        wstr = [self.sb(f"pBg{i}", [128, 2048], F32, scope) for i in range(2)]
        wstrb = [self.sb(f"pBgb{i}", [128, 2048], BF16, scope) for i in range(2)]
        wstr2 = [self.sb(f"pBu{i}", [128, 2048], F32, scope) for i in range(2)]
        wstr2b = [self.sb(f"pBub{i}", [128, 2048], BF16, scope) for i in range(2)]
        qt = self.sb("qt11", [128, 4096], F32, scope)
        zt1 = [self.sb(f"zt1_{i}", [128, TS], F32, scope) for i in range(2)]
        zt2 = [self.sb(f"zt2_{i}", [128, TS], F32, scope) for i in range(2)]
        zbf = [self.sb(f"zbf{i}", [128, TS], BF16, scope) for i in range(2)]
        u2ev = [self.sb(f"u2ev{i}", [128, TS], BF16, scope) for i in range(2)]

        gq, gq_a, u2_dma = [], [], []
        z_v, u_a, s1_t = {}, {}, {}
        mm_done = []
        for o in range(64):
            if o >= 2:
                W("sync", "sact", gq_a[o - 2])
            d1 = dma("sync", f"tg{o % 2}", wstr[o % 2][:, :], d["tg_b"].ap()[o, :, :])
            d2 = dma("sync", f"tu{o % 2}", wstr2[o % 2][:, :], d["tu_b"].ap()[o, :, :])
            self.WD("scalar", d1)
            self.WD("scalar", d2)
            if o >= 2:
                W("scalar", "stn", mm_done[o - 2])
            for qx, (src, dst, wsc) in enumerate(
                    ((wstr[o % 2], wstrb[o % 2], self.wsg_i),
                     (wstr2[o % 2], wstr2b[o % 2], self.wsu_i))):
                av = E("scalar", lambda e, src=src, wsc=wsc, qx=qx: e.activation(
                    qt[:, qx * 2048:(qx + 1) * 2048], src[:, :], AF.Identity,
                    scale=wsc[:, :]), "sact")
                W("vector", "sact", av)
                E("vector", lambda e, qx=qx: e.tensor_scalar(
                    qt[:, qx * 2048:(qx + 1) * 2048], qt[:, qx * 2048:(qx + 1) * 2048],
                    1.0, -1.0, ALU.min, ALU.max))
                qv = E("vector", lambda e, dst=dst, qx=qx: e.tensor_scalar(
                    dst[:, :], qt[:, qx * 2048:(qx + 1) * 2048],
                    MAGIC, -MAGIC, ALU.add, ALU.add), "svec")
            gq.append(qv)
            gq_a.append(av)
            W("tensor", "svec", qv)
            for t4 in range(4):
                it = o * 4 + t4
                if it >= 2:
                    W("tensor", "svec", z_v[it - 2])
                    W("tensor", "sact", u_a[it - 2])
                for k in range(16):
                    E("tensor", lambda e, o=o, t4=t4, k=k: e.matmul(
                        ps[t4 % 2][:, :], wstrb[o % 2][:, k * 128:(k + 1) * 128],
                        self.hA(k)[:, t4 * TS:(t4 + 1) * TS],
                        start=(k == 0), stop=(k == 15)), "stn")
                for k in range(16):
                    gu = E("tensor", lambda e, o=o, t4=t4, k=k: e.matmul(
                        ps[2 + t4 % 2][:, :], wstr2b[o % 2][:, k * 128:(k + 1) * 128],
                        self.hA(k)[:, t4 * TS:(t4 + 1) * TS],
                        start=(k == 0), stop=(k == 15)), "stn")
                W("vector", "stn", gu)
                if it >= 2:
                    W("vector", "sact", z_a_prev[it - 2])
                E("vector", lambda e, t4=t4: e.tensor_tensor(
                    zt1[t4 % 2][:, :], ps[t4 % 2][:, :],
                    self.bc1[:, t4 * TS:(t4 + 1) * TS], ALU.mult), "svec")
                ztv = self.val("svec")
                W("scalar", "svec", ztv)
                if it >= 2:
                    W("scalar", "svec", z_v[it - 2])
                za = E("scalar", lambda e, t4=t4: e.activation(
                    zt2[t4 % 2][:, :], zt1[t4 % 2][:, :], AF.Silu), "sact")
                if it == 0:
                    z_a_prev = {}
                z_a_prev[it] = za
                W("vector", "sact", za)
                if it >= 2:
                    W("vector", "stn", s1_t[it - 2])
                z_v[it] = E("vector", lambda e, t4=t4: e.tensor_tensor(
                    zbf[t4 % 2][:, :], zt2[t4 % 2][:, :], ps[2 + t4 % 2][:, :],
                    ALU.mult), "svec")
                W("tensor", "svec", z_v[it])
                s1_t[it] = E("tensor", lambda e, t4=t4: e.matmul(
                    ps[4 + t4 % 2][:, :], self.sh128[:, :], zbf[t4 % 2][:, :],
                    start=True, stop=True), "stn")
                W("scalar", "stn", s1_t[it])
                if it >= 2:
                    self.WD("scalar", u2_dma[it - 2])
                u_a[it] = E("scalar", lambda e, t4=t4: e.activation(
                    u2ev[t4 % 2][:, :], ps[4 + t4 % 2][:, :], AF.Copy), "sact")
                W("gpsimd", "sact", u_a[it])
                u2_dma.append(dma("gpsimd", f"ub{t4 % 2}",
                                  d["u2_d"].ap()[:, o, t4 * TS:(t4 + 1) * TS],
                                  u2ev[t4 % 2][:, :]))
            mm_done.append(self.val("stn"))
        self.GU_TOT = [("ub0", self.val("ub0")), ("ub1", self.val("ub1"))]

    def p12_down(self, scope):
        srow_a, srow_b, srow_c, srow_d = (self.srow_a, self.srow_b,
                                          self.srow_c, self.srow_d)
        xstr = self.xstr
        E, W, dma = self.E, self.W, self.dma
        d, ps = self.dram, self.ps
        tds = [self.sb(f"tds{i}", [128, 8192], BF16, scope) for i in range(2)]
        outsb = [self.sb(f"outsb{i}", [128, TS], F32, scope) for i in range(2)]
        mxp = self.sb("mxp12", [128, TS], F32, scope)
        mnp = self.sb("mnp12", [128, TS], F32, scope)
        zq = self.sb("zq12", [128, TS], F32, scope)

        def v2A(g):
            return self.bigA1[:, g * 512:(g + 1) * 512]

        out_dmas = []
        prev_down_done = None
        for tok in self.GU_TOT:
            self.WD("gpsimd", tok)
        for q4 in range(4):
            HB = q4 * TS
            if prev_down_done is not None:
                W("gpsimd", "stn", prev_down_done)
            v2_in = []
            for g in range(64):
                v2_in.append(dma("gpsimd", "vin", v2A(g)[:, :],
                                 d["u2_d"].ap()[2 * g:2 * g + 2, :, HB:HB + TS]
                                 .rearrange("e b t -> (e b) t")))
            self.WD("tensor", v2_in[-1])
            mx_v = {}
            for g in range(64):
                bank = 2 + g % 2
                if g >= 2:
                    W("tensor", "svec", mx_v[g - 2])
                pa = E("tensor", lambda e, g=g, bank=bank: e.matmul(
                    ps[bank][:, :], self.si2h64[:, :], v2A(g)[:, :],
                    start=True, stop=True), "stn")
                W("vector", "stn", pa)
                if g == 0:
                    E("vector", lambda e, bank=bank: e.tensor_copy(
                        mxp[:, :], ps[bank][:, :]))
                    v = E("vector", lambda e, bank=bank: e.tensor_copy(
                        mnp[:, :], ps[bank][:, :]), "svec")
                else:
                    E("vector", lambda e, bank=bank: e.tensor_tensor(
                        mxp[:, :], mxp[:, :], ps[bank][:, :], ALU.max))
                    v = E("vector", lambda e, bank=bank: e.tensor_tensor(
                        mnp[:, :], mnp[:, :], ps[bank][:, :], ALU.min), "svec")
                mx_v[g] = v
            E("vector", lambda e: e.tensor_scalar(mnp[:, :], mnp[:, :], -1.0, None,
                                                  ALU.mult))
            mab = E("vector", lambda e: e.tensor_tensor(mxp[:, :], mxp[:, :], mnp[:, :],
                                                        ALU.max), "svec")
            self.xpart_max(mxp, TS, mab, srow_c)
            E("vector", lambda e, HB=HB: e.tensor_tensor(
                srow_c[0:1, 0:TS], srow_c[0:1, 0:TS],
                self.srow_su[0:1, HB:HB + TS], ALU.mult))
            E("vector", lambda e: e.tensor_scalar(
                srow_d[0:1, 0:TS], srow_c[0:1, 0:TS], C8192, EPS,
                ALU.mult, ALU.max))
            rcv = E("vector", lambda e: e.reciprocal(srow_a[0:1, 0:TS],
                                                     srow_d[0:1, 0:TS]), "svec")
            W("vector", "svec", rcv)
            E("vector", lambda e: e.tensor_scalar(
                srow_a[0:1, 0:TS], srow_a[0:1, 0:TS], 127.0 * C8192,
                None, ALU.mult))
            E("vector", lambda e, HB=HB: e.tensor_tensor(
                srow_a[0:1, 0:TS], srow_a[0:1, 0:TS],
                self.srow_su[0:1, HB:HB + TS], ALU.mult))
            s2v = E("vector", lambda e: e.tensor_scalar(
                srow_b[0:1, 0:TS], srow_d[0:1, 0:TS],
                self.wsrow[0:1, 10:11], 1.0 / 127.0, ALU.mult, ALU.mult), "svec")
            bcv = self.bcast_rows([(srow_a, self.bc2, TS),
                                   (srow_b, self.bc1, TS)], s2v)
            W("tensor", "svec", bcv)
            rq = {}
            for g in range(64):
                bank = 2 + g % 2
                if g >= 2:
                    W("tensor", "svec", rq[g - 2])
                pb = E("tensor", lambda e, g=g, bank=bank: e.matmul(
                    ps[bank][:, :], self.si2h64[:, :], v2A(g)[:, :],
                    start=True, stop=True), "stn")
                W("vector", "stn", pb)
                E("vector", lambda e, bank=bank: e.tensor_tensor(
                    zq[:, :], ps[bank][:, :], self.bc2[:, 0:TS], ALU.mult))
                rq[g] = E("vector", lambda e, g=g: e.tensor_scalar(
                    v2A(g)[:, :], zq[:, :], MAGIC, -MAGIC, ALU.add, ALU.add), "svec")
            W("tensor", "svec", rq[63])
            td_in, ep_a, ep_v, mmd = [], {}, {}, {}
            for o2 in range(16):
                if o2 >= 2:
                    W("sync", "stn", mmd[o2 - 2])
                if q4 == 0 and o2 == 0:
                    for tok in self.TDQ_TOT:
                        self.WD("sync", tok)
                td_in.append(dma("sync", f"ts{o2 % 2}", tds[o2 % 2][:, :],
                                 d["tdq_d"].ap()[o2, :, :]))
                if o2 >= 2:
                    W("sync", "svec", ep_v[o2 - 2])
                xrd = dma("sync", f"xs{o2 % 2}", xstr[o2 % 2][:, 0:TS],
                          d["xprime"].ap()[o2 * 128:(o2 + 1) * 128, HB:HB + TS])
                self.WD("tensor", td_in[o2])
                bank = 4 + o2 % 2
                if o2 >= 2:
                    W("tensor", "sact", ep_a[o2 - 2])
                for g in range(64):
                    dmm = E("tensor", lambda e, o2=o2, g=g, bank=bank: e.matmul(
                        ps[bank][:, :], tds[o2 % 2][:, g * 128:(g + 1) * 128],
                        v2A(g)[:, :], start=(g == 0), stop=(g == 63)), "stn")
                mmd[o2] = self.val("stn")
                W("scalar", "stn", dmm)
                if o2 >= 1:
                    W("scalar", "svec", ep_v[o2 - 1])
                da = E("scalar", lambda e, o2=o2, bank=bank: e.activation(
                    self.tmp2[:, 0:TS], ps[bank][:, :], AF.Identity,
                    scale=self.gbcols[:, o2:o2 + 1]), "sact")
                ep_a[o2] = da
                W("vector", "sact", da)
                self.WD("vector", xrd)
                if len(out_dmas) >= 2:
                    self.WD("vector", out_dmas[-2])
                E("vector", lambda e: e.tensor_tensor(
                    self.tmp2[:, 0:TS], self.tmp2[:, 0:TS], self.bc1[:, 0:TS],
                    ALU.mult))
                ep_v[o2] = E("vector", lambda e, o2=o2: e.tensor_tensor(
                    outsb[o2 % 2][:, :], self.tmp2[:, 0:TS],
                    xstr[o2 % 2][:, 0:TS], ALU.add), "svec")
                W("sync", "svec", ep_v[o2])
                out_dmas.append(dma("sync", f"ou{o2 % 2}",
                                    d["out_fm"].ap()[o2 * 128:(o2 + 1) * 128,
                                                     HB:HB + TS],
                                    outsb[o2 % 2][:, :]))
            prev_down_done = self.val("stn")
        self.W("sync", "ou0", self.val("ou0"))
        self.W("sync", "ou1", self.val("ou1"))

    def tap(self, key, tile, rows, cols):
        self.dbg_tiles[key] = (tile, rows, cols)

    def emit_taps(self):
        if not self.dbg:
            return
        self.barrier()
        off = 0
        for key, r, c in self.dbg:
            tile, tr, tc = self.dbg_tiles[key]
            assert r == tr and c == tc, (key, r, c, tr, tc)
            self.dma("gpsimd", "dbg", self.dram["dbg_d"].ap()[0:r, off:off + c],
                     tile[0:r, 0:c])
            off += c
        self.W("gpsimd", "dbg", self.val("dbg"))

    def build(self):
        self.declare()
        self.p0_consts()
        with contextlib.ExitStack() as sc:
            self.p1_mod_ws(sc)
            self.barrier()
        if self.upto >= 3:
            with contextlib.ExitStack() as sc:
                xbf = [self.sb(f"p3b{i}", [128, 2048], BF16, sc) for i in range(2)]
                xsq = [self.sb(f"p3q{i}", [128, 2048], BF16, sc) for i in range(2)]
                xln = self.sb("p3xln", [128, T], F32, sc)
                self.alloc_rows(sc)
                self.alloc_xstr(sc)
                self.ln_stats(self.dram["x_fm"], xbf, xsq)
                self.barrier()
                self.p4_h(self.dram["x_fm"], self.onePca, 0, self.hA, xln)
                self.barrier()
            self.tap("h0", self.hA(0), 128, 2048)
            self.tap("h15", self.hA(15), 128, 2048)
        if self.upto >= 5:
            with contextlib.ExitStack() as sc:
                self.p5_kv(sc)
                self.barrier()
                if self.upto >= 6:
                    self.p6_attention(sc)
                    self.barrier()
                for _ki in range(8):
                    self.tap(f"k{_ki}", self.kfm[_ki], 128, 2048)
                self.tap("v0", self.vaug[0], 128, 520)
        if self.upto >= 7:
            with contextlib.ExitStack() as sc:
                self.alloc_xstr(sc)
                self.p7_outproj(sc)
                self.barrier()
        if self.upto >= 8:
            with contextlib.ExitStack() as sc:
                xbf = [self.sb(f"p8b{i}", [128, 2048], BF16, sc) for i in range(2)]
                xsq = [self.sb(f"p8q{i}", [128, 2048], BF16, sc) for i in range(2)]
                self.alloc_rows(sc)
                self.alloc_xstr(sc)
                self.ln_stats(self.dram["xprime"], xbf, xsq)
                self.barrier()
        if self.upto >= 9:
            with contextlib.ExitStack() as sc:
                self.alloc_rows(sc)
                self.alloc_xstr(sc)
                self.p9_h2_fwht(sc)
                self.barrier()
        if self.upto >= 10:
            with contextlib.ExitStack() as sc:
                self.p10_tdq(sc)
                self.barrier()
        if self.upto >= 11:
            with contextlib.ExitStack() as sc:
                self.p11_gateup(sc)
                self.barrier()
        if self.upto >= 12:
            with contextlib.ExitStack() as sc:
                self.alloc_rows(sc, TS)
                self.alloc_xstr(sc)
                self.p12_down(sc)
        self.emit_taps()
        self.play()
        self.es.close()
        return self.nc



def _blk4(wT):
    K, M = wT.shape
    ko, mo = K // 128, M // 128
    out = np.ascontiguousarray(
        wT.reshape(ko, 128, mo, 128).transpose(2, 1, 0, 3))
    return np.ascontiguousarray(out.reshape(mo, 128, ko * 128))


def host_prep(inputs):
    x = inputs["L0_attn_x"]
    wq, wk, wv, wo = (inputs["L0_attn_wq"], inputs["L0_attn_wk"],
                      inputs["L0_attn_wv"], inputs["L0_attn_wo"])
    wg, wu, wd = (inputs["L0_ffn_w_gate"], inputs["L0_ffn_w_up"],
                  inputs["L0_ffn_w_down"])
    cc_in, wmod = inputs["L0_c"], inputs["L0_wmod"]

    sh = {}
    sh["wq_b"] = _blk4(np.ascontiguousarray(np.asarray(wq).T))
    wkT = np.ascontiguousarray(np.asarray(wk).T)
    wk_dup = np.concatenate([
        np.concatenate([wkT[:, h * 64:(h + 1) * 64]] * 2, axis=1)
        for h in range(8)], axis=1)
    sh["wk_b"] = _blk4(wk_dup)
    sh["wv_t"] = np.ascontiguousarray(np.asarray(wv).T)
    sh["wo_b"] = _blk4(np.ascontiguousarray(np.asarray(wo).T))
    qq = np.arange(128)
    kk16 = np.arange(16)
    frow = (qq[None, :] % 16) * 128 + 8 * kk16[:, None] + qq[None, :] // 16
    og = np.arange(64)
    cg = np.arange(128)
    row_idx = cg[None, None, None, :] * 64 + og[:, None, None, None]
    col_idx = frow.T[None, :, :, None]
    wg_a, wu_a, wd_a = np.asarray(wg), np.asarray(wu), np.asarray(wd)
    tg4 = wg_a[row_idx, col_idx]
    tu4 = wu_a[row_idx, col_idx]
    sh["tg_b"] = np.ascontiguousarray(tg4.reshape(64, 128, 2048).astype(np.float32))
    sh["tu_b"] = np.ascontiguousarray(tu4.reshape(64, 128, 2048).astype(np.float32))
    sh["td_b"] = _blk4(np.ascontiguousarray(wd_a.T))
    sh["ident_f"] = np.eye(128, dtype=np.float32)
    H128 = _hadamard(128)
    sh["h128_b"] = H128.astype(ml_dtypes.bfloat16)
    sh["i8h16_b"] = np.kron(np.eye(8), _hadamard(16)).astype(ml_dtypes.bfloat16)
    sh["i2h64_b"] = np.kron(np.eye(2), _hadamard(64)).astype(ml_dtypes.bfloat16)
    psw = np.zeros((128, 128), np.float32)
    for i in range(128):
        psw[i, i ^ 1] = 1.0
    sh["pswap_b"] = psw.astype(ml_dtypes.bfloat16)
    i_idx = (np.arange(128) % 64) // 2
    omega = 1.0 / 10000.0 ** (np.arange(0, 64, 2, dtype=np.float64) / 64.0)
    ang = omega[i_idx][:, None] * np.arange(T, dtype=np.float64)[None, :]
    cos_t = np.cos(ang)
    sin_t = np.sin(ang)
    sign = np.where(np.arange(128) % 2 == 0, -1.0, 1.0)[:, None]
    sh["cosq"] = (0.125 * cos_t).astype(np.float32)
    sh["sinq"] = (0.125 * sin_t * sign).astype(np.float32)
    sh["cosk"] = cos_t.astype(np.float32)
    sh["sink"] = (sin_t * sign).astype(np.float32)
    masks = np.zeros((4, 128, TS), np.float32)
    rr = np.arange(128)[:, None]
    ss = np.arange(TS)[None, :]
    for dd in range(4):
        masks[dd] = (ss >= rr + 128 * dd).astype(np.float32)
    sh["masks"] = np.ascontiguousarray(
        masks.transpose(1, 0, 2).reshape(128, 4 * TS)).astype(ml_dtypes.bfloat16)
    ccols = np.zeros((128, 8), np.float32)
    ccols[:, 0] = EXPB
    ccols[:, 1] = EPS
    ccols[:, 2] = 1.0
    ccols[:, 3] = 0.25
    sh["ccols"] = ccols
    sh["ones1f"] = np.ones((1, 128), np.float32)
    sh["onescolb"] = np.ones((128, 1), ml_dtypes.bfloat16)
    c_np = np.asarray(cc_in)
    c_t = np.zeros((128, 128), np.float32)
    for ch in range(16):
        for b in range(8):
            c_t[:, ch * 8 + b] = c_np[b, ch * 128:(ch + 1) * 128]
    sh["c_t"] = c_t

    wmod_np = np.asarray(wmod)
    x_np = np.asarray(x)
    per_core = []
    for i in range(NCORES):
        dd = dict(sh)
        dd["x_fm"] = np.ascontiguousarray(x_np[i].T.astype(np.float32))
        oh = np.zeros((8, 1), np.float32)
        oh[i, 0] = 1.0
        dd["onehot"] = oh
        dd["wmod_sl"] = np.ascontiguousarray(wmod_np[:, i * 1536:(i + 1) * 1536])
        dd["wgs"] = np.ascontiguousarray(sh["tg_b"][8 * i:8 * i + 8])
        dd["wus"] = np.ascontiguousarray(sh["tu_b"][8 * i:8 * i + 8])
        dd["wds"] = np.ascontiguousarray(
            sh["td_b"][2 * i:2 * i + 2].reshape(2, 128, 4, 2048)
            .transpose(0, 2, 1, 3).reshape(8, 128, 2048))
        per_core.append(dd)
    return per_core


def kernel(**inputs):
    in_maps = host_prep(inputs)
    nc = KB().build()
    res = run_bass_kernel_spmd(nc, in_maps, core_ids=list(range(NCORES)))
    out = np.stack([np.asarray(res.results[i]["out_fm"]).T for i in range(NCORES)])
    return np.ascontiguousarray(out).astype(np.float32)


# version 03
# speedup vs baseline: 2.2214x; 1.2102x over previous
import contextlib
import numpy as np
import ml_dtypes
import concourse.bass as bass
import concourse.mybir as mybir
from concourse.bass_utils import run_bass_kernel_spmd

F32 = mybir.dt.float32
BF16 = mybir.dt.bfloat16
AF = mybir.ActivationFunctionType
ALU = mybir.AluOpType
AX = mybir.AxisListType

NCORES = 8
T = 2048
D = 2048
NK = 16
NTS = 4
TS = 512
NPAIR = 16
MAGIC = 12582912.0
EXPB = -18.0
EPS = 1e-5
C2048 = float(1.0 / np.sqrt(np.float64(2048.0)))
C8192 = float(1.0 / np.sqrt(np.float64(8192.0)))
NELW = 8192.0 * 2048.0

ENGS = ("tensor", "vector", "scalar", "gpsimd", "sync")
NO_CC = False
DBG_WS = False


def _hadamard(n):
    h = np.array([[1.0]])
    while h.shape[0] < n:
        h = np.block([[h, h], [h, -h]])
    return h


class KB:

    def __init__(self, upto=99, dbg=()):
        self.upto = upto
        self.dbg = list(dbg)
        nc = self.nc = bass.Bass("TRN2", target_bir_lowering=False,
                                 num_devices=NCORES)
        self.q = {e: [] for e in ENGS}
        self.es = contextlib.ExitStack()
        self.sems = {}
        for s in ("cc", "cc2", "stn", "svec", "sact", "sgps"):
            self.sems[s] = [self.es.enter_context(nc.semaphore(s)), 0]
        self.dbg_tiles = {}

    FENCE = True
    _ENG_SEM = {"vector": "svec", "scalar": "sact", "gpsimd": "sgps"}

    def E(self, eng, build, sem=None, k=1, nofence=False):
        fence = self.FENCE and k == 1 and eng in self._ENG_SEM and not nofence
        if fence and sem is None:
            sem = self._ENG_SEM[eng]
        v = None
        if sem is not None:
            s = self.sem_of(sem)
            s[1] += k
            v = s[1]
            h = s[0]

            def go(e, build=build, h=h, k=k):
                build(e).then_inc(h, k)
        else:
            def go(e, build=build):
                build(e)
        self.q[eng].append(go)
        if fence:
            self.W(eng, sem, v)
        return v

    def W(self, eng, sem, v):
        if v is None or v <= 0:
            return
        h = self.sem_of(sem)[0]
        self.q[eng].append(lambda e, h=h, v=v: e.wait_ge(h, v))

    def sem_of(self, name):
        if name not in self.sems:
            self.sems[name] = [self.es.enter_context(self.nc.semaphore(name)), 0]
        return self.sems[name]

    def dma(self, eng, stream, dst, src, **kw):
        self.sem_of(stream)
        v = self.E(eng, lambda e, d=dst, s=src, kw=kw: e.dma_start(d, s, **kw),
                   stream, 16)
        return (stream, v)

    def WD(self, eng, tok):
        if tok is None:
            return
        self.W(eng, tok[0], tok[1])

    def val(self, sem):
        return self.sems[sem][1]

    def barrier(self):
        vals = [(s, sv[1]) for s, sv in list(self.sems.items())]
        for eng in ENGS:
            for s, v in vals:
                self.W(eng, s, v)

    _uid = 0

    def sb(self, name, shape, dtype, scope=None):
        KB._uid += 1
        t = (scope or self.es).enter_context(
            self.nc.sbuf_tensor(f"{name}_u{KB._uid}", shape, dtype))
        return t

    def play(self):
        q = self.q
        with self.nc.Block() as blk:
            @blk.tensor
            def _(e):
                for f in q["tensor"]:
                    f(e)

            @blk.vector
            def _(e):
                for f in q["vector"]:
                    f(e)

            @blk.scalar
            def _(e):
                for f in q["scalar"]:
                    f(e)

            @blk.gpsimd
            def _(e):
                for f in q["gpsimd"]:
                    f(e)

            @blk.sync
            def _(e):
                for f in q["sync"]:
                    f(e)

    def declare(self):
        nc = self.nc
        dram = self.dram = {}
        for name, shape in [
                ("x_fm", [D, T]), ("wq_b", [16, 128, 2048]), ("wk_b", [8, 128, 2048]),
                ("wv_t", [2048, 512]), ("wo_b", [16, 128, 2048]),
                ("tg_b", [64, 128, 2048]), ("tu_b", [64, 128, 2048]),
                ("td_b", [16, 128, 8192]), ("wmod_sl", [2048, 1536]),
                ("c_t", [128, 128]), ("onehot", [8, 1]),
                ("wgs", [8, 128, 2048]), ("wus", [8, 128, 2048]), ("wds", [8, 128, 2048]),
                ("ident_f", [128, 128]), ("cosq", [128, T]), ("sinq", [128, T]),
                ("cosk", [128, T]), ("sink", [128, T]), ("ccols", [128, 8]),
                ("ones1f", [1, 128])]:
            dram[name] = nc.dram_tensor(name, shape, F32, kind="ExternalInput")
        for name, shape in [
                ("h128_b", [128, 128]), ("i8h16_b", [128, 128]), ("i2h64_b", [128, 128]),
                ("pswap_b", [128, 128]), ("masks", [128, 4 * TS]),
                ("onescolb", [128, 1])]:
            dram[name] = nc.dram_tensor(name, shape, BF16, kind="ExternalInput")
        dram["out_fm"] = nc.dram_tensor("out_fm", [D, T], F32, kind="ExternalOutput")
        if self.dbg:
            dram["dbg_d"] = nc.dram_tensor("dbg_d", [128, 65536], F32,
                                           kind="ExternalOutput")
        dram["xprime"] = nc.dram_tensor("xprime", [D, T], F32)
        dram["u1_d"] = nc.dram_tensor("u1_d", [128, 16, T], BF16)
        dram["u2_d"] = nc.dram_tensor("u2_d", [128, 64, T], BF16)
        dram["tdq_d"] = nc.dram_tensor("tdq_d", [16, 128, 8192], BF16)
        dram["opair_d"] = nc.dram_tensor("opair_d", [16, 128, T], BF16)
        dram["ag_in"] = nc.dram_tensor("ag_in", [8, 1536], F32)
        dram["ag_out"] = nc.dram_tensor("ag_out", [64, 1536], F32)
        dram["ar_in"] = nc.dram_tensor("ar_in", [1, 4], F32)
        dram["ar_out"] = nc.dram_tensor("ar_out", [1, 4], F32)

        sb = self.sb
        self.bigA1 = sb("bigA1", [128, 32768], BF16)
        self.sid_f = sb("sid_f", [128, 128], F32)
        self.sh128 = sb("sh128", [128, 128], BF16)
        self.si8h16 = sb("si8h16", [128, 128], BF16)
        self.si2h64 = sb("si2h64", [128, 128], BF16)
        self.spswap = sb("spswap", [128, 128], BF16)
        self.sccols = sb("sccols", [128, 8], F32)
        self.sones1 = sb("sones1", [1, 128], F32)
        self.sonescol = sb("sonescol", [128, 1], BF16)
        self.sonehot = sb("sonehot", [8, 1], F32)
        self.mod96 = sb("mod96", [128, 96], F32)
        self.onePca = sb("onePca", [128, 16], F32)
        self.onePcb = sb("onePcb", [128, 16], F32)
        self.gacols = sb("gacols", [128, 16], F32)
        self.gbcols = sb("gbcols", [128, 16], F32)
        self.wsrow = sb("wsrow", [1, 16], F32)
        self.wsg_i = sb("wsg_i", [128, 1], F32)
        self.wsu_i = sb("wsu_i", [128, 1], F32)
        self.wsd_i = sb("wsd_i", [128, 1], F32)
        self.srow_su = sb("srow_su", [1, T], F32)
        self.bc1 = sb("bc1", [128, T], F32)
        self.bc2 = sb("bc2", [128, T], F32)
        self.tmp1 = sb("tmp1", [128, TS], F32)
        self.tmp2 = sb("tmp2", [128, TS], F32)
        self.mcol = sb("mcol", [128, 16], F32)

        self.ps = [self.es.enter_context(self.nc.psum_tensor(f"ps{i}", [128, TS], F32))
                   for i in range(6)]
        self.ps_row = self.es.enter_context(self.nc.psum_tensor("ps_row", [1, TS], F32))
        self.ps_row2 = self.es.enter_context(self.nc.psum_tensor("ps_row2", [1, TS], F32))

    def hA(self, i):
        return self.bigA1[:, i * 2048:(i + 1) * 2048]

    def alloc_xstr(self, scope):
        self.xstr = [self.sb(f"xstr{i}", [128, T], F32, scope) for i in range(2)]

    def alloc_rows(self, scope, width=T):
        self.srow_a = self.sb("srow_a", [1, width], F32, scope)
        self.srow_b = self.sb("srow_b", [1, width], F32, scope)
        self.srow_c = self.sb("srow_c", [1, width], F32, scope)
        self.srow_d = self.sb("srow_d", [1, width], F32, scope)

    def p0_consts(self):
        d = self.dram
        pairs = [(self.sid_f, d["ident_f"]), (self.sh128, d["h128_b"]),
                 (self.si8h16, d["i8h16_b"]), (self.si2h64, d["i2h64_b"]),
                 (self.spswap, d["pswap_b"]), (self.sccols, d["ccols"]),
                 (self.sonehot, d["onehot"]), (self.sonescol, d["onescolb"]),
                 (self.sones1, d["ones1f"])]
        for dst, src in pairs:
            tok = self.dma("sync", "cst", dst[:, :], src.ap())
        self.CONSTS_D = tok

    def p1_mod_ws(self, scope):
        E, W, dma = self.E, self.W, self.dma
        d, ps = self.dram, self.ps
        siluc = self.sb("siluc", [128, 128], F32, scope)
        siluc_b = self.sb("siluc_b", [128, 128], BF16, scope)
        modsl = self.sb("modsl", [8, 1536], F32, scope)
        exsl = [self.sb(f"exsl{i}", [8, TS], F32, scope) for i in range(2)]
        rowbuf = [self.sb(f"rowbuf{i}", [1, TS], F32, scope) for i in range(2)]
        wsacc = self.sb("wsacc", [128, 4], F32, scope)
        wsred = self.sb("wsred", [128, 1], F32, scope)
        if DBG_WS:
            wsdbg = self.sb("wsdbg", [128, 24], F32, scope)
            self.tap("wsdbg", wsdbg, 128, 24)
        gtmp = self.sb("gtmp", [128, 64], F32, scope)
        wstr = [self.sb(f"p1w{i}", [128, 2048], F32, scope) for i in range(2)]
        wmf = [self.sb(f"p1m{i}", [128, 1536], F32, scope) for i in range(2)]
        wmb = [self.sb(f"p1mb{i}", [128, 1536], BF16, scope) for i in range(2)]

        cdone = self.dma("sync", "cst", siluc[:, :], d["c_t"].ap())
        self.WD("scalar", cdone)
        sa = E("scalar", lambda e: e.activation(siluc[:, :], siluc[:, :], AF.Silu),
               "sact")
        W("vector", "sact", sa)
        sg = E("vector", lambda e: e.tensor_copy(siluc_b[:, :], siluc[:, :]), "svec")

        slab_d, slab_v = [], []
        for t24 in range(24):
            m, i = divmod(t24, 8)
            src = (d["wgs"], d["wus"], d["wds"])[m]
            if t24 >= 2:
                W("sync", "svec", slab_v[t24 - 2])
            slab_d.append(dma("sync", f"sw{t24 % 2}", wstr[t24 % 2][:, :],
                              src.ap()[i, :, :]))
            self.WD("vector", slab_d[t24])
            rdv = E("vector", lambda e, b=t24 % 2: e.tensor_reduce(
                wsred[:, :], wstr[b][:, :], AX.X, ALU.add,
                apply_absolute_value=True), "svec")
            W("vector", "svec", rdv)
            if DBG_WS:
                E("vector", lambda e, t24=t24: e.tensor_copy(
                    wsdbg[:, t24:t24 + 1], wsred[:, :]))
            if i == 0:
                slab_v.append(E("vector", lambda e, m=m: e.tensor_copy(
                    wsacc[:, m:m + 1], wsred[:, :]), "svec"))
            else:
                slab_v.append(E("vector", lambda e, m=m: e.tensor_tensor(
                    wsacc[:, m:m + 1], wsacc[:, m:m + 1], wsred[:, :], ALU.add), "svec"))
        self.WD("tensor", self.CONSTS_D)
        W("tensor", "svec", slab_v[-1])
        wst = E("tensor", lambda e: e.matmul(self.ps_row[0:1, 0:4], self.sccols[:, 2:3],
                                             wsacc[:, :], start=True, stop=True), "stn")
        W("vector", "stn", wst)
        wsv = E("vector", lambda e: e.tensor_copy(self.wsrow[0:1, 0:4],
                                                  self.ps_row[0:1, 0:4]), "svec")

        W("tensor", "svec", sg)
        wm_d, wm_v, wm_t = [], [], []
        for k in range(16):
            if k >= 2:
                W("sync", "svec", wm_v[k - 2])
            wm_d.append(dma("sync", f"wm{k % 2}", wmf[k % 2][:, :],
                            d["wmod_sl"].ap()[k * 128:(k + 1) * 128, :]))
            self.WD("vector", wm_d[k])
            if k >= 2:
                W("vector", "stn", wm_t[k - 2])
            wm_v.append(E("vector", lambda e, b=k % 2: e.tensor_copy(
                wmb[b][:, :], wmf[b][:, :]), "svec"))
            W("tensor", "svec", wm_v[k])
            for n3 in range(3):
                v = E("tensor", lambda e, k=k, n3=n3: e.matmul(
                    ps[n3][0:8, :], siluc_b[:, k * 8:(k + 1) * 8],
                    wmb[k % 2][:, n3 * 512:(n3 + 1) * 512],
                    start=(k == 0), stop=(k == 15)), "stn")
            wm_t.append(v)
        W("vector", "stn", wm_t[-1])
        for n3 in range(3):
            mev = E("vector", lambda e, n3=n3: e.tensor_copy(
                modsl[0:8, n3 * 512:(n3 + 1) * 512], ps[n3][0:8, :]), "svec")

        W("gpsimd", "svec", mev)
        W("gpsimd", "svec", wsv)
        dma("gpsimd", "ccb", d["ag_in"].ap(), modsl[0:8, :])
        dma("gpsimd", "ccb", d["ar_in"].ap()[0:1, :], self.wsrow[0:1, 0:4])
        W("gpsimd", "ccb", self.val("ccb"))
        if NO_CC:
            for rr in range(8):
                self.dma("gpsimd", "ccf", d["ag_out"].ap()[rr * 8:rr * 8 + 8, :],
                         d["ag_in"].ap())
            self.dma("gpsimd", "ccf", d["ar_out"].ap(), d["ar_in"].ap())
        else:
            cch, cc2h = self.sems["cc"][0], self.sems["cc2"][0]
            E("gpsimd", lambda e: e.collective_compute(
                "AllGather", ALU.bypass, replica_groups=[list(range(NCORES))],
                ins=[d["ag_in"].ap().opt()],
                outs=[d["ag_out"].ap().opt()]).then_inc(cch), nofence=True)
            E("gpsimd", lambda e: e.collective_compute(
                "AllReduce", ALU.add, replica_groups=[list(range(NCORES))],
                ins=[d["ar_in"].ap().opt()],
                outs=[d["ar_out"].ap().opt()]).then_inc(cc2h), nofence=True)
            self.sems["cc"][1] = 1
            self.sems["cc2"][1] = 1
            W("gpsimd", "cc", 1)
            W("gpsimd", "cc2", 1)
        if NO_CC:
            W("gpsimd", "ccf", self.val("ccf"))
        ccd = dma("gpsimd", "ccb", self.wsrow[0:1, 4:8], d["ar_out"].ap()[0:1, :])

        self.WD("sync", ccd)
        ex_d, ex_t, ex_v, tp_t = [], [], [], []
        for n in range(24):
            if n >= 2:
                W("sync", "stn", ex_t[n - 2])
            r0, js = n // 3, (n % 3) * 512
            ex_d.append(dma("sync", f"ex{n % 2}", exsl[n % 2][0:8, :],
                            d["ag_out"].ap()[r0 * 8:r0 * 8 + 8, js:js + 512]))
            self.WD("tensor", ex_d[n])
            if n >= 2:
                W("tensor", "svec", ex_v[n - 2])
            ex_t.append(E("tensor", lambda e, n=n: e.matmul(
                ps[n % 2][0:1, :], self.sonehot[0:8, 0:1],
                exsl[n % 2][0:8, :], start=True, stop=True), "stn"))
            W("vector", "stn", ex_t[n])
            if n >= 2:
                W("vector", "stn", tp_t[n - 2])
            ex_v.append(E("vector", lambda e, n=n: e.tensor_copy(
                rowbuf[n % 2][0:1, :], ps[n % 2][0:1, :]), "svec"))
            W("tensor", "svec", ex_v[n])
            for j4 in range(4):
                tp = E("tensor", lambda e, n=n, j4=j4: e.transpose(
                    ps[5][:, n * 4 + j4:n * 4 + j4 + 1],
                    rowbuf[n % 2][0:1, j4 * 128:(j4 + 1) * 128],
                    self.sones1[0:1, 0:1]), "stn")
            tp_t.append(tp)
        W("vector", "stn", tp_t[-1])
        E("vector", lambda e: e.tensor_copy(self.mod96[:, :], ps[5][:, 0:96]))
        E("vector", lambda e: e.tensor_scalar(self.onePca[:, :], self.mod96[:, 16:32],
                                              1.0, None, ALU.add))
        E("vector", lambda e: e.tensor_scalar(self.onePcb[:, :], self.mod96[:, 64:80],
                                              1.0, None, ALU.add))
        for dst, off, lo in ((self.gacols, 32, 0), (self.gbcols, 80, 16)):
            E("vector", lambda e, off=off, lo=lo: e.tensor_scalar(
                gtmp[:, lo:lo + 16], self.mod96[:, off:off + 16], 0.0, None, ALU.is_ge))
            E("vector", lambda e, lo=lo: e.tensor_scalar(
                gtmp[:, lo:lo + 16], gtmp[:, lo:lo + 16], 2.0, -1.0, ALU.mult, ALU.add))
            E("vector", lambda e, off=off, lo=lo: e.tensor_scalar(
                gtmp[:, 32 + lo:48 + lo], self.mod96[:, off:off + 16], -1.0, None,
                ALU.mult))
            E("vector", lambda e, off=off, lo=lo: e.tensor_tensor(
                gtmp[:, 32 + lo:48 + lo], gtmp[:, 32 + lo:48 + lo],
                self.mod96[:, off:off + 16], ALU.max))
            E("vector", lambda e, lo=lo: e.tensor_scalar(
                gtmp[:, 32 + lo:48 + lo], gtmp[:, 32 + lo:48 + lo], 0.25, None, ALU.max))
            E("vector", lambda e, dst=dst, lo=lo: e.tensor_tensor(
                dst[:, :], gtmp[:, 32 + lo:48 + lo], gtmp[:, lo:lo + 16], ALU.mult))
        E("vector", lambda e: e.tensor_scalar(self.wsrow[0:1, 8:11], self.wsrow[0:1, 4:7],
                                              1.0 / NELW, EPS, ALU.mult, ALU.max))
        mv = E("vector", lambda e: e.reciprocal(self.wsrow[0:1, 12:15],
                                                self.wsrow[0:1, 8:11]), "svec")
        W("vector", "svec", mv)
        W("tensor", "svec", mv)
        for i in range(3):
            wb = E("tensor", lambda e, i=i: e.matmul(
                ps[4][:, i:i + 1], self.sones1[0:1, :], self.wsrow[0:1, 12 + i:13 + i],
                start=True, stop=True), "stn")
        W("vector", "stn", wb)
        E("vector", lambda e: e.tensor_copy(self.wsg_i[:, :], ps[4][:, 0:1]))
        E("vector", lambda e: e.tensor_copy(self.wsu_i[:, :], ps[4][:, 1:2]))
        E("vector", lambda e: e.tensor_copy(self.wsd_i[:, :], ps[4][:, 2:3]), "svec")
        self.tap("mod96", self.mod96, 128, 96)
        self.tap("wsrow", self.wsrow, 1, 16)
        self.tap("wsacc", wsacc, 128, 4)

    def ln_stats(self, src_dram, xbf, xsq):
        srow_a, srow_b, srow_c, srow_d = (self.srow_a, self.srow_b,
                                          self.srow_c, self.srow_d)
        xstr = self.xstr
        E, W, dma = self.E, self.W, self.dma
        ps = self.ps
        sq_ps = [ps[4][0:1, :], ps[5][0:1, :], self.ps_row[0:1, :], self.ps_row2[0:1, :]]
        cast_v, sq_a, mm_t = [], [], []
        for c in range(NK):
            if c >= 2:
                W("sync", "svec", cast_v[c - 2])
            xd = dma("sync", f"xs{c % 2}", xstr[c % 2][:, :],
                     src_dram.ap()[c * 128:(c + 1) * 128, :])
            self.WD("vector", xd)
            if c >= 2:
                W("vector", "stn", mm_t[c - 2])
            cast_v.append(E("vector", lambda e, b=c % 2: e.tensor_copy(
                xbf[b][:, :], xstr[b][:, :]), "svec"))
            W("scalar", "svec", cast_v[c])
            if c >= 2:
                W("scalar", "stn", mm_t[c - 2])
            sq_a.append(E("scalar", lambda e, b=c % 2: e.activation(
                xsq[b][:, :], xbf[b][:, :], AF.Square), "sact"))
            W("tensor", "sact", sq_a[c])
            for t4 in range(NTS):
                E("tensor", lambda e, c=c, t4=t4: e.matmul(
                    ps[t4][0:1, :], self.sonescol[:, :],
                    xbf[c % 2][:, t4 * TS:(t4 + 1) * TS],
                    start=(c == 0), stop=(c == NK - 1)))
                mm = E("tensor", lambda e, c=c, t4=t4, tgt=sq_ps[t4]: e.matmul(
                    tgt, self.sonescol[:, :], xsq[c % 2][:, t4 * TS:(t4 + 1) * TS],
                    start=(c == 0), stop=(c == NK - 1)), "stn")
            mm_t.append(mm)
        W("vector", "stn", mm_t[-1])
        for t4 in range(NTS):
            E("vector", lambda e, t4=t4: e.tensor_copy(
                srow_a[0:1, t4 * TS:(t4 + 1) * TS], ps[t4][0:1, :]))
            E("vector", lambda e, t4=t4, s=sq_ps[t4]: e.tensor_copy(
                srow_b[0:1, t4 * TS:(t4 + 1) * TS], s))
        E("vector", lambda e: e.tensor_scalar(srow_a[0:1, :], srow_a[0:1, :],
                                              1.0 / D, None, ALU.mult))
        E("vector", lambda e: e.tensor_tensor(srow_c[0:1, :], srow_a[0:1, :],
                                              srow_a[0:1, :], ALU.mult))
        rv = E("vector", lambda e: e.scalar_tensor_tensor(
            srow_b[0:1, :], srow_b[0:1, :], 1.0 / D, srow_c[0:1, :],
            ALU.mult, ALU.subtract), "svec")
        W("scalar", "svec", rv)
        ra = E("scalar", lambda e: e.activation(
            srow_d[0:1, :], srow_b[0:1, :], AF.Sqrt,
            bias=self.sccols[0:1, 1:2]), "sact")
        W("vector", "sact", ra)
        rv2 = E("vector", lambda e: e.reciprocal(srow_b[0:1, :],
                                                 srow_d[0:1, :]), "svec")
        W("vector", "svec", rv2)
        self.bcast_rows([(srow_a, self.bc1, T), (srow_b, self.bc2, T)], rv2)

    def bcast_rows(self, jobs, ready_v):
        E, W = self.E, self.W
        ps = self.ps
        W("tensor", "svec", ready_v)
        i = 0
        last_ev = None
        for row, dstt, width in jobs:
            for t4 in range(width // TS):
                bank = i % 4
                if last_ev is not None and i >= 4:
                    W("tensor", "svec", evs[i - 4])
                mm = E("tensor", lambda e, row=row, t4=t4, bank=bank: e.matmul(
                    ps[bank][:, :], self.sones1[0:1, :],
                    row[0:1, t4 * TS:(t4 + 1) * TS], start=True, stop=True), "stn")
                W("vector", "stn", mm)
                ev = E("vector", lambda e, dstt=dstt, t4=t4, bank=bank: e.tensor_copy(
                    dstt[:, t4 * TS:(t4 + 1) * TS], ps[bank][:, :]), "svec")
                if i == 0:
                    evs = {}
                evs[i] = ev
                last_ev = ev
                i += 1
        return last_ev

    def p4_h(self, src_dram, scale_tile, bias_off, dst_fn, xln):
        xstr = self.xstr
        E, W, dma = self.E, self.W, self.dma
        sub_v, act_a = [], []
        for c in range(NK):
            if c >= 2:
                W("sync", "svec", sub_v[c - 2])
            xd = dma("sync", f"xs{c % 2}", xstr[c % 2][:, :],
                     src_dram.ap()[c * 128:(c + 1) * 128, :])
            self.WD("vector", xd)
            if c >= 1:
                W("vector", "sact", act_a[c - 1])
            E("vector", lambda e, b=c % 2: e.tensor_tensor(
                xln[:, :], xstr[b][:, :], self.bc1[:, :], ALU.subtract))
            sub_v.append(E("vector", lambda e: e.tensor_tensor(
                xln[:, :], xln[:, :], self.bc2[:, :], ALU.mult), "svec"))
            W("scalar", "svec", sub_v[c])
            act_a.append(E("scalar", lambda e, c=c, st=scale_tile, bo=bias_off:
                           e.activation(dst_fn(c)[:, :], xln[:, :], AF.Identity,
                                        scale=st[:, c:c + 1],
                                        bias=self.mod96[:, bo + c:bo + c + 1]), "sact"))
        return act_a[-1]

    def p5_kv(self, scope):
        E, W, dma = self.E, self.W, self.dma
        d, ps = self.dram, self.ps
        self.kfm = [self.sb(f"kfm{i}", [128, T], BF16, scope) for i in range(8)]
        self.vaug = [self.sb(f"vaug{i}", [128, 520], BF16, scope) for i in range(16)]
        self.qpair = self.sb("qpair", [128, T], BF16, scope)
        self.qraw = self.sb("qraw", [128, TS], BF16, scope)
        wv_all = self.sb("wv_all", [128, 8192], BF16, scope)
        sc5 = contextlib.ExitStack()
        rope_ck = self.bc1
        rope_sk = self.bc2
        wstr = [self.sb("p5w0", [128, 2048], F32, sc5)]
        wstr.append(wstr[0])
        wstrb = [self.sb("p5wb0", [128, 2048], BF16, sc5)]
        wstrb.append(wstrb[0])

        ckd = dma("sync", "cst", rope_ck[:, :], d["cosk"].ap())
        skd = dma("sync", "cst", rope_sk[:, :], d["sink"].ap())

        wk_d, wk_g, mm_last = [], [], []
        rope_done = []
        for o in range(8):
            if o >= 1:
                W("sync", "sact", wk_g[o - 1])
            wk_d.append(dma("sync", "wk", wstr[o % 2][:, :], d["wk_b"].ap()[o, :, :]))
            self.WD("scalar", wk_d[o])
            if o >= 1:
                W("scalar", "stn", mm_last[o - 1])
            wk_g.append(E("scalar", lambda e, b=o % 2: e.copy(
                wstrb[b][:, :], wstr[b][:, :]), "sact"))
            W("tensor", "sact", wk_g[o])
            if o == 0:
                self.WD("tensor", skd)
                self.WD("vector", skd)
                swv_prev = None
            for t4 in range(4):
                if t4 >= 2 or o > 0:
                    W("tensor", "svec", rope_done[-2] if len(rope_done) >= 2 else None)
                for k in range(16):
                    mmv = E("tensor", lambda e, o=o, t4=t4, k=k: e.matmul(
                        ps[t4 % 2][:, :], wstrb[o % 2][:, k * 128:(k + 1) * 128],
                        self.hA(k)[:, t4 * TS:(t4 + 1) * TS],
                        start=(k == 0), stop=(k == 15)), "stn")
                W("scalar", "stn", mmv)
                if swv_prev is not None:
                    W("scalar", "stn", swv_prev)
                arv = E("scalar", lambda e, t4=t4: e.activation(
                    self.qraw[:, :], ps[t4 % 2][:, :], AF.Copy), "sact")
                W("tensor", "sact", arv)
                swv_prev = E("tensor", lambda e, t4=t4: e.matmul(
                    ps[2 + t4 % 2][:, :], self.spswap[:, :], self.qraw[:, :],
                    start=True, stop=True), "stn")
                W("vector", "stn", swv_prev)
                E("vector", lambda e, t4=t4: e.tensor_tensor(
                    self.tmp1[:, 0:TS], ps[t4 % 2][:, :],
                    rope_ck[:, t4 * TS:(t4 + 1) * TS], ALU.mult))
                E("vector", lambda e, t4=t4: e.tensor_tensor(
                    self.tmp2[:, 0:TS], ps[2 + t4 % 2][:, :],
                    rope_sk[:, t4 * TS:(t4 + 1) * TS], ALU.mult))
                rope_done.append(E("vector", lambda e, o=o, t4=t4: e.tensor_tensor(
                    self.kfm[o][:, t4 * TS:(t4 + 1) * TS], self.tmp1[:, 0:TS],
                    self.tmp2[:, 0:TS], ALU.add), "svec"))
            mm_last.append(self.val("stn"))
        self.K_DONE = rope_done[-1]

        for i in range(16):
            for h8 in range(8):
                E("gpsimd", lambda e, i=i, h8=h8: e.memset(
                    self.vaug[i][:, h8 * 65 + 64:h8 * 65 + 65], 1.0))
        vones = E("gpsimd", lambda e, i=15: e.tensor_copy(
            self.vaug[15][:, 519:520], self.vaug[15][:, 519:520]), "sgps")
        W("sync", "sact", wk_g[-1])
        for qv4 in range(4):
            if qv4 >= 1:
                W("sync", "sact", wvg)
            wvd = dma("sync", "wv", wstr[0].ap().rearrange("p (a v) -> p a v", a=4),
                      d["wv_t"].ap()[qv4 * 512:(qv4 + 1) * 512, :]
                      .rearrange("(a p) v -> p a v", p=128))
            self.WD("scalar", wvd)
            wvg = E("scalar", lambda e, qv4=qv4: e.copy(
                wv_all[:, qv4 * 2048:(qv4 + 1) * 2048], wstr[0][:, :]), "sact")
        W("tensor", "sact", wvg)
        v_ev = []
        for i in range(16):
            bank = 4 + i % 2
            if i >= 2:
                W("tensor", "svec", v_ev[i - 2])
            for k in range(16):
                vmm = E("tensor", lambda e, i=i, k=k, bank=bank: e.matmul(
                    ps[bank][:, :], self.hA(k)[:, i * 128:(i + 1) * 128],
                    wv_all[:, k * 512:(k + 1) * 512],
                    start=(k == 0), stop=(k == 15)), "stn")
            W("vector", "stn", vmm)
            if i == 0:
                W("vector", "sgps", vones)
            for h8 in range(8):
                vv = E("vector", lambda e, i=i, h8=h8, bank=bank: e.tensor_copy(
                    self.vaug[i][:, h8 * 65:h8 * 65 + 64],
                    ps[bank][:, h8 * 64:(h8 + 1) * 64]), "svec")
            v_ev.append(vv)
        self.V_DONE = v_ev[-1]
        self.barrier()
        sc5.close()

    def p6_attention(self, scope):
        E, W, dma = self.E, self.W, self.dma
        d, ps = self.dram, self.ps
        ptile = [self.sb(f"ptile{i}", [128, TS], BF16, scope) for i in range(4)]
        opevA = self.sb("opevA", [64, T], BF16, scope)
        opevB = self.sb("opevB", [64, T], BF16, scope)
        recrow = self.sb("recrow", [1, TS], F32, scope)
        recrow2 = self.sb("recrow2", [1, TS], F32, scope)
        bcsbA = self.sb("bcsbA", [64, TS], F32, scope)
        bcsbB = self.sb("bcsbB", [64, TS], F32, scope)
        rope_c = self.bc1
        rope_s = self.bc2
        self.smasks = self.sb("smasks", [128, 4 * TS], BF16, scope)
        wstr = [self.sb("p6w0", [128, 2048], F32, scope)]
        wstr.append(wstr[0])
        wstrb = [self.sb("p6wb0", [128, 2048], BF16, scope)]
        wstrb.append(wstrb[0])

        W("sync", "svec", self.K_DONE)
        dma("sync", "cst", self.smasks[:, :], d["masks"].ap())
        cqd = dma("sync", "cst", rope_c[:, :], d["cosq"].ap())
        sqd = dma("sync", "cst", rope_s[:, :], d["sinq"].ap())
        self.WD("tensor", sqd)
        self.WD("vector", sqd)
        W("tensor", "svec", self.K_DONE)
        W("tensor", "svec", self.V_DONE)

        wq_g, q_mm_done, op_dma = [], [], []
        prev_o_last_smm = None
        prev_ov = None
        swv_prev = None
        prev_pv_last = None
        for o in range(NPAIR):
            if o >= 1:
                W("sync", "svec", wq_g[o - 1])
            wqd = dma("sync", "wq", wstr[o % 2][:, :], d["wq_b"].ap()[o, :, :])
            self.WD("vector", wqd)
            if o >= 1:
                W("vector", "stn", q_mm_done[o - 1])
            wq_g.append(E("vector", lambda e, b=o % 2: e.tensor_copy(
                wstrb[b][:, :], wstr[b][:, :]), "svec"))
            W("tensor", "svec", wq_g[o])
            rope_v = []
            for t4 in range(4):
                if prev_ov is not None and t4 == 0:
                    W("tensor", "svec", prev_ov)
                if t4 >= 2:
                    W("tensor", "svec", rope_v[t4 - 2])
                for k in range(16):
                    mmv = E("tensor", lambda e, o=o, t4=t4, k=k: e.matmul(
                        ps[t4 % 2][:, :], wstrb[o % 2][:, k * 128:(k + 1) * 128],
                        self.hA(k)[:, t4 * TS:(t4 + 1) * TS],
                        start=(k == 0), stop=(k == 15)), "stn")
                W("scalar", "stn", mmv)
                if swv_prev is not None:
                    W("scalar", "stn", swv_prev)
                arv = E("scalar", lambda e, t4=t4: e.activation(
                    self.qraw[:, :], ps[t4 % 2][:, :], AF.Copy), "sact")
                W("tensor", "sact", arv)
                swv_prev = E("tensor", lambda e, t4=t4: e.matmul(
                    ps[2 + t4 % 2][:, :], self.spswap[:, :], self.qraw[:, :],
                    start=True, stop=True), "stn")
                W("vector", "stn", swv_prev)
                if prev_o_last_smm is not None and t4 == 0:
                    W("vector", "stn", prev_o_last_smm)
                E("vector", lambda e, t4=t4: e.tensor_tensor(
                    self.tmp1[:, 0:TS], ps[t4 % 2][:, :],
                    rope_c[:, t4 * TS:(t4 + 1) * TS], ALU.mult))
                E("vector", lambda e, t4=t4: e.tensor_tensor(
                    self.tmp2[:, 0:TS], ps[2 + t4 % 2][:, :],
                    rope_s[:, t4 * TS:(t4 + 1) * TS], ALU.mult))
                rope_v.append(E("vector", lambda e, t4=t4: e.tensor_tensor(
                    self.qpair[:, t4 * TS:(t4 + 1) * TS], self.tmp1[:, 0:TS],
                    self.tmp2[:, 0:TS], ALU.add), "svec"))
            q_mm_done.append(self.val("stn"))
            kvch = o // 2
            vgrp = o // 2
            W("tensor", "svec", rope_v[-1])
            ov = None
            for qc in range(NTS):
                njt = 4 * (qc + 1)
                exp_a, pv_t, mk_v = {}, {}, {}
                if ov is not None:
                    W("tensor", "svec", ov)
                if prev_pv_last is not None:
                    W("scalar", "stn", prev_pv_last)
                def emit_pv(loc, j):
                    pti = 2 * (j % 2) + loc
                    if (loc, j) in mk_v:
                        W("tensor", "svec", mk_v[(loc, j)])
                    else:
                        W("tensor", "sact", exp_a[(loc, j)])
                    pv = E("tensor", lambda e, j=j, loc=loc, pti=pti, vgrp=vgrp,
                           njt=njt: e.matmul(
                               ps[loc][0:65, :],
                               self.vaug[j][:, vgrp * 65:vgrp * 65 + 65],
                               ptile[pti][:, :],
                               start=(j == 0), stop=(j == njt - 1)), "stn")
                    pv_t[(loc, j)] = pv

                for loc in range(2):
                    for j in range(njt):
                        sbank = (2 if j % 2 == 0 else 4) + loc
                        pti = 2 * (j % 2) + loc
                        if (loc, j - 2) in exp_a:
                            W("tensor", "sact", exp_a[(loc, j - 2)])
                        smm = E("tensor", lambda e, j=j, qc=qc, loc=loc, sbank=sbank,
                                kvch=kvch: e.matmul(
                                    ps[sbank][:, :],
                                    self.kfm[kvch][loc * 64:loc * 64 + 64,
                                                   j * 128:(j + 1) * 128],
                                    self.qpair[loc * 64:loc * 64 + 64,
                                               qc * TS:(qc + 1) * TS],
                                    start=True, stop=True), "stn")
                        W("scalar", "stn", smm)
                        if (loc, j - 2) in pv_t:
                            W("scalar", "stn", pv_t[(loc, j - 2)])
                        if (loc, j - 2) in mk_v:
                            W("scalar", "svec", mk_v[(loc, j - 2)])
                        ea = E("scalar", lambda e, sbank=sbank, pti=pti: e.activation(
                            ptile[pti][:, :], ps[sbank][:, :], AF.Exp,
                            bias=self.sccols[:, 0:1]), "sact")
                        exp_a[(loc, j)] = ea
                        dmask = j - 4 * qc
                        if dmask >= 0:
                            W("vector", "sact", ea)
                            mv = E("vector", lambda e, pti=pti, dm=dmask: e.tensor_tensor(
                                ptile[pti][:, :], ptile[pti][:, :],
                                self.smasks[:, dm * TS:(dm + 1) * TS], ALU.mult), "svec")
                            mk_v[(loc, j)] = mv
                        if j >= 1:
                            emit_pv(loc, j - 1)
                    emit_pv(loc, njt - 1)
                prev_o_last_smm = self.val("stn")
                prev_pv_last = pv_t[(1, njt - 1)]
                W("vector", "stn", pv_t[(1, njt - 1)])
                E("vector", lambda e: e.reciprocal(recrow[0:1, :], ps[0][64:65, :]))
                rv = E("vector", lambda e: e.reciprocal(recrow2[0:1, :],
                                                        ps[1][64:65, :]), "svec")
                W("tensor", "svec", rv)
                E("tensor", lambda e: e.matmul(
                    ps[2][0:64, :], self.sones1[0:1, 0:64], recrow[0:1, :],
                    start=True, stop=True), "stn")
                b2 = E("tensor", lambda e: e.matmul(
                    ps[3][0:64, :], self.sones1[0:1, 0:64], recrow2[0:1, :],
                    start=True, stop=True), "stn")
                W("vector", "stn", b2)
                if op_dma:
                    self.WD("vector", op_dma[-1])
                E("vector", lambda e: e.tensor_copy(bcsbA[:, :], ps[2][0:64, :]))
                E("vector", lambda e: e.tensor_copy(bcsbB[:, :], ps[3][0:64, :]))
                E("vector", lambda e, qc=qc: e.tensor_tensor(
                    opevA[:, qc * TS:(qc + 1) * TS], ps[0][0:64, :], bcsbA[:, :],
                    ALU.mult))
                ov = E("vector", lambda e, qc=qc: e.tensor_tensor(
                    opevB[:, qc * TS:(qc + 1) * TS], ps[1][0:64, :], bcsbB[:, :],
                    ALU.mult), "svec")
            prev_ov = ov
            W("gpsimd", "svec", ov)
            dma("gpsimd", "opo", d["opair_d"].ap()[o, 0:64, :], opevA[:, :])
            op_dma.append(dma("gpsimd", "opo", d["opair_d"].ap()[o, 64:128, :],
                              opevB[:, :]))
        self.ATT_DMA = op_dma[-1]
        self.tap("qpair", self.qpair, 128, 2048)
        self.tap("pt3", ptile[3], 128, 512)
        self.tap("rec2", recrow2, 1, 512)
        self.tap("bcsbB", bcsbB, 64, 512)
        self.tap("opevA", opevA, 64, 2048)
        self.tap("opevB", opevB, 64, 2048)

    def p7_outproj(self, scope):
        xstr = self.xstr
        E, W, dma = self.E, self.W, self.dma
        d, ps = self.dram, self.ps
        wstr = [self.sb(f"p7w{i}", [128, 2048], F32, scope) for i in range(2)]
        wstrb = [self.sb(f"p7wb{i}", [128, 2048], BF16, scope) for i in range(2)]
        xnout = [self.sb(f"xnout{i}", [128, T], F32, scope) for i in range(2)]
        xln = self.sb("p7xln", [128, T], F32, scope)

        self.WD("sync", self.ATT_DMA)
        o_in = []
        for i in range(16):
            o_in.append(dma("sync", "oin", self.hA(i)[:, :], d["opair_d"].ap()[i, :, :]))
        wo_g, xp_dma = [], []
        mm_done = []
        ep_a, ep_v = {}, {}
        for o2 in range(16):
            if o2 >= 2:
                W("sync", "sact", wo_g[o2 - 2])
            wod = dma("sync", f"wo{o2 % 2}", wstr[o2 % 2][:, :], d["wo_b"].ap()[o2, :, :])
            if o2 >= 2:
                W("sync", "svec", ep_v[(o2 - 2, 3)])
            xrd = dma("sync", f"xs{o2 % 2}", xstr[o2 % 2][:, :],
                      d["x_fm"].ap()[o2 * 128:(o2 + 1) * 128, :])
            self.WD("scalar", wod)
            if o2 >= 2:
                W("scalar", "stn", mm_done[o2 - 2])
            wo_g.append(E("scalar", lambda e, b=o2 % 2: e.copy(
                wstrb[b][:, :], wstr[b][:, :]), "sact"))
            W("tensor", "sact", wo_g[o2])
            if o2 == 0:
                self.WD("tensor", o_in[-1])
            for t4 in range(4):
                if (o2, t4 - 2) in ep_a:
                    W("tensor", "sact", ep_a[(o2, t4 - 2)])
                elif o2 >= 1:
                    W("tensor", "sact", ep_a[(o2 - 1, t4 + 2)])
                for k in range(16):
                    mmv = E("tensor", lambda e, o2=o2, t4=t4, k=k: e.matmul(
                        ps[t4 % 2][:, :], wstrb[o2 % 2][:, k * 128:(k + 1) * 128],
                        self.hA(k)[:, t4 * TS:(t4 + 1) * TS],
                        start=(k == 0), stop=(k == 15)), "stn")
                W("scalar", "stn", mmv)
                if (o2 - 1, t4) in ep_v:
                    W("scalar", "svec", ep_v[(o2 - 1, t4)])
                av = E("scalar", lambda e, o2=o2, t4=t4: e.activation(
                    xln[:, t4 * TS:(t4 + 1) * TS], ps[t4 % 2][:, :],
                    AF.Identity, scale=self.gacols[:, o2:o2 + 1]), "sact")
                ep_a[(o2, t4)] = av
                W("vector", "sact", av)
                if t4 == 0:
                    self.WD("vector", xrd)
                    if o2 >= 2:
                        self.WD("vector", xp_dma[o2 - 2])
                ep_v[(o2, t4)] = E("vector", lambda e, o2=o2, t4=t4: e.tensor_tensor(
                    xnout[o2 % 2][:, t4 * TS:(t4 + 1) * TS],
                    xln[:, t4 * TS:(t4 + 1) * TS],
                    xstr[o2 % 2][:, t4 * TS:(t4 + 1) * TS], ALU.add), "svec")
            mm_done.append(self.val("stn"))
            W("sync", "svec", ep_v[(o2, 3)])
            xp_dma.append(dma("sync", f"xp{o2 % 2}",
                              d["xprime"].ap()[o2 * 128:(o2 + 1) * 128, :],
                              xnout[o2 % 2][:, :]))
        self.XP_DMA = xp_dma[-1]
        self.tap("xn15", xnout[1], 128, 2048)
        self.tap("gacols", self.gacols, 128, 16)

    def p9_h2_fwht(self, scope):
        srow_a, srow_b, srow_c, srow_d = (self.srow_a, self.srow_b,
                                          self.srow_c, self.srow_d)
        xstr = self.xstr
        E, W, dma = self.E, self.W, self.dma
        d, ps = self.dram, self.ps
        xbf = [self.sb(f"p9b{i}", [128, 2048], BF16, scope) for i in range(2)]
        u1ev = [self.sb(f"u1ev{i}", [128, TS], BF16, scope) for i in range(2)]
        xln = self.sb("p9xln", [128, T], F32, scope)
        mxp = self.sb("mxp9", [128, T], F32, scope)
        mnp = self.sb("mnp9", [128, T], F32, scope)
        zq = self.sb("zq9", [128, TS], F32, scope)
        srow_sg = self.sb("srow_sg", [1, T], F32, scope)
        self.srow_sg = srow_sg

        sub_v, act_a, st1_t, u1_dma, uev_a = [], [], {}, [], []
        for c in range(NK):
            if c >= 2:
                W("sync", "svec", sub_v[c - 2])
            xd = dma("sync", f"xs{c % 2}", xstr[c % 2][:, :],
                     d["xprime"].ap()[c * 128:(c + 1) * 128, :])
            self.WD("vector", xd)
            if c >= 1:
                W("vector", "sact", act_a[c - 1])
            E("vector", lambda e, b=c % 2: e.tensor_tensor(
                xln[:, :], xstr[b][:, :], self.bc1[:, :], ALU.subtract))
            sub_v.append(E("vector", lambda e: e.tensor_tensor(
                xln[:, :], xln[:, :], self.bc2[:, :], ALU.mult), "svec"))
            W("scalar", "svec", sub_v[c])
            if c >= 2:
                W("scalar", "stn", st1_t[(c - 2, 3)])
            act_a.append(E("scalar", lambda e, c=c: e.activation(
                xbf[c % 2][:, :], xln[:, :], AF.Identity,
                scale=self.onePcb[:, c:c + 1],
                bias=self.mod96[:, 48 + c:49 + c]), "sact"))
            W("tensor", "sact", act_a[c])
            for t4 in range(4):
                it = c * 4 + t4
                if it >= 2:
                    W("tensor", "sact", uev_a[it - 2])
                stv = E("tensor", lambda e, c=c, t4=t4: e.matmul(
                    ps[t4 % 2][:, :], self.sh128[:, :],
                    xbf[c % 2][:, t4 * TS:(t4 + 1) * TS], start=True, stop=True), "stn")
                st1_t[(c, t4)] = stv
                W("scalar", "stn", stv)
                if it >= 2:
                    self.WD("scalar", u1_dma[it - 2])
                ua = E("scalar", lambda e, t4=t4: e.activation(
                    u1ev[t4 % 2][:, :], ps[t4 % 2][:, :], AF.Copy), "sact")
                uev_a.append(ua)
                W("gpsimd", "sact", ua)
                u1_dma.append(dma("gpsimd", f"ub{t4 % 2}",
                                  d["u1_d"].ap()[:, c, t4 * TS:(t4 + 1) * TS],
                                  u1ev[t4 % 2][:, :]))
        W("gpsimd", "ub0", self.val("ub0"))
        W("gpsimd", "ub1", self.val("ub1"))
        v1_in = []
        for g in range(16):
            v1_in.append(dma("gpsimd", "vin", self.hA(g)[:, :],
                             d["u1_d"].ap()[8 * g:8 * g + 8, :, :]
                             .rearrange("e c t -> (e c) t")))
        self.WD("tensor", v1_in[-1])
        mx_v = {}
        for g in range(16):
            for t4 in range(4):
                it = g * 4 + t4
                bank = 2 + it % 2
                if it >= 2:
                    W("tensor", "svec", mx_v[it - 2])
                pa = E("tensor", lambda e, g=g, t4=t4, bank=bank: e.matmul(
                    ps[bank][:, :], self.si8h16[:, :],
                    self.hA(g)[:, t4 * TS:(t4 + 1) * TS], start=True, stop=True), "stn")
                W("vector", "stn", pa)
                if g == 0:
                    E("vector", lambda e, t4=t4, bank=bank: e.tensor_copy(
                        mxp[:, t4 * TS:(t4 + 1) * TS], ps[bank][:, :]))
                    v = E("vector", lambda e, t4=t4, bank=bank: e.tensor_copy(
                        mnp[:, t4 * TS:(t4 + 1) * TS], ps[bank][:, :]), "svec")
                else:
                    E("vector", lambda e, t4=t4, bank=bank: e.tensor_tensor(
                        mxp[:, t4 * TS:(t4 + 1) * TS], mxp[:, t4 * TS:(t4 + 1) * TS],
                        ps[bank][:, :], ALU.max))
                    v = E("vector", lambda e, t4=t4, bank=bank: e.tensor_tensor(
                        mnp[:, t4 * TS:(t4 + 1) * TS], mnp[:, t4 * TS:(t4 + 1) * TS],
                        ps[bank][:, :], ALU.min), "svec")
                mx_v[it] = v
        E("vector", lambda e: e.tensor_scalar(mnp[:, :], mnp[:, :], -1.0, None, ALU.mult))
        mab = E("vector", lambda e: e.tensor_tensor(mxp[:, :], mxp[:, :], mnp[:, :],
                                                    ALU.max), "svec")
        mrow = self.xpart_max(mxp, T, mab, srow_c)
        E("vector", lambda e: e.tensor_scalar(srow_d[0:1, :], srow_c[0:1, :],
                                              C2048, EPS, ALU.mult, ALU.max))
        rcv = E("vector", lambda e: e.reciprocal(srow_a[0:1, :], srow_d[0:1, :]),
                "svec")
        W("vector", "svec", rcv)
        E("vector", lambda e: e.tensor_scalar(srow_a[0:1, :], srow_a[0:1, :],
                                              127.0 * C2048, None, ALU.mult))
        E("vector", lambda e: e.tensor_scalar(
            srow_sg[0:1, :], srow_d[0:1, :], self.wsrow[0:1, 8:9],
            1.0 / 127.0, ALU.mult, ALU.mult))
        srv = E("vector", lambda e: e.tensor_scalar(
            self.srow_su[0:1, :], srow_d[0:1, :], self.wsrow[0:1, 9:10],
            1.0 / 127.0, ALU.mult, ALU.mult), "svec")
        bcv = self.bcast_rows([(srow_a, self.bc2, T), (srow_sg, self.bc1, T)],
                              srv)
        W("tensor", "svec", bcv)
        rq = {}
        for g in range(16):
            for t4 in range(4):
                it = g * 4 + t4
                bank = 2 + it % 2
                if it >= 2:
                    W("tensor", "svec", rq[it - 2])
                pb = E("tensor", lambda e, g=g, t4=t4, bank=bank: e.matmul(
                    ps[bank][:, :], self.si8h16[:, :],
                    self.hA(g)[:, t4 * TS:(t4 + 1) * TS], start=True, stop=True), "stn")
                W("vector", "stn", pb)
                E("vector", lambda e, t4=t4, bank=bank: e.tensor_tensor(
                    zq[:, :], ps[bank][:, :], self.bc2[:, t4 * TS:(t4 + 1) * TS],
                    ALU.mult))
                rq[it] = E("vector", lambda e, g=g, t4=t4: e.tensor_scalar(
                    self.hA(g)[:, t4 * TS:(t4 + 1) * TS], zq[:, :],
                    MAGIC, -MAGIC, ALU.add, ALU.add), "svec")
        self.R_DONE = rq[63]
        self.tap("R0", self.hA(0), 128, 2048)

    def xpart_max(self, mtile, width, ready_v, out_row):
        E, W = self.E, self.W
        ps = self.ps
        W("tensor", "svec", ready_v)
        ns = width // 128
        tc = []
        for s in range(ns):
            bank = 2 + s % 2
            if s >= 2:
                W("tensor", "svec", tc[s - 2])
            tv = E("tensor", lambda e, s=s, bank=bank: e.transpose(
                ps[bank][:, 0:128], mtile[:, s * 128:(s + 1) * 128],
                self.sid_f[:, :]), "stn")
            W("vector", "stn", tv)
            tc.append(E("vector", lambda e, s=s, bank=bank: e.tensor_reduce(
                self.mcol[:, s:s + 1], ps[bank][:, 0:128], AX.X, ALU.max), "svec"))
        W("tensor", "svec", tc[-1])
        rv = []
        for s in range(ns):
            pr = self.ps_row if s % 2 == 0 else self.ps_row2
            if s >= 2:
                W("tensor", "svec", rv[s - 2])
            tv = E("tensor", lambda e, s=s, pr=pr: e.transpose(
                pr[0:1, 0:128], self.mcol[:, s:s + 1], self.sid_f[:, :]), "stn")
            W("vector", "stn", tv)
            rv.append(E("vector", lambda e, s=s, pr=pr, out_row=out_row: e.tensor_copy(
                out_row[0:1, s * 128:(s + 1) * 128], pr[0:1, 0:128]), "svec"))
        return rv[-1]

    def p10_tdq(self, scope):
        E, W, dma = self.E, self.W, self.dma
        d = self.dram
        wstr = [self.sb(f"pAw{i}", [128, 2048], F32, scope) for i in range(2)]
        wstrb = [self.sb(f"pAwb{i}", [128, 2048], BF16, scope) for i in range(2)]
        qt = self.sb("qt10", [128, 2048], F32, scope)
        tq, tdma = [], []
        for i in range(64):
            o2, kq = divmod(i, 4)
            if i >= 2:
                W("sync", "svec", tq[i - 2])
            dd = dma("sync", f"tf{i % 2}", wstr[i % 2][:, :],
                     d["td_b"].ap()[o2, :, kq * 2048:(kq + 1) * 2048])
            self.WD("vector", dd)
            if i >= 2:
                self.WD("vector", tdma[i - 2])
            E("vector", lambda e, b=i % 2: e.tensor_scalar(
                qt[:, :], wstr[b][:, :], self.wsd_i[:, :], None, ALU.mult))
            E("vector", lambda e: e.tensor_scalar(
                qt[:, :], qt[:, :], 1.0, -1.0, ALU.min, ALU.max))
            tq.append(E("vector", lambda e, b=i % 2: e.tensor_scalar(
                wstrb[b][:, :], qt[:, :], MAGIC, -MAGIC, ALU.add, ALU.add), "svec"))
            W("gpsimd", "svec", tq[-1])
            tdma.append(dma("gpsimd", f"tq{i % 2}",
                            d["tdq_d"].ap()[o2, :, kq * 2048:(kq + 1) * 2048],
                            wstrb[i % 2][:, :]))
        self.TDQ_TOT = [("tq0", self.val("tq0")), ("tq1", self.val("tq1"))]

    def p11_gateup(self, scope):
        E, W, dma = self.E, self.W, self.dma
        d, ps = self.dram, self.ps
        wstr = [self.sb(f"pBg{i}", [128, 2048], F32, scope) for i in range(2)]
        wstrb = [self.sb(f"pBgb{i}", [128, 2048], BF16, scope) for i in range(2)]
        wstr2 = [self.sb(f"pBu{i}", [128, 2048], F32, scope) for i in range(2)]
        wstr2b = [self.sb(f"pBub{i}", [128, 2048], BF16, scope) for i in range(2)]
        qt = self.sb("qt11", [128, 4096], F32, scope)
        zt1 = [self.sb(f"zt1_{i}", [128, TS], F32, scope) for i in range(2)]
        zt2 = [self.sb(f"zt2_{i}", [128, TS], F32, scope) for i in range(2)]
        zbf = [self.sb(f"zbf{i}", [128, TS], BF16, scope) for i in range(2)]
        u2ev = [self.sb(f"u2ev{i}", [128, TS], BF16, scope) for i in range(2)]

        gq, gq_a, u2_dma = [], [], []
        z_v, u_a, s1_t = {}, {}, {}
        mm_done = []
        for o in range(64):
            if o >= 2:
                W("sync", "sact", gq_a[o - 2])
            d1 = dma("sync", f"tg{o % 2}", wstr[o % 2][:, :], d["tg_b"].ap()[o, :, :])
            d2 = dma("sync", f"tu{o % 2}", wstr2[o % 2][:, :], d["tu_b"].ap()[o, :, :])
            self.WD("scalar", d1)
            self.WD("scalar", d2)
            if o >= 2:
                W("scalar", "stn", mm_done[o - 2])
            for qx, (src, dst, wsc) in enumerate(
                    ((wstr[o % 2], wstrb[o % 2], self.wsg_i),
                     (wstr2[o % 2], wstr2b[o % 2], self.wsu_i))):
                av = E("scalar", lambda e, src=src, wsc=wsc, qx=qx: e.activation(
                    qt[:, qx * 2048:(qx + 1) * 2048], src[:, :], AF.Identity,
                    scale=wsc[:, :]), "sact")
                W("vector", "sact", av)
                E("vector", lambda e, qx=qx: e.tensor_scalar(
                    qt[:, qx * 2048:(qx + 1) * 2048], qt[:, qx * 2048:(qx + 1) * 2048],
                    1.0, -1.0, ALU.min, ALU.max))
                qv = E("vector", lambda e, dst=dst, qx=qx: e.tensor_scalar(
                    dst[:, :], qt[:, qx * 2048:(qx + 1) * 2048],
                    MAGIC, -MAGIC, ALU.add, ALU.add), "svec")
            gq.append(qv)
            gq_a.append(av)
            W("tensor", "svec", qv)
            def emit_s1(pit):
                po, pt4 = divmod(pit, 4)
                if pit >= 2:
                    W("tensor", "sact", u_a[pit - 2])
                W("tensor", "svec", z_v[pit])
                s1_t[pit] = E("tensor", lambda e, pt4=pt4: e.matmul(
                    ps[4 + pit % 2][:, :], self.sh128[:, :], zbf[pt4 % 2][:, :],
                    start=True, stop=True), "stn")
                W("scalar", "stn", s1_t[pit])
                if pit >= 2:
                    self.WD("scalar", u2_dma[pit - 2])
                u_a[pit] = E("scalar", lambda e, pt4=pt4: e.activation(
                    u2ev[pt4 % 2][:, :], ps[4 + pit % 2][:, :], AF.Copy), "sact")
                W("gpsimd", "sact", u_a[pit])
                u2_dma.append(dma("gpsimd", f"ub{pt4 % 2}",
                                  d["u2_d"].ap()[:, po, pt4 * TS:(pt4 + 1) * TS],
                                  u2ev[pt4 % 2][:, :]))

            for t4 in range(4):
                it = o * 4 + t4
                if it >= 2:
                    W("tensor", "svec", z_v[it - 2])
                for k in range(16):
                    E("tensor", lambda e, o=o, t4=t4, k=k: e.matmul(
                        ps[t4 % 2][:, :], wstrb[o % 2][:, k * 128:(k + 1) * 128],
                        self.hA(k)[:, t4 * TS:(t4 + 1) * TS],
                        start=(k == 0), stop=(k == 15)), "stn")
                for k in range(16):
                    gu = E("tensor", lambda e, o=o, t4=t4, k=k: e.matmul(
                        ps[2 + t4 % 2][:, :], wstr2b[o % 2][:, k * 128:(k + 1) * 128],
                        self.hA(k)[:, t4 * TS:(t4 + 1) * TS],
                        start=(k == 0), stop=(k == 15)), "stn")
                W("vector", "stn", gu)
                if it >= 2:
                    W("vector", "sact", z_a_prev[it - 2])
                E("vector", lambda e, t4=t4: e.tensor_tensor(
                    zt1[t4 % 2][:, :], ps[t4 % 2][:, :],
                    self.bc1[:, t4 * TS:(t4 + 1) * TS], ALU.mult), "svec")
                ztv = self.val("svec")
                W("scalar", "svec", ztv)
                if it >= 2:
                    W("scalar", "svec", z_v[it - 2])
                za = E("scalar", lambda e, t4=t4: e.activation(
                    zt2[t4 % 2][:, :], zt1[t4 % 2][:, :], AF.Silu), "sact")
                if it == 0:
                    z_a_prev = {}
                z_a_prev[it] = za
                W("vector", "sact", za)
                if it >= 2:
                    W("vector", "stn", s1_t.get(it - 2, None) or s1_t[it - 2])
                z_v[it] = E("vector", lambda e, t4=t4: e.tensor_tensor(
                    zbf[t4 % 2][:, :], zt2[t4 % 2][:, :], ps[2 + t4 % 2][:, :],
                    ALU.mult), "svec")
                if it >= 1:
                    emit_s1(it - 1)
            mm_done.append(self.val("stn"))
        emit_s1(255)
        self.GU_TOT = [("ub0", self.val("ub0")), ("ub1", self.val("ub1"))]

    def p12_down(self, scope):
        srow_a, srow_b, srow_c, srow_d = (self.srow_a, self.srow_b,
                                          self.srow_c, self.srow_d)
        xstr = self.xstr
        E, W, dma = self.E, self.W, self.dma
        d, ps = self.dram, self.ps
        tds = [self.sb(f"tds{i}", [128, 8192], BF16, scope) for i in range(2)]
        outsb = [self.sb(f"outsb{i}", [128, TS], F32, scope) for i in range(2)]
        mxp = self.sb("mxp12", [128, TS], F32, scope)
        mnp = self.sb("mnp12", [128, TS], F32, scope)
        zq = self.sb("zq12", [128, TS], F32, scope)

        def v2A(g):
            return self.bigA1[:, g * 512:(g + 1) * 512]

        out_dmas = []
        prev_down_done = None
        for tok in self.GU_TOT:
            self.WD("gpsimd", tok)
        for q4 in range(4):
            HB = q4 * TS
            if prev_down_done is not None:
                W("gpsimd", "stn", prev_down_done)
            v2_in = []
            for g in range(64):
                v2_in.append(dma("gpsimd", "vin", v2A(g)[:, :],
                                 d["u2_d"].ap()[2 * g:2 * g + 2, :, HB:HB + TS]
                                 .rearrange("e b t -> (e b) t")))
            self.WD("tensor", v2_in[-1])
            mx_v = {}
            for g in range(64):
                bank = 2 + g % 2
                if g >= 2:
                    W("tensor", "svec", mx_v[g - 2])
                pa = E("tensor", lambda e, g=g, bank=bank: e.matmul(
                    ps[bank][:, :], self.si2h64[:, :], v2A(g)[:, :],
                    start=True, stop=True), "stn")
                W("vector", "stn", pa)
                if g == 0:
                    E("vector", lambda e, bank=bank: e.tensor_copy(
                        mxp[:, :], ps[bank][:, :]))
                    v = E("vector", lambda e, bank=bank: e.tensor_copy(
                        mnp[:, :], ps[bank][:, :]), "svec")
                else:
                    E("vector", lambda e, bank=bank: e.tensor_tensor(
                        mxp[:, :], mxp[:, :], ps[bank][:, :], ALU.max))
                    v = E("vector", lambda e, bank=bank: e.tensor_tensor(
                        mnp[:, :], mnp[:, :], ps[bank][:, :], ALU.min), "svec")
                mx_v[g] = v
            E("vector", lambda e: e.tensor_scalar(mnp[:, :], mnp[:, :], -1.0, None,
                                                  ALU.mult))
            mab = E("vector", lambda e: e.tensor_tensor(mxp[:, :], mxp[:, :], mnp[:, :],
                                                        ALU.max), "svec")
            self.xpart_max(mxp, TS, mab, srow_c)
            E("vector", lambda e, HB=HB: e.tensor_tensor(
                srow_c[0:1, 0:TS], srow_c[0:1, 0:TS],
                self.srow_su[0:1, HB:HB + TS], ALU.mult))
            E("vector", lambda e: e.tensor_scalar(
                srow_d[0:1, 0:TS], srow_c[0:1, 0:TS], C8192, EPS,
                ALU.mult, ALU.max))
            rcv = E("vector", lambda e: e.reciprocal(srow_a[0:1, 0:TS],
                                                     srow_d[0:1, 0:TS]), "svec")
            W("vector", "svec", rcv)
            E("vector", lambda e: e.tensor_scalar(
                srow_a[0:1, 0:TS], srow_a[0:1, 0:TS], 127.0 * C8192,
                None, ALU.mult))
            E("vector", lambda e, HB=HB: e.tensor_tensor(
                srow_a[0:1, 0:TS], srow_a[0:1, 0:TS],
                self.srow_su[0:1, HB:HB + TS], ALU.mult))
            s2v = E("vector", lambda e: e.tensor_scalar(
                srow_b[0:1, 0:TS], srow_d[0:1, 0:TS],
                self.wsrow[0:1, 10:11], 1.0 / 127.0, ALU.mult, ALU.mult), "svec")
            bcv = self.bcast_rows([(srow_a, self.bc2, TS),
                                   (srow_b, self.bc1, TS)], s2v)
            W("tensor", "svec", bcv)
            rq = {}
            for g in range(64):
                bank = 2 + g % 2
                if g >= 2:
                    W("tensor", "svec", rq[g - 2])
                pb = E("tensor", lambda e, g=g, bank=bank: e.matmul(
                    ps[bank][:, :], self.si2h64[:, :], v2A(g)[:, :],
                    start=True, stop=True), "stn")
                W("vector", "stn", pb)
                E("vector", lambda e, bank=bank: e.tensor_tensor(
                    zq[:, :], ps[bank][:, :], self.bc2[:, 0:TS], ALU.mult))
                rq[g] = E("vector", lambda e, g=g: e.tensor_scalar(
                    v2A(g)[:, :], zq[:, :], MAGIC, -MAGIC, ALU.add, ALU.add), "svec")
            W("tensor", "svec", rq[63])
            td_in, ep_a, ep_v, mmd = [], {}, {}, {}
            for o2 in range(16):
                if o2 >= 2:
                    W("sync", "stn", mmd[o2 - 2])
                if q4 == 0 and o2 == 0:
                    for tok in self.TDQ_TOT:
                        self.WD("sync", tok)
                td_in.append(dma("sync", f"ts{o2 % 2}", tds[o2 % 2][:, :],
                                 d["tdq_d"].ap()[o2, :, :]))
                if o2 >= 2:
                    W("sync", "svec", ep_v[o2 - 2])
                xrd = dma("sync", f"xs{o2 % 2}", xstr[o2 % 2][:, 0:TS],
                          d["xprime"].ap()[o2 * 128:(o2 + 1) * 128, HB:HB + TS])
                self.WD("tensor", td_in[o2])
                bank = 4 + o2 % 2
                if o2 >= 2:
                    W("tensor", "sact", ep_a[o2 - 2])
                for g in range(64):
                    dmm = E("tensor", lambda e, o2=o2, g=g, bank=bank: e.matmul(
                        ps[bank][:, :], tds[o2 % 2][:, g * 128:(g + 1) * 128],
                        v2A(g)[:, :], start=(g == 0), stop=(g == 63)), "stn")
                mmd[o2] = self.val("stn")
                W("scalar", "stn", dmm)
                if o2 >= 1:
                    W("scalar", "svec", ep_v[o2 - 1])
                da = E("scalar", lambda e, o2=o2, bank=bank: e.activation(
                    self.tmp2[:, 0:TS], ps[bank][:, :], AF.Identity,
                    scale=self.gbcols[:, o2:o2 + 1]), "sact")
                ep_a[o2] = da
                W("vector", "sact", da)
                self.WD("vector", xrd)
                if len(out_dmas) >= 2:
                    self.WD("vector", out_dmas[-2])
                E("vector", lambda e: e.tensor_tensor(
                    self.tmp2[:, 0:TS], self.tmp2[:, 0:TS], self.bc1[:, 0:TS],
                    ALU.mult))
                ep_v[o2] = E("vector", lambda e, o2=o2: e.tensor_tensor(
                    outsb[o2 % 2][:, :], self.tmp2[:, 0:TS],
                    xstr[o2 % 2][:, 0:TS], ALU.add), "svec")
                W("sync", "svec", ep_v[o2])
                out_dmas.append(dma("sync", f"ou{o2 % 2}",
                                    d["out_fm"].ap()[o2 * 128:(o2 + 1) * 128,
                                                     HB:HB + TS],
                                    outsb[o2 % 2][:, :]))
            prev_down_done = self.val("stn")
        self.W("sync", "ou0", self.val("ou0"))
        self.W("sync", "ou1", self.val("ou1"))

    def tap(self, key, tile, rows, cols):
        self.dbg_tiles[key] = (tile, rows, cols)

    def emit_taps(self):
        if not self.dbg:
            return
        self.barrier()
        off = 0
        for key, r, c in self.dbg:
            tile, tr, tc = self.dbg_tiles[key]
            assert r == tr and c == tc, (key, r, c, tr, tc)
            self.dma("gpsimd", "dbg", self.dram["dbg_d"].ap()[0:r, off:off + c],
                     tile[0:r, 0:c])
            off += c
        self.W("gpsimd", "dbg", self.val("dbg"))

    def build(self):
        self.declare()
        self.p0_consts()
        with contextlib.ExitStack() as sc:
            self.p1_mod_ws(sc)
            self.barrier()
        if self.upto >= 3:
            with contextlib.ExitStack() as sc:
                xbf = [self.sb(f"p3b{i}", [128, 2048], BF16, sc) for i in range(2)]
                xsq = [self.sb(f"p3q{i}", [128, 2048], BF16, sc) for i in range(2)]
                xln = self.sb("p3xln", [128, T], F32, sc)
                self.alloc_rows(sc)
                self.alloc_xstr(sc)
                self.ln_stats(self.dram["x_fm"], xbf, xsq)
                self.barrier()
                self.p4_h(self.dram["x_fm"], self.onePca, 0, self.hA, xln)
                self.barrier()
            self.tap("h0", self.hA(0), 128, 2048)
            self.tap("h15", self.hA(15), 128, 2048)
        if self.upto >= 5:
            with contextlib.ExitStack() as sc:
                self.p5_kv(sc)
                self.barrier()
                if self.upto >= 6:
                    self.p6_attention(sc)
                    self.barrier()
                for _ki in range(8):
                    self.tap(f"k{_ki}", self.kfm[_ki], 128, 2048)
                self.tap("v0", self.vaug[0], 128, 520)
        if self.upto >= 7:
            with contextlib.ExitStack() as sc:
                self.alloc_xstr(sc)
                self.p7_outproj(sc)
                self.barrier()
        if self.upto >= 8:
            with contextlib.ExitStack() as sc:
                xbf = [self.sb(f"p8b{i}", [128, 2048], BF16, sc) for i in range(2)]
                xsq = [self.sb(f"p8q{i}", [128, 2048], BF16, sc) for i in range(2)]
                self.alloc_rows(sc)
                self.alloc_xstr(sc)
                self.ln_stats(self.dram["xprime"], xbf, xsq)
                self.barrier()
        if self.upto >= 9:
            with contextlib.ExitStack() as sc:
                self.alloc_rows(sc)
                self.alloc_xstr(sc)
                self.p9_h2_fwht(sc)
                self.barrier()
        if self.upto >= 10:
            with contextlib.ExitStack() as sc:
                self.p10_tdq(sc)
                self.barrier()
        if self.upto >= 11:
            with contextlib.ExitStack() as sc:
                self.p11_gateup(sc)
                self.barrier()
        if self.upto >= 12:
            with contextlib.ExitStack() as sc:
                self.alloc_rows(sc, TS)
                self.alloc_xstr(sc)
                self.p12_down(sc)
        self.emit_taps()
        self.play()
        self.es.close()
        return self.nc



def _blk4(wT):
    K, M = wT.shape
    ko, mo = K // 128, M // 128
    out = np.ascontiguousarray(
        wT.reshape(ko, 128, mo, 128).transpose(2, 1, 0, 3))
    return np.ascontiguousarray(out.reshape(mo, 128, ko * 128))


def host_prep(inputs):
    x = inputs["L0_attn_x"]
    wq, wk, wv, wo = (inputs["L0_attn_wq"], inputs["L0_attn_wk"],
                      inputs["L0_attn_wv"], inputs["L0_attn_wo"])
    wg, wu, wd = (inputs["L0_ffn_w_gate"], inputs["L0_ffn_w_up"],
                  inputs["L0_ffn_w_down"])
    cc_in, wmod = inputs["L0_c"], inputs["L0_wmod"]

    sh = {}
    sh["wq_b"] = _blk4(np.ascontiguousarray(np.asarray(wq).T))
    wkT = np.ascontiguousarray(np.asarray(wk).T)
    wk_dup = np.concatenate([
        np.concatenate([wkT[:, h * 64:(h + 1) * 64]] * 2, axis=1)
        for h in range(8)], axis=1)
    sh["wk_b"] = _blk4(wk_dup)
    sh["wv_t"] = np.ascontiguousarray(np.asarray(wv).T)
    sh["wo_b"] = _blk4(np.ascontiguousarray(np.asarray(wo).T))
    qq = np.arange(128)
    kk16 = np.arange(16)
    frow = (qq[None, :] % 16) * 128 + 8 * kk16[:, None] + qq[None, :] // 16
    og = np.arange(64)
    cg = np.arange(128)
    row_idx = cg[None, None, None, :] * 64 + og[:, None, None, None]
    col_idx = frow.T[None, :, :, None]
    wg_a, wu_a, wd_a = np.asarray(wg), np.asarray(wu), np.asarray(wd)
    tg4 = wg_a[row_idx, col_idx]
    tu4 = wu_a[row_idx, col_idx]
    sh["tg_b"] = np.ascontiguousarray(tg4.reshape(64, 128, 2048).astype(np.float32))
    sh["tu_b"] = np.ascontiguousarray(tu4.reshape(64, 128, 2048).astype(np.float32))
    sh["td_b"] = _blk4(np.ascontiguousarray(wd_a.T))
    sh["ident_f"] = np.eye(128, dtype=np.float32)
    H128 = _hadamard(128)
    sh["h128_b"] = H128.astype(ml_dtypes.bfloat16)
    sh["i8h16_b"] = np.kron(np.eye(8), _hadamard(16)).astype(ml_dtypes.bfloat16)
    sh["i2h64_b"] = np.kron(np.eye(2), _hadamard(64)).astype(ml_dtypes.bfloat16)
    psw = np.zeros((128, 128), np.float32)
    for i in range(128):
        psw[i, i ^ 1] = 1.0
    sh["pswap_b"] = psw.astype(ml_dtypes.bfloat16)
    i_idx = (np.arange(128) % 64) // 2
    omega = 1.0 / 10000.0 ** (np.arange(0, 64, 2, dtype=np.float64) / 64.0)
    ang = omega[i_idx][:, None] * np.arange(T, dtype=np.float64)[None, :]
    cos_t = np.cos(ang)
    sin_t = np.sin(ang)
    sign = np.where(np.arange(128) % 2 == 0, -1.0, 1.0)[:, None]
    sh["cosq"] = (0.125 * cos_t).astype(np.float32)
    sh["sinq"] = (0.125 * sin_t * sign).astype(np.float32)
    sh["cosk"] = cos_t.astype(np.float32)
    sh["sink"] = (sin_t * sign).astype(np.float32)
    masks = np.zeros((4, 128, TS), np.float32)
    rr = np.arange(128)[:, None]
    ss = np.arange(TS)[None, :]
    for dd in range(4):
        masks[dd] = (ss >= rr + 128 * dd).astype(np.float32)
    sh["masks"] = np.ascontiguousarray(
        masks.transpose(1, 0, 2).reshape(128, 4 * TS)).astype(ml_dtypes.bfloat16)
    ccols = np.zeros((128, 8), np.float32)
    ccols[:, 0] = EXPB
    ccols[:, 1] = EPS
    ccols[:, 2] = 1.0
    ccols[:, 3] = 0.25
    sh["ccols"] = ccols
    sh["ones1f"] = np.ones((1, 128), np.float32)
    sh["onescolb"] = np.ones((128, 1), ml_dtypes.bfloat16)
    c_np = np.asarray(cc_in)
    c_t = np.zeros((128, 128), np.float32)
    for ch in range(16):
        for b in range(8):
            c_t[:, ch * 8 + b] = c_np[b, ch * 128:(ch + 1) * 128]
    sh["c_t"] = c_t

    wmod_np = np.asarray(wmod)
    x_np = np.asarray(x)
    per_core = []
    for i in range(NCORES):
        dd = dict(sh)
        dd["x_fm"] = np.ascontiguousarray(x_np[i].T.astype(np.float32))
        oh = np.zeros((8, 1), np.float32)
        oh[i, 0] = 1.0
        dd["onehot"] = oh
        dd["wmod_sl"] = np.ascontiguousarray(wmod_np[:, i * 1536:(i + 1) * 1536])
        dd["wgs"] = np.ascontiguousarray(sh["tg_b"][8 * i:8 * i + 8])
        dd["wus"] = np.ascontiguousarray(sh["tu_b"][8 * i:8 * i + 8])
        dd["wds"] = np.ascontiguousarray(
            sh["td_b"][2 * i:2 * i + 2].reshape(2, 128, 4, 2048)
            .transpose(0, 2, 1, 3).reshape(8, 128, 2048))
        per_core.append(dd)
    return per_core


def kernel(**inputs):
    in_maps = host_prep(inputs)
    nc = KB().build()
    res = run_bass_kernel_spmd(nc, in_maps, core_ids=list(range(NCORES)))
    out = np.stack([np.asarray(res.results[i]["out_fm"]).T for i in range(NCORES)])
    return np.ascontiguousarray(out).astype(np.float32)


# version 04
# speedup vs baseline: 2.2441x; 1.0103x over previous
import contextlib
import numpy as np
import ml_dtypes
import concourse.bass as bass
import concourse.mybir as mybir
from concourse.bass_utils import run_bass_kernel_spmd

F32 = mybir.dt.float32
BF16 = mybir.dt.bfloat16
AF = mybir.ActivationFunctionType
ALU = mybir.AluOpType
AX = mybir.AxisListType

NCORES = 8
T = 2048
D = 2048
NK = 16
NTS = 4
TS = 512
NPAIR = 16
MAGIC = 12582912.0
EXPB = -18.0
EPS = 1e-5
C2048 = float(1.0 / np.sqrt(np.float64(2048.0)))
C8192 = float(1.0 / np.sqrt(np.float64(8192.0)))
NELW = 8192.0 * 2048.0

ENGS = ("tensor", "vector", "scalar", "gpsimd", "sync")
NO_CC = False
DBG_WS = False


def _hadamard(n):
    h = np.array([[1.0]])
    while h.shape[0] < n:
        h = np.block([[h, h], [h, -h]])
    return h


class KB:

    def __init__(self, upto=99, dbg=()):
        self.upto = upto
        self.dbg = list(dbg)
        nc = self.nc = bass.Bass("TRN2", target_bir_lowering=False,
                                 num_devices=NCORES)
        self.q = {e: [] for e in ENGS}
        self.es = contextlib.ExitStack()
        self.sems = {}
        for s in ("cc", "cc2", "stn", "svec", "sact", "sgps"):
            self.sems[s] = [self.es.enter_context(nc.semaphore(s)), 0]
        self.dbg_tiles = {}

    FENCE = True
    _ENG_SEM = {"vector": "svec", "scalar": "sact", "gpsimd": "sgps"}

    def E(self, eng, build, sem=None, k=1, nofence=False):
        fence = self.FENCE and k == 1 and eng in self._ENG_SEM and not nofence
        if fence and sem is None:
            sem = self._ENG_SEM[eng]
        v = None
        if sem is not None:
            s = self.sem_of(sem)
            s[1] += k
            v = s[1]
            h = s[0]

            def go(e, build=build, h=h, k=k):
                build(e).then_inc(h, k)
        else:
            def go(e, build=build):
                build(e)
        self.q[eng].append(go)
        if fence:
            self.W(eng, sem, v)
        return v

    def W(self, eng, sem, v):
        if v is None or v <= 0:
            return
        h = self.sem_of(sem)[0]
        self.q[eng].append(lambda e, h=h, v=v: e.wait_ge(h, v))

    def sem_of(self, name):
        if name not in self.sems:
            self.sems[name] = [self.es.enter_context(self.nc.semaphore(name)), 0]
        return self.sems[name]

    def dma(self, eng, stream, dst, src, **kw):
        self.sem_of(stream)
        v = self.E(eng, lambda e, d=dst, s=src, kw=kw: e.dma_start(d, s, **kw),
                   stream, 16)
        return (stream, v)

    def WD(self, eng, tok):
        if tok is None:
            return
        self.W(eng, tok[0], tok[1])

    def val(self, sem):
        return self.sems[sem][1]

    def barrier(self):
        vals = [(s, sv[1]) for s, sv in list(self.sems.items())]
        for eng in ENGS:
            for s, v in vals:
                self.W(eng, s, v)

    _uid = 0

    def sb(self, name, shape, dtype, scope=None):
        KB._uid += 1
        t = (scope or self.es).enter_context(
            self.nc.sbuf_tensor(f"{name}_u{KB._uid}", shape, dtype))
        return t

    def play(self):
        q = self.q
        with self.nc.Block() as blk:
            @blk.tensor
            def _(e):
                for f in q["tensor"]:
                    f(e)

            @blk.vector
            def _(e):
                for f in q["vector"]:
                    f(e)

            @blk.scalar
            def _(e):
                for f in q["scalar"]:
                    f(e)

            @blk.gpsimd
            def _(e):
                for f in q["gpsimd"]:
                    f(e)

            @blk.sync
            def _(e):
                for f in q["sync"]:
                    f(e)

    def declare(self):
        nc = self.nc
        dram = self.dram = {}
        for name, shape in [
                ("x_fm", [D, T]), ("wq_b", [16, 128, 2048]), ("wk_b", [8, 128, 2048]),
                ("wv_t", [2048, 512]), ("wo_b", [16, 128, 2048]),
                ("tg_b", [64, 128, 2048]), ("tu_b", [64, 128, 2048]),
                ("td_b", [16, 128, 8192]), ("wmod_sl", [2048, 1536]),
                ("c_t", [128, 128]), ("onehot", [8, 1]),
                ("wgs", [8, 128, 2048]), ("wus", [8, 128, 2048]), ("wds", [8, 128, 2048]),
                ("ident_f", [128, 128]), ("cosq", [128, T]), ("sinq", [128, T]),
                ("cosk", [128, T]), ("sink", [128, T]), ("ccols", [128, 8]),
                ("ones1f", [1, 128])]:
            dram[name] = nc.dram_tensor(name, shape, F32, kind="ExternalInput")
        for name, shape in [
                ("h128_b", [128, 128]), ("i8h16_b", [128, 128]), ("i2h64_b", [128, 128]),
                ("pswap_b", [128, 128]), ("masks", [128, 4 * TS]),
                ("onescolb", [128, 1])]:
            dram[name] = nc.dram_tensor(name, shape, BF16, kind="ExternalInput")
        dram["out_fm"] = nc.dram_tensor("out_fm", [D, T], F32, kind="ExternalOutput")
        if self.dbg:
            dram["dbg_d"] = nc.dram_tensor("dbg_d", [128, 65536], F32,
                                           kind="ExternalOutput")
        dram["xprime"] = nc.dram_tensor("xprime", [D, T], F32)
        dram["u1_d"] = nc.dram_tensor("u1_d", [128, 16, T], BF16)
        dram["u2_d"] = nc.dram_tensor("u2_d", [128, 64, T], BF16)
        dram["tdq_d"] = nc.dram_tensor("tdq_d", [16, 128, 8192], BF16)
        dram["opair_d"] = nc.dram_tensor("opair_d", [16, 128, T], BF16)
        dram["ag_in"] = nc.dram_tensor("ag_in", [8, 1536], F32)
        dram["ag_out"] = nc.dram_tensor("ag_out", [64, 1536], F32)
        dram["ar_in"] = nc.dram_tensor("ar_in", [1, 4], F32)
        dram["ar_out"] = nc.dram_tensor("ar_out", [1, 4], F32)

        sb = self.sb
        self.bigA1 = sb("bigA1", [128, 32768], BF16)
        self.sid_f = sb("sid_f", [128, 128], F32)
        self.sh128 = sb("sh128", [128, 128], BF16)
        self.si8h16 = sb("si8h16", [128, 128], BF16)
        self.si2h64 = sb("si2h64", [128, 128], BF16)
        self.spswap = sb("spswap", [128, 128], BF16)
        self.sccols = sb("sccols", [128, 8], F32)
        self.sones1 = sb("sones1", [1, 128], F32)
        self.sonescol = sb("sonescol", [128, 1], BF16)
        self.sonehot = sb("sonehot", [8, 1], F32)
        self.mod96 = sb("mod96", [128, 96], F32)
        self.onePca = sb("onePca", [128, 16], F32)
        self.onePcb = sb("onePcb", [128, 16], F32)
        self.gacols = sb("gacols", [128, 16], F32)
        self.gbcols = sb("gbcols", [128, 16], F32)
        self.wsrow = sb("wsrow", [1, 16], F32)
        self.wsg_i = sb("wsg_i", [128, 1], F32)
        self.wsu_i = sb("wsu_i", [128, 1], F32)
        self.wsd_i = sb("wsd_i", [128, 1], F32)
        self.srow_su = sb("srow_su", [1, T], F32)
        self.bc1 = sb("bc1", [128, T], F32)
        self.bc2 = sb("bc2", [128, T], F32)
        self.tmp1 = sb("tmp1", [128, TS], F32)
        self.tmp2 = sb("tmp2", [128, TS], F32)
        self.mcol = sb("mcol", [128, 16], F32)

        self.ps = [self.es.enter_context(self.nc.psum_tensor(f"ps{i}", [128, TS], F32))
                   for i in range(6)]
        self.ps_row = self.es.enter_context(self.nc.psum_tensor("ps_row", [1, TS], F32))
        self.ps_row2 = self.es.enter_context(self.nc.psum_tensor("ps_row2", [1, TS], F32))

    def hA(self, i):
        return self.bigA1[:, i * 2048:(i + 1) * 2048]

    def alloc_xstr(self, scope):
        self.xstr = [self.sb(f"xstr{i}", [128, T], F32, scope) for i in range(2)]

    def alloc_rows(self, scope, width=T):
        self.srow_a = self.sb("srow_a", [1, width], F32, scope)
        self.srow_b = self.sb("srow_b", [1, width], F32, scope)
        self.srow_c = self.sb("srow_c", [1, width], F32, scope)
        self.srow_d = self.sb("srow_d", [1, width], F32, scope)

    def p0_consts(self):
        d = self.dram
        pairs = [(self.sid_f, d["ident_f"]), (self.sh128, d["h128_b"]),
                 (self.si8h16, d["i8h16_b"]), (self.si2h64, d["i2h64_b"]),
                 (self.spswap, d["pswap_b"]), (self.sccols, d["ccols"]),
                 (self.sonehot, d["onehot"]), (self.sonescol, d["onescolb"]),
                 (self.sones1, d["ones1f"])]
        for dst, src in pairs:
            tok = self.dma("sync", "cst", dst[:, :], src.ap())
        self.CONSTS_D = tok

    def p1_mod_ws(self, scope):
        E, W, dma = self.E, self.W, self.dma
        d, ps = self.dram, self.ps
        siluc = self.sb("siluc", [128, 128], F32, scope)
        siluc_b = self.sb("siluc_b", [128, 128], BF16, scope)
        modsl = self.sb("modsl", [8, 1536], F32, scope)
        exsl = [self.sb(f"exsl{i}", [8, TS], F32, scope) for i in range(2)]
        rowbuf = [self.sb(f"rowbuf{i}", [1, TS], F32, scope) for i in range(2)]
        wsacc = self.sb("wsacc", [128, 4], F32, scope)
        wsred = self.sb("wsred", [128, 1], F32, scope)
        if DBG_WS:
            wsdbg = self.sb("wsdbg", [128, 24], F32, scope)
            self.tap("wsdbg", wsdbg, 128, 24)
        gtmp = self.sb("gtmp", [128, 64], F32, scope)
        wstr = [self.sb(f"p1w{i}", [128, 2048], F32, scope) for i in range(2)]
        wmf = [self.sb(f"p1m{i}", [128, 1536], F32, scope) for i in range(2)]
        wmb = [self.sb(f"p1mb{i}", [128, 1536], BF16, scope) for i in range(2)]

        cdone = self.dma("sync", "cst", siluc[:, :], d["c_t"].ap())
        self.WD("scalar", cdone)
        sa = E("scalar", lambda e: e.activation(siluc[:, :], siluc[:, :], AF.Silu),
               "sact")
        W("vector", "sact", sa)
        sg = E("vector", lambda e: e.tensor_copy(siluc_b[:, :], siluc[:, :]), "svec")

        slab_d, slab_v = [], []
        for t24 in range(24):
            m, i = divmod(t24, 8)
            src = (d["wgs"], d["wus"], d["wds"])[m]
            if t24 >= 2:
                W("sync", "svec", slab_v[t24 - 2])
            slab_d.append(dma("sync", f"sw{t24 % 2}", wstr[t24 % 2][:, :],
                              src.ap()[i, :, :]))
            self.WD("vector", slab_d[t24])
            rdv = E("vector", lambda e, b=t24 % 2: e.tensor_reduce(
                wsred[:, :], wstr[b][:, :], AX.X, ALU.add,
                apply_absolute_value=True), "svec")
            W("vector", "svec", rdv)
            if DBG_WS:
                E("vector", lambda e, t24=t24: e.tensor_copy(
                    wsdbg[:, t24:t24 + 1], wsred[:, :]))
            if i == 0:
                slab_v.append(E("vector", lambda e, m=m: e.tensor_copy(
                    wsacc[:, m:m + 1], wsred[:, :]), "svec"))
            else:
                slab_v.append(E("vector", lambda e, m=m: e.tensor_tensor(
                    wsacc[:, m:m + 1], wsacc[:, m:m + 1], wsred[:, :], ALU.add), "svec"))
        self.WD("tensor", self.CONSTS_D)
        W("tensor", "svec", slab_v[-1])
        wst = E("tensor", lambda e: e.matmul(self.ps_row[0:1, 0:4], self.sccols[:, 2:3],
                                             wsacc[:, :], start=True, stop=True), "stn")
        W("vector", "stn", wst)
        wsv = E("vector", lambda e: e.tensor_copy(self.wsrow[0:1, 0:4],
                                                  self.ps_row[0:1, 0:4]), "svec")

        W("tensor", "svec", sg)
        wm_d, wm_v, wm_t = [], [], []
        for k in range(16):
            if k >= 2:
                W("sync", "svec", wm_v[k - 2])
            wm_d.append(dma("sync", f"wm{k % 2}", wmf[k % 2][:, :],
                            d["wmod_sl"].ap()[k * 128:(k + 1) * 128, :]))
            self.WD("vector", wm_d[k])
            if k >= 2:
                W("vector", "stn", wm_t[k - 2])
            wm_v.append(E("vector", lambda e, b=k % 2: e.tensor_copy(
                wmb[b][:, :], wmf[b][:, :]), "svec"))
            W("tensor", "svec", wm_v[k])
            for n3 in range(3):
                v = E("tensor", lambda e, k=k, n3=n3: e.matmul(
                    ps[n3][0:8, :], siluc_b[:, k * 8:(k + 1) * 8],
                    wmb[k % 2][:, n3 * 512:(n3 + 1) * 512],
                    start=(k == 0), stop=(k == 15)), "stn")
            wm_t.append(v)
        W("vector", "stn", wm_t[-1])
        for n3 in range(3):
            mev = E("vector", lambda e, n3=n3: e.tensor_copy(
                modsl[0:8, n3 * 512:(n3 + 1) * 512], ps[n3][0:8, :]), "svec")

        W("gpsimd", "svec", mev)
        W("gpsimd", "svec", wsv)
        dma("gpsimd", "ccb", d["ag_in"].ap(), modsl[0:8, :])
        dma("gpsimd", "ccb", d["ar_in"].ap()[0:1, :], self.wsrow[0:1, 0:4])
        W("gpsimd", "ccb", self.val("ccb"))
        if NO_CC:
            for rr in range(8):
                self.dma("gpsimd", "ccf", d["ag_out"].ap()[rr * 8:rr * 8 + 8, :],
                         d["ag_in"].ap())
            self.dma("gpsimd", "ccf", d["ar_out"].ap(), d["ar_in"].ap())
        else:
            cch, cc2h = self.sems["cc"][0], self.sems["cc2"][0]
            E("gpsimd", lambda e: e.collective_compute(
                "AllGather", ALU.bypass, replica_groups=[list(range(NCORES))],
                ins=[d["ag_in"].ap().opt()],
                outs=[d["ag_out"].ap().opt()]).then_inc(cch), nofence=True)
            E("gpsimd", lambda e: e.collective_compute(
                "AllReduce", ALU.add, replica_groups=[list(range(NCORES))],
                ins=[d["ar_in"].ap().opt()],
                outs=[d["ar_out"].ap().opt()]).then_inc(cc2h), nofence=True)
            self.sems["cc"][1] = 1
            self.sems["cc2"][1] = 1
            W("gpsimd", "cc", 1)
            W("gpsimd", "cc2", 1)
        if NO_CC:
            W("gpsimd", "ccf", self.val("ccf"))
        ccd = dma("gpsimd", "ccb", self.wsrow[0:1, 4:8], d["ar_out"].ap()[0:1, :])

        self.WD("sync", ccd)
        ex_d, ex_t, ex_v, tp_t = [], [], [], []
        for n in range(24):
            if n >= 2:
                W("sync", "stn", ex_t[n - 2])
            r0, js = n // 3, (n % 3) * 512
            ex_d.append(dma("sync", f"ex{n % 2}", exsl[n % 2][0:8, :],
                            d["ag_out"].ap()[r0 * 8:r0 * 8 + 8, js:js + 512]))
            self.WD("tensor", ex_d[n])
            if n >= 2:
                W("tensor", "svec", ex_v[n - 2])
            ex_t.append(E("tensor", lambda e, n=n: e.matmul(
                ps[n % 2][0:1, :], self.sonehot[0:8, 0:1],
                exsl[n % 2][0:8, :], start=True, stop=True), "stn"))
            W("vector", "stn", ex_t[n])
            if n >= 2:
                W("vector", "stn", tp_t[n - 2])
            ex_v.append(E("vector", lambda e, n=n: e.tensor_copy(
                rowbuf[n % 2][0:1, :], ps[n % 2][0:1, :]), "svec"))
            W("tensor", "svec", ex_v[n])
            for j4 in range(4):
                tp = E("tensor", lambda e, n=n, j4=j4: e.transpose(
                    ps[5][:, n * 4 + j4:n * 4 + j4 + 1],
                    rowbuf[n % 2][0:1, j4 * 128:(j4 + 1) * 128],
                    self.sones1[0:1, 0:1]), "stn")
            tp_t.append(tp)
        W("vector", "stn", tp_t[-1])
        E("vector", lambda e: e.tensor_copy(self.mod96[:, :], ps[5][:, 0:96]))
        E("vector", lambda e: e.tensor_scalar(self.onePca[:, :], self.mod96[:, 16:32],
                                              1.0, None, ALU.add))
        E("vector", lambda e: e.tensor_scalar(self.onePcb[:, :], self.mod96[:, 64:80],
                                              1.0, None, ALU.add))
        for dst, off, lo in ((self.gacols, 32, 0), (self.gbcols, 80, 16)):
            E("vector", lambda e, off=off, lo=lo: e.tensor_scalar(
                gtmp[:, lo:lo + 16], self.mod96[:, off:off + 16], 0.0, None, ALU.is_ge))
            E("vector", lambda e, lo=lo: e.tensor_scalar(
                gtmp[:, lo:lo + 16], gtmp[:, lo:lo + 16], 2.0, -1.0, ALU.mult, ALU.add))
            E("vector", lambda e, off=off, lo=lo: e.tensor_scalar(
                gtmp[:, 32 + lo:48 + lo], self.mod96[:, off:off + 16], -1.0, None,
                ALU.mult))
            E("vector", lambda e, off=off, lo=lo: e.tensor_tensor(
                gtmp[:, 32 + lo:48 + lo], gtmp[:, 32 + lo:48 + lo],
                self.mod96[:, off:off + 16], ALU.max))
            E("vector", lambda e, lo=lo: e.tensor_scalar(
                gtmp[:, 32 + lo:48 + lo], gtmp[:, 32 + lo:48 + lo], 0.25, None, ALU.max))
            E("vector", lambda e, dst=dst, lo=lo: e.tensor_tensor(
                dst[:, :], gtmp[:, 32 + lo:48 + lo], gtmp[:, lo:lo + 16], ALU.mult))
        E("vector", lambda e: e.tensor_scalar(self.wsrow[0:1, 8:11], self.wsrow[0:1, 4:7],
                                              1.0 / NELW, EPS, ALU.mult, ALU.max))
        mv = E("vector", lambda e: e.reciprocal(self.wsrow[0:1, 12:15],
                                                self.wsrow[0:1, 8:11]), "svec")
        W("vector", "svec", mv)
        W("tensor", "svec", mv)
        for i in range(3):
            wb = E("tensor", lambda e, i=i: e.matmul(
                ps[4][:, i:i + 1], self.sones1[0:1, :], self.wsrow[0:1, 12 + i:13 + i],
                start=True, stop=True), "stn")
        W("vector", "stn", wb)
        E("vector", lambda e: e.tensor_copy(self.wsg_i[:, :], ps[4][:, 0:1]))
        E("vector", lambda e: e.tensor_copy(self.wsu_i[:, :], ps[4][:, 1:2]))
        E("vector", lambda e: e.tensor_copy(self.wsd_i[:, :], ps[4][:, 2:3]), "svec")
        self.tap("mod96", self.mod96, 128, 96)
        self.tap("wsrow", self.wsrow, 1, 16)
        self.tap("wsacc", wsacc, 128, 4)

    def ln_stats(self, src_dram, xbf, xsq):
        srow_a, srow_b, srow_c, srow_d = (self.srow_a, self.srow_b,
                                          self.srow_c, self.srow_d)
        xstr = self.xstr
        E, W, dma = self.E, self.W, self.dma
        ps = self.ps
        sq_ps = [ps[4][0:1, :], ps[5][0:1, :], self.ps_row[0:1, :], self.ps_row2[0:1, :]]
        cast_v, sq_a, mm_t = [], [], []
        for c in range(NK):
            if c >= 2:
                W("sync", "svec", cast_v[c - 2])
            xd = dma("sync", f"xs{c % 2}", xstr[c % 2][:, :],
                     src_dram.ap()[c * 128:(c + 1) * 128, :])
            self.WD("vector", xd)
            if c >= 2:
                W("vector", "stn", mm_t[c - 2])
            cast_v.append(E("vector", lambda e, b=c % 2: e.tensor_copy(
                xbf[b][:, :], xstr[b][:, :]), "svec"))
            W("scalar", "svec", cast_v[c])
            if c >= 2:
                W("scalar", "stn", mm_t[c - 2])
            sq_a.append(E("scalar", lambda e, b=c % 2: e.activation(
                xsq[b][:, :], xbf[b][:, :], AF.Square), "sact"))
            W("tensor", "sact", sq_a[c])
            for t4 in range(NTS):
                E("tensor", lambda e, c=c, t4=t4: e.matmul(
                    ps[t4][0:1, :], self.sonescol[:, :],
                    xbf[c % 2][:, t4 * TS:(t4 + 1) * TS],
                    start=(c == 0), stop=(c == NK - 1)))
                mm = E("tensor", lambda e, c=c, t4=t4, tgt=sq_ps[t4]: e.matmul(
                    tgt, self.sonescol[:, :], xsq[c % 2][:, t4 * TS:(t4 + 1) * TS],
                    start=(c == 0), stop=(c == NK - 1)), "stn")
            mm_t.append(mm)
        W("vector", "stn", mm_t[-1])
        for t4 in range(NTS):
            E("vector", lambda e, t4=t4: e.tensor_copy(
                srow_a[0:1, t4 * TS:(t4 + 1) * TS], ps[t4][0:1, :]))
            E("vector", lambda e, t4=t4, s=sq_ps[t4]: e.tensor_copy(
                srow_b[0:1, t4 * TS:(t4 + 1) * TS], s))
        E("vector", lambda e: e.tensor_scalar(srow_a[0:1, :], srow_a[0:1, :],
                                              1.0 / D, None, ALU.mult))
        E("vector", lambda e: e.tensor_tensor(srow_c[0:1, :], srow_a[0:1, :],
                                              srow_a[0:1, :], ALU.mult))
        rv = E("vector", lambda e: e.scalar_tensor_tensor(
            srow_b[0:1, :], srow_b[0:1, :], 1.0 / D, srow_c[0:1, :],
            ALU.mult, ALU.subtract), "svec")
        W("scalar", "svec", rv)
        ra = E("scalar", lambda e: e.activation(
            srow_d[0:1, :], srow_b[0:1, :], AF.Sqrt,
            bias=self.sccols[0:1, 1:2]), "sact")
        W("vector", "sact", ra)
        rv2 = E("vector", lambda e: e.reciprocal(srow_b[0:1, :],
                                                 srow_d[0:1, :]), "svec")
        W("vector", "svec", rv2)
        self.bcast_rows([(srow_a, self.bc1, T), (srow_b, self.bc2, T)], rv2)

    def bcast_rows(self, jobs, ready_v):
        E, W = self.E, self.W
        ps = self.ps
        W("tensor", "svec", ready_v)
        i = 0
        last_ev = None
        for row, dstt, width in jobs:
            for t4 in range(width // TS):
                bank = i % 4
                if last_ev is not None and i >= 4:
                    W("tensor", "svec", evs[i - 4])
                mm = E("tensor", lambda e, row=row, t4=t4, bank=bank: e.matmul(
                    ps[bank][:, :], self.sones1[0:1, :],
                    row[0:1, t4 * TS:(t4 + 1) * TS], start=True, stop=True), "stn")
                W("vector", "stn", mm)
                ev = E("vector", lambda e, dstt=dstt, t4=t4, bank=bank: e.tensor_copy(
                    dstt[:, t4 * TS:(t4 + 1) * TS], ps[bank][:, :]), "svec")
                if i == 0:
                    evs = {}
                evs[i] = ev
                last_ev = ev
                i += 1
        return last_ev

    def p4_h(self, src_dram, scale_tile, bias_off, dst_fn, xln):
        xstr = self.xstr
        E, W, dma = self.E, self.W, self.dma
        sub_v, act_a = [], []
        for c in range(NK):
            if c >= 2:
                W("sync", "svec", sub_v[c - 2])
            xd = dma("sync", f"xs{c % 2}", xstr[c % 2][:, :],
                     src_dram.ap()[c * 128:(c + 1) * 128, :])
            self.WD("vector", xd)
            if c >= 1:
                W("vector", "sact", act_a[c - 1])
            E("vector", lambda e, b=c % 2: e.tensor_tensor(
                xln[:, :], xstr[b][:, :], self.bc1[:, :], ALU.subtract))
            sub_v.append(E("vector", lambda e: e.tensor_tensor(
                xln[:, :], xln[:, :], self.bc2[:, :], ALU.mult), "svec"))
            W("scalar", "svec", sub_v[c])
            act_a.append(E("scalar", lambda e, c=c, st=scale_tile, bo=bias_off:
                           e.activation(dst_fn(c)[:, :], xln[:, :], AF.Identity,
                                        scale=st[:, c:c + 1],
                                        bias=self.mod96[:, bo + c:bo + c + 1]), "sact"))
        return act_a[-1]

    def p5_kv(self, scope):
        E, W, dma = self.E, self.W, self.dma
        d, ps = self.dram, self.ps
        self.kfm = [self.sb(f"kfm{i}", [128, T], BF16, scope) for i in range(8)]
        self.vaug = [self.sb(f"vaug{i}", [128, 520], BF16, scope) for i in range(16)]
        self.qpair = self.sb("qpair", [128, T], BF16, scope)
        self.qraw = self.sb("qraw", [128, TS], BF16, scope)
        wv_all = self.sb("wv_all", [128, 8192], BF16, scope)
        sc5 = contextlib.ExitStack()
        rope_ck = self.bc1
        rope_sk = self.bc2
        wstr = [self.sb("p5w0", [128, 2048], F32, sc5)]
        wstr.append(wstr[0])
        wstrb = [self.sb("p5wb0", [128, 2048], BF16, sc5)]
        wstrb.append(wstrb[0])

        ckd = dma("sync", "cst", rope_ck[:, :], d["cosk"].ap())
        skd = dma("sync", "cst", rope_sk[:, :], d["sink"].ap())

        wk_d, wk_g, mm_last = [], [], []
        rope_done = []
        for o in range(8):
            if o >= 1:
                W("sync", "sact", wk_g[o - 1])
            wk_d.append(dma("sync", "wk", wstr[o % 2][:, :], d["wk_b"].ap()[o, :, :]))
            self.WD("scalar", wk_d[o])
            if o >= 1:
                W("scalar", "stn", mm_last[o - 1])
            wk_g.append(E("scalar", lambda e, b=o % 2: e.copy(
                wstrb[b][:, :], wstr[b][:, :]), "sact"))
            W("tensor", "sact", wk_g[o])
            if o == 0:
                self.WD("tensor", skd)
                self.WD("vector", skd)
                swv_prev = None
            for t4 in range(4):
                if t4 >= 2 or o > 0:
                    W("tensor", "svec", rope_done[-2] if len(rope_done) >= 2 else None)
                for k in range(16):
                    mmv = E("tensor", lambda e, o=o, t4=t4, k=k: e.matmul(
                        ps[t4 % 2][:, :], wstrb[o % 2][:, k * 128:(k + 1) * 128],
                        self.hA(k)[:, t4 * TS:(t4 + 1) * TS],
                        start=(k == 0), stop=(k == 15)), "stn")
                W("scalar", "stn", mmv)
                if swv_prev is not None:
                    W("scalar", "stn", swv_prev)
                arv = E("scalar", lambda e, t4=t4: e.activation(
                    self.qraw[:, :], ps[t4 % 2][:, :], AF.Copy), "sact")
                W("tensor", "sact", arv)
                swv_prev = E("tensor", lambda e, t4=t4: e.matmul(
                    ps[2 + t4 % 2][:, :], self.spswap[:, :], self.qraw[:, :],
                    start=True, stop=True), "stn")
                W("vector", "stn", swv_prev)
                E("vector", lambda e, t4=t4: e.tensor_tensor(
                    self.tmp1[:, 0:TS], ps[t4 % 2][:, :],
                    rope_ck[:, t4 * TS:(t4 + 1) * TS], ALU.mult))
                E("vector", lambda e, t4=t4: e.tensor_tensor(
                    self.tmp2[:, 0:TS], ps[2 + t4 % 2][:, :],
                    rope_sk[:, t4 * TS:(t4 + 1) * TS], ALU.mult))
                rope_done.append(E("vector", lambda e, o=o, t4=t4: e.tensor_tensor(
                    self.kfm[o][:, t4 * TS:(t4 + 1) * TS], self.tmp1[:, 0:TS],
                    self.tmp2[:, 0:TS], ALU.add), "svec"))
            mm_last.append(self.val("stn"))
        self.K_DONE = rope_done[-1]

        for i in range(16):
            for h8 in range(8):
                E("gpsimd", lambda e, i=i, h8=h8: e.memset(
                    self.vaug[i][:, h8 * 65 + 64:h8 * 65 + 65], 1.0))
        vones = E("gpsimd", lambda e, i=15: e.tensor_copy(
            self.vaug[15][:, 519:520], self.vaug[15][:, 519:520]), "sgps")
        W("sync", "sact", wk_g[-1])
        for qv4 in range(4):
            if qv4 >= 1:
                W("sync", "sact", wvg)
            wvd = dma("sync", "wv", wstr[0].ap().rearrange("p (a v) -> p a v", a=4),
                      d["wv_t"].ap()[qv4 * 512:(qv4 + 1) * 512, :]
                      .rearrange("(a p) v -> p a v", p=128))
            self.WD("scalar", wvd)
            wvg = E("scalar", lambda e, qv4=qv4: e.copy(
                wv_all[:, qv4 * 2048:(qv4 + 1) * 2048], wstr[0][:, :]), "sact")
        W("tensor", "sact", wvg)
        v_ev = []
        for i in range(16):
            bank = 4 + i % 2
            if i >= 2:
                W("tensor", "svec", v_ev[i - 2])
            for k in range(16):
                vmm = E("tensor", lambda e, i=i, k=k, bank=bank: e.matmul(
                    ps[bank][:, :], self.hA(k)[:, i * 128:(i + 1) * 128],
                    wv_all[:, k * 512:(k + 1) * 512],
                    start=(k == 0), stop=(k == 15)), "stn")
            W("vector", "stn", vmm)
            if i == 0:
                W("vector", "sgps", vones)
            for h8 in range(8):
                vv = E("vector", lambda e, i=i, h8=h8, bank=bank: e.tensor_copy(
                    self.vaug[i][:, h8 * 65:h8 * 65 + 64],
                    ps[bank][:, h8 * 64:(h8 + 1) * 64]), "svec")
            v_ev.append(vv)
        self.V_DONE = v_ev[-1]
        self.barrier()
        sc5.close()

    def p6_attention(self, scope):
        E, W, dma = self.E, self.W, self.dma
        d, ps = self.dram, self.ps
        ptile = [self.sb(f"ptile{i}", [128, TS], BF16, scope) for i in range(4)]
        opevA = self.sb("opevA", [64, T], BF16, scope)
        opevB = self.sb("opevB", [64, T], BF16, scope)
        recrow = self.sb("recrow", [1, TS], F32, scope)
        recrow2 = self.sb("recrow2", [1, TS], F32, scope)
        bcsbA = self.sb("bcsbA", [64, TS], F32, scope)
        bcsbB = self.sb("bcsbB", [64, TS], F32, scope)
        rope_c = self.bc1
        rope_s = self.bc2
        self.smasks = self.sb("smasks", [128, 4 * TS], BF16, scope)
        wstr = [self.sb("p6w0", [128, 2048], F32, scope)]
        wstr.append(wstr[0])
        wstrb = [self.sb("p6wb0", [128, 2048], BF16, scope)]
        wstrb.append(wstrb[0])

        W("sync", "svec", self.K_DONE)
        dma("sync", "cst", self.smasks[:, :], d["masks"].ap())
        cqd = dma("sync", "cst", rope_c[:, :], d["cosq"].ap())
        sqd = dma("sync", "cst", rope_s[:, :], d["sinq"].ap())
        self.WD("tensor", sqd)
        self.WD("vector", sqd)
        W("tensor", "svec", self.K_DONE)
        W("tensor", "svec", self.V_DONE)

        wq_g, q_mm_done, op_dma = [], [], []
        prev_o_last_smm = None
        prev_ov = None
        swv_prev = None
        prev_pv_last = None
        for o in range(NPAIR):
            if o >= 1:
                W("sync", "svec", wq_g[o - 1])
            wqd = dma("sync", "wq", wstr[o % 2][:, :], d["wq_b"].ap()[o, :, :])
            self.WD("vector", wqd)
            if o >= 1:
                W("vector", "stn", q_mm_done[o - 1])
            wq_g.append(E("vector", lambda e, b=o % 2: e.tensor_copy(
                wstrb[b][:, :], wstr[b][:, :]), "svec"))
            W("tensor", "svec", wq_g[o])
            rope_v = []
            for t4 in range(4):
                if prev_ov is not None and t4 == 0:
                    W("tensor", "svec", prev_ov)
                if t4 >= 2:
                    W("tensor", "svec", rope_v[t4 - 2])
                for k in range(16):
                    mmv = E("tensor", lambda e, o=o, t4=t4, k=k: e.matmul(
                        ps[t4 % 2][:, :], wstrb[o % 2][:, k * 128:(k + 1) * 128],
                        self.hA(k)[:, t4 * TS:(t4 + 1) * TS],
                        start=(k == 0), stop=(k == 15)), "stn")
                W("scalar", "stn", mmv)
                if swv_prev is not None:
                    W("scalar", "stn", swv_prev)
                arv = E("scalar", lambda e, t4=t4: e.activation(
                    self.qraw[:, :], ps[t4 % 2][:, :], AF.Copy), "sact")
                W("tensor", "sact", arv)
                swv_prev = E("tensor", lambda e, t4=t4: e.matmul(
                    ps[2 + t4 % 2][:, :], self.spswap[:, :], self.qraw[:, :],
                    start=True, stop=True), "stn")
                W("vector", "stn", swv_prev)
                if prev_o_last_smm is not None and t4 == 0:
                    W("vector", "stn", prev_o_last_smm)
                E("vector", lambda e, t4=t4: e.tensor_tensor(
                    self.tmp1[:, 0:TS], ps[t4 % 2][:, :],
                    rope_c[:, t4 * TS:(t4 + 1) * TS], ALU.mult))
                E("vector", lambda e, t4=t4: e.tensor_tensor(
                    self.tmp2[:, 0:TS], ps[2 + t4 % 2][:, :],
                    rope_s[:, t4 * TS:(t4 + 1) * TS], ALU.mult))
                rope_v.append(E("vector", lambda e, t4=t4: e.tensor_tensor(
                    self.qpair[:, t4 * TS:(t4 + 1) * TS], self.tmp1[:, 0:TS],
                    self.tmp2[:, 0:TS], ALU.add), "svec"))
            q_mm_done.append(self.val("stn"))
            kvch = o // 2
            vgrp = o // 2
            W("tensor", "svec", rope_v[-1])
            ov = None
            for qc in range(NTS):
                njt = 4 * (qc + 1)
                exp_a, pv_t, mk_v = {}, {}, {}
                if ov is not None:
                    W("tensor", "svec", ov)
                if prev_pv_last is not None:
                    W("scalar", "stn", prev_pv_last)
                def emit_pv(loc, j):
                    pti = 2 * (j % 2) + loc
                    if (loc, j) in mk_v:
                        W("tensor", "svec", mk_v[(loc, j)])
                    else:
                        W("tensor", "sact", exp_a[(loc, j)])
                    pv = E("tensor", lambda e, j=j, loc=loc, pti=pti, vgrp=vgrp,
                           njt=njt: e.matmul(
                               ps[loc][0:65, :],
                               self.vaug[j][:, vgrp * 65:vgrp * 65 + 65],
                               ptile[pti][:, :],
                               start=(j == 0), stop=(j == njt - 1)), "stn")
                    pv_t[(loc, j)] = pv

                for loc in range(2):
                    for j in range(njt):
                        sbank = (2 if j % 2 == 0 else 4) + loc
                        pti = 2 * (j % 2) + loc
                        if (loc, j - 2) in exp_a:
                            W("tensor", "sact", exp_a[(loc, j - 2)])
                        smm = E("tensor", lambda e, j=j, qc=qc, loc=loc, sbank=sbank,
                                kvch=kvch: e.matmul(
                                    ps[sbank][:, :],
                                    self.kfm[kvch][loc * 64:loc * 64 + 64,
                                                   j * 128:(j + 1) * 128],
                                    self.qpair[loc * 64:loc * 64 + 64,
                                               qc * TS:(qc + 1) * TS],
                                    start=True, stop=True), "stn")
                        W("scalar", "stn", smm)
                        if (loc, j - 2) in pv_t:
                            W("scalar", "stn", pv_t[(loc, j - 2)])
                        if (loc, j - 2) in mk_v:
                            W("scalar", "svec", mk_v[(loc, j - 2)])
                        ea = E("scalar", lambda e, sbank=sbank, pti=pti: e.activation(
                            ptile[pti][:, :], ps[sbank][:, :], AF.Exp,
                            bias=self.sccols[:, 0:1]), "sact")
                        exp_a[(loc, j)] = ea
                        dmask = j - 4 * qc
                        if dmask >= 0:
                            W("vector", "sact", ea)
                            mv = E("vector", lambda e, pti=pti, dm=dmask: e.tensor_tensor(
                                ptile[pti][:, :], ptile[pti][:, :],
                                self.smasks[:, dm * TS:(dm + 1) * TS], ALU.mult), "svec")
                            mk_v[(loc, j)] = mv
                        if j >= 1:
                            emit_pv(loc, j - 1)
                    emit_pv(loc, njt - 1)
                prev_o_last_smm = self.val("stn")
                prev_pv_last = pv_t[(1, njt - 1)]
                W("vector", "stn", pv_t[(1, njt - 1)])
                E("vector", lambda e: e.reciprocal(recrow[0:1, :], ps[0][64:65, :]))
                rv = E("vector", lambda e: e.reciprocal(recrow2[0:1, :],
                                                        ps[1][64:65, :]), "svec")
                W("tensor", "svec", rv)
                E("tensor", lambda e: e.matmul(
                    ps[2][0:64, :], self.sones1[0:1, 0:64], recrow[0:1, :],
                    start=True, stop=True), "stn")
                b2 = E("tensor", lambda e: e.matmul(
                    ps[3][0:64, :], self.sones1[0:1, 0:64], recrow2[0:1, :],
                    start=True, stop=True), "stn")
                W("vector", "stn", b2)
                if op_dma:
                    self.WD("vector", op_dma[-1])
                E("vector", lambda e: e.tensor_copy(bcsbA[:, :], ps[2][0:64, :]))
                E("vector", lambda e: e.tensor_copy(bcsbB[:, :], ps[3][0:64, :]))
                E("vector", lambda e, qc=qc: e.tensor_tensor(
                    opevA[:, qc * TS:(qc + 1) * TS], ps[0][0:64, :], bcsbA[:, :],
                    ALU.mult))
                ov = E("vector", lambda e, qc=qc: e.tensor_tensor(
                    opevB[:, qc * TS:(qc + 1) * TS], ps[1][0:64, :], bcsbB[:, :],
                    ALU.mult), "svec")
            prev_ov = ov
            W("gpsimd", "svec", ov)
            dma("gpsimd", "opo", d["opair_d"].ap()[o, 0:64, :], opevA[:, :])
            op_dma.append(dma("gpsimd", "opo", d["opair_d"].ap()[o, 64:128, :],
                              opevB[:, :]))
        self.ATT_DMA = op_dma[-1]
        self.tap("qpair", self.qpair, 128, 2048)
        self.tap("pt3", ptile[3], 128, 512)
        self.tap("rec2", recrow2, 1, 512)
        self.tap("bcsbB", bcsbB, 64, 512)
        self.tap("opevA", opevA, 64, 2048)
        self.tap("opevB", opevB, 64, 2048)

    def p7_outproj(self, scope):
        xstr = self.xstr
        E, W, dma = self.E, self.W, self.dma
        d, ps = self.dram, self.ps
        wstr = [self.sb(f"p7w{i}", [128, 2048], F32, scope) for i in range(2)]
        wstrb = [self.sb(f"p7wb{i}", [128, 2048], BF16, scope) for i in range(2)]
        xnout = [self.sb(f"xnout{i}", [128, T], F32, scope) for i in range(2)]
        xln = self.sb("p7xln", [128, T], F32, scope)

        self.WD("sync", self.ATT_DMA)
        o_in = []
        for i in range(16):
            o_in.append(dma("sync", "oin", self.hA(i)[:, :], d["opair_d"].ap()[i, :, :]))
        wo_g, xp_dma = [], []
        mm_done = []
        ep_a, ep_v = {}, {}
        for o2 in range(16):
            if o2 >= 2:
                W("sync", "sact", wo_g[o2 - 2])
            wod = dma("sync", f"wo{o2 % 2}", wstr[o2 % 2][:, :], d["wo_b"].ap()[o2, :, :])
            if o2 >= 2:
                W("sync", "svec", ep_v[(o2 - 2, 3)])
            xrd = dma("sync", f"xs{o2 % 2}", xstr[o2 % 2][:, :],
                      d["x_fm"].ap()[o2 * 128:(o2 + 1) * 128, :])
            self.WD("scalar", wod)
            if o2 >= 2:
                W("scalar", "stn", mm_done[o2 - 2])
            wo_g.append(E("scalar", lambda e, b=o2 % 2: e.copy(
                wstrb[b][:, :], wstr[b][:, :]), "sact"))
            W("tensor", "sact", wo_g[o2])
            if o2 == 0:
                self.WD("tensor", o_in[-1])
            for t4 in range(4):
                if (o2, t4 - 2) in ep_a:
                    W("tensor", "sact", ep_a[(o2, t4 - 2)])
                elif o2 >= 1:
                    W("tensor", "sact", ep_a[(o2 - 1, t4 + 2)])
                for k in range(16):
                    mmv = E("tensor", lambda e, o2=o2, t4=t4, k=k: e.matmul(
                        ps[t4 % 2][:, :], wstrb[o2 % 2][:, k * 128:(k + 1) * 128],
                        self.hA(k)[:, t4 * TS:(t4 + 1) * TS],
                        start=(k == 0), stop=(k == 15)), "stn")
                W("scalar", "stn", mmv)
                if (o2 - 1, t4) in ep_v:
                    W("scalar", "svec", ep_v[(o2 - 1, t4)])
                av = E("scalar", lambda e, o2=o2, t4=t4: e.activation(
                    xln[:, t4 * TS:(t4 + 1) * TS], ps[t4 % 2][:, :],
                    AF.Identity, scale=self.gacols[:, o2:o2 + 1]), "sact")
                ep_a[(o2, t4)] = av
                W("vector", "sact", av)
                if t4 == 0:
                    self.WD("vector", xrd)
                    if o2 >= 2:
                        self.WD("vector", xp_dma[o2 - 2])
                ep_v[(o2, t4)] = E("vector", lambda e, o2=o2, t4=t4: e.tensor_tensor(
                    xnout[o2 % 2][:, t4 * TS:(t4 + 1) * TS],
                    xln[:, t4 * TS:(t4 + 1) * TS],
                    xstr[o2 % 2][:, t4 * TS:(t4 + 1) * TS], ALU.add), "svec")
            mm_done.append(self.val("stn"))
            W("sync", "svec", ep_v[(o2, 3)])
            xp_dma.append(dma("sync", f"xp{o2 % 2}",
                              d["xprime"].ap()[o2 * 128:(o2 + 1) * 128, :],
                              xnout[o2 % 2][:, :]))
        self.XP_DMA = xp_dma[-1]
        self.tap("xn15", xnout[1], 128, 2048)
        self.tap("gacols", self.gacols, 128, 16)

    def p9_h2_fwht(self, scope):
        srow_a, srow_b, srow_c, srow_d = (self.srow_a, self.srow_b,
                                          self.srow_c, self.srow_d)
        xstr = self.xstr
        E, W, dma = self.E, self.W, self.dma
        d, ps = self.dram, self.ps
        xbf = [self.sb(f"p9b{i}", [128, 2048], BF16, scope) for i in range(2)]
        u1ev = [self.sb(f"u1ev{i}", [128, TS], BF16, scope) for i in range(2)]
        xln = self.sb("p9xln", [128, T], F32, scope)
        mxp = self.sb("mxp9", [128, T], F32, scope)
        mnp = self.sb("mnp9", [128, T], F32, scope)
        zq = self.sb("zq9", [128, TS], F32, scope)
        srow_sg = self.sb("srow_sg", [1, T], F32, scope)
        self.srow_sg = srow_sg

        sub_v, act_a, st1_t, u1_dma, uev_a = [], [], {}, [], []
        for c in range(NK):
            if c >= 2:
                W("sync", "svec", sub_v[c - 2])
            xd = dma("sync", f"xs{c % 2}", xstr[c % 2][:, :],
                     d["xprime"].ap()[c * 128:(c + 1) * 128, :])
            self.WD("vector", xd)
            if c >= 1:
                W("vector", "sact", act_a[c - 1])
            E("vector", lambda e, b=c % 2: e.tensor_tensor(
                xln[:, :], xstr[b][:, :], self.bc1[:, :], ALU.subtract))
            sub_v.append(E("vector", lambda e: e.tensor_tensor(
                xln[:, :], xln[:, :], self.bc2[:, :], ALU.mult), "svec"))
            W("scalar", "svec", sub_v[c])
            if c >= 2:
                W("scalar", "stn", st1_t[(c - 2, 3)])
            act_a.append(E("scalar", lambda e, c=c: e.activation(
                xbf[c % 2][:, :], xln[:, :], AF.Identity,
                scale=self.onePcb[:, c:c + 1],
                bias=self.mod96[:, 48 + c:49 + c]), "sact"))
            W("tensor", "sact", act_a[c])
            for t4 in range(4):
                it = c * 4 + t4
                if it >= 2:
                    W("tensor", "sact", uev_a[it - 2])
                stv = E("tensor", lambda e, c=c, t4=t4: e.matmul(
                    ps[t4 % 2][:, :], self.sh128[:, :],
                    xbf[c % 2][:, t4 * TS:(t4 + 1) * TS], start=True, stop=True), "stn")
                st1_t[(c, t4)] = stv
                W("scalar", "stn", stv)
                if it >= 2:
                    self.WD("scalar", u1_dma[it - 2])
                ua = E("scalar", lambda e, t4=t4: e.activation(
                    u1ev[t4 % 2][:, :], ps[t4 % 2][:, :], AF.Copy), "sact")
                uev_a.append(ua)
                W("gpsimd", "sact", ua)
                u1_dma.append(dma("gpsimd", f"ub{t4 % 2}",
                                  d["u1_d"].ap()[:, c, t4 * TS:(t4 + 1) * TS],
                                  u1ev[t4 % 2][:, :]))
        W("gpsimd", "ub0", self.val("ub0"))
        W("gpsimd", "ub1", self.val("ub1"))
        v1_in = []
        for g in range(16):
            v1_in.append(dma("gpsimd", "vin", self.hA(g)[:, :],
                             d["u1_d"].ap()[8 * g:8 * g + 8, :, :]
                             .rearrange("e c t -> (e c) t")))
        self.WD("tensor", v1_in[-1])
        mx_v = {}
        for g in range(16):
            for t4 in range(4):
                it = g * 4 + t4
                bank = 2 + it % 2
                if it >= 2:
                    W("tensor", "svec", mx_v[it - 2])
                pa = E("tensor", lambda e, g=g, t4=t4, bank=bank: e.matmul(
                    ps[bank][:, :], self.si8h16[:, :],
                    self.hA(g)[:, t4 * TS:(t4 + 1) * TS], start=True, stop=True), "stn")
                W("vector", "stn", pa)
                if g == 0:
                    E("vector", lambda e, t4=t4, bank=bank: e.tensor_copy(
                        mxp[:, t4 * TS:(t4 + 1) * TS], ps[bank][:, :]))
                    v = E("vector", lambda e, t4=t4, bank=bank: e.tensor_copy(
                        mnp[:, t4 * TS:(t4 + 1) * TS], ps[bank][:, :]), "svec")
                else:
                    E("vector", lambda e, t4=t4, bank=bank: e.tensor_tensor(
                        mxp[:, t4 * TS:(t4 + 1) * TS], mxp[:, t4 * TS:(t4 + 1) * TS],
                        ps[bank][:, :], ALU.max))
                    v = E("vector", lambda e, t4=t4, bank=bank: e.tensor_tensor(
                        mnp[:, t4 * TS:(t4 + 1) * TS], mnp[:, t4 * TS:(t4 + 1) * TS],
                        ps[bank][:, :], ALU.min), "svec")
                mx_v[it] = v
        E("vector", lambda e: e.tensor_scalar(mnp[:, :], mnp[:, :], -1.0, None, ALU.mult))
        mab = E("vector", lambda e: e.tensor_tensor(mxp[:, :], mxp[:, :], mnp[:, :],
                                                    ALU.max), "svec")
        mrow = self.xpart_max(mxp, T, mab, srow_c)
        E("vector", lambda e: e.tensor_scalar(srow_d[0:1, :], srow_c[0:1, :],
                                              C2048, EPS, ALU.mult, ALU.max))
        rcv = E("vector", lambda e: e.reciprocal(srow_a[0:1, :], srow_d[0:1, :]),
                "svec")
        W("vector", "svec", rcv)
        E("vector", lambda e: e.tensor_scalar(srow_a[0:1, :], srow_a[0:1, :],
                                              127.0 * C2048, None, ALU.mult))
        E("vector", lambda e: e.tensor_scalar(
            srow_sg[0:1, :], srow_d[0:1, :], self.wsrow[0:1, 8:9],
            1.0 / 127.0, ALU.mult, ALU.mult))
        srv = E("vector", lambda e: e.tensor_scalar(
            self.srow_su[0:1, :], srow_d[0:1, :], self.wsrow[0:1, 9:10],
            1.0 / 127.0, ALU.mult, ALU.mult), "svec")
        bcv = self.bcast_rows([(srow_a, self.bc2, T), (srow_sg, self.bc1, T)],
                              srv)
        W("tensor", "svec", bcv)
        rq = {}
        for g in range(16):
            for t4 in range(4):
                it = g * 4 + t4
                bank = 2 + it % 2
                if it >= 2:
                    W("tensor", "svec", rq[it - 2])
                pb = E("tensor", lambda e, g=g, t4=t4, bank=bank: e.matmul(
                    ps[bank][:, :], self.si8h16[:, :],
                    self.hA(g)[:, t4 * TS:(t4 + 1) * TS], start=True, stop=True), "stn")
                W("vector", "stn", pb)
                E("vector", lambda e, t4=t4, bank=bank: e.tensor_tensor(
                    zq[:, :], ps[bank][:, :], self.bc2[:, t4 * TS:(t4 + 1) * TS],
                    ALU.mult))
                rq[it] = E("vector", lambda e, g=g, t4=t4: e.tensor_scalar(
                    self.hA(g)[:, t4 * TS:(t4 + 1) * TS], zq[:, :],
                    MAGIC, -MAGIC, ALU.add, ALU.add), "svec")
        self.R_DONE = rq[63]
        self.tap("R0", self.hA(0), 128, 2048)

    def xpart_max(self, mtile, width, ready_v, out_row):
        E, W = self.E, self.W
        ps = self.ps
        W("tensor", "svec", ready_v)
        ns = width // 128
        tc = []
        for s in range(ns):
            bank = 2 + s % 2
            if s >= 2:
                W("tensor", "svec", tc[s - 2])
            tv = E("tensor", lambda e, s=s, bank=bank: e.transpose(
                ps[bank][:, 0:128], mtile[:, s * 128:(s + 1) * 128],
                self.sid_f[:, :]), "stn")
            W("vector", "stn", tv)
            tc.append(E("vector", lambda e, s=s, bank=bank: e.tensor_reduce(
                self.mcol[:, s:s + 1], ps[bank][:, 0:128], AX.X, ALU.max), "svec"))
        W("tensor", "svec", tc[-1])
        rv = []
        for s in range(ns):
            pr = self.ps_row if s % 2 == 0 else self.ps_row2
            if s >= 2:
                W("tensor", "svec", rv[s - 2])
            tv = E("tensor", lambda e, s=s, pr=pr: e.transpose(
                pr[0:1, 0:128], self.mcol[:, s:s + 1], self.sid_f[:, :]), "stn")
            W("vector", "stn", tv)
            rv.append(E("vector", lambda e, s=s, pr=pr, out_row=out_row: e.tensor_copy(
                out_row[0:1, s * 128:(s + 1) * 128], pr[0:1, 0:128]), "svec"))
        return rv[-1]

    def p10_tdq(self, scope):
        E, W, dma = self.E, self.W, self.dma
        d = self.dram
        wstr = [self.sb(f"pAw{i}", [128, 2048], F32, scope) for i in range(2)]
        wstrb = [self.sb(f"pAwb{i}", [128, 2048], BF16, scope) for i in range(2)]
        qt = self.sb("qt10", [128, 2048], F32, scope)
        tq, tdma = [], []
        for i in range(64):
            o2, kq = divmod(i, 4)
            if i >= 2:
                W("sync", "svec", tq[i - 2])
            dd = dma("sync", f"tf{i % 2}", wstr[i % 2][:, :],
                     d["td_b"].ap()[o2, :, kq * 2048:(kq + 1) * 2048])
            self.WD("vector", dd)
            if i >= 2:
                self.WD("vector", tdma[i - 2])
            E("vector", lambda e, b=i % 2: e.tensor_scalar(
                qt[:, :], wstr[b][:, :], self.wsd_i[:, :], None, ALU.mult))
            E("vector", lambda e: e.tensor_scalar(
                qt[:, :], qt[:, :], 1.0, -1.0, ALU.min, ALU.max))
            tq.append(E("vector", lambda e, b=i % 2: e.tensor_scalar(
                wstrb[b][:, :], qt[:, :], MAGIC, -MAGIC, ALU.add, ALU.add), "svec"))
            W("gpsimd", "svec", tq[-1])
            tdma.append(dma("gpsimd", f"tq{i % 2}",
                            d["tdq_d"].ap()[o2, :, kq * 2048:(kq + 1) * 2048],
                            wstrb[i % 2][:, :]))
        self.TDQ_TOT = [("tq0", self.val("tq0")), ("tq1", self.val("tq1"))]

    def p11_gateup(self, scope):
        E, W, dma = self.E, self.W, self.dma
        d, ps = self.dram, self.ps
        wstr = [self.sb(f"pBg{i}", [128, 2048], F32, scope) for i in range(2)]
        wstrb = [self.sb(f"pBgb{i}", [128, 2048], BF16, scope) for i in range(2)]
        wstr2 = [self.sb(f"pBu{i}", [128, 2048], F32, scope) for i in range(2)]
        wstr2b = [self.sb(f"pBub{i}", [128, 2048], BF16, scope) for i in range(2)]
        qt = self.sb("qt11", [128, 4096], F32, scope)
        zt1 = [self.sb(f"zt1_{i}", [128, TS], F32, scope) for i in range(2)]
        zt2 = [self.sb(f"zt2_{i}", [128, TS], F32, scope) for i in range(2)]
        zbf = [self.sb(f"zbf{i}", [128, TS], BF16, scope) for i in range(2)]
        u2ev = [self.sb(f"u2ev{i}", [128, TS], BF16, scope) for i in range(2)]

        gq, gq_a, u2_dma = [], [], []
        z_v, u_a, s1_t = {}, {}, {}
        mm_done = []
        for o in range(64):
            if o >= 2:
                W("sync", "sact", gq_a[o - 2])
            d1 = dma("sync", f"tg{o % 2}", wstr[o % 2][:, :], d["tg_b"].ap()[o, :, :])
            d2 = dma("sync", f"tu{o % 2}", wstr2[o % 2][:, :], d["tu_b"].ap()[o, :, :])
            self.WD("scalar", d1)
            self.WD("scalar", d2)
            if o >= 2:
                W("scalar", "stn", mm_done[o - 2])
            for qx, (src, dst, wsc) in enumerate(
                    ((wstr[o % 2], wstrb[o % 2], self.wsg_i),
                     (wstr2[o % 2], wstr2b[o % 2], self.wsu_i))):
                av = E("scalar", lambda e, src=src, wsc=wsc, qx=qx: e.activation(
                    qt[:, qx * 2048:(qx + 1) * 2048], src[:, :], AF.Identity,
                    scale=wsc[:, :]), "sact")
                W("vector", "sact", av)
                E("vector", lambda e, qx=qx: e.tensor_scalar(
                    qt[:, qx * 2048:(qx + 1) * 2048], qt[:, qx * 2048:(qx + 1) * 2048],
                    1.0, -1.0, ALU.min, ALU.max))
                qv = E("vector", lambda e, dst=dst, qx=qx: e.tensor_scalar(
                    dst[:, :], qt[:, qx * 2048:(qx + 1) * 2048],
                    MAGIC, -MAGIC, ALU.add, ALU.add), "svec")
            gq.append(qv)
            gq_a.append(av)
            W("tensor", "svec", qv)
            def emit_s1(pit):
                po, pt4 = divmod(pit, 4)
                if pit >= 2:
                    W("tensor", "sact", u_a[pit - 2])
                W("tensor", "svec", z_v[pit])
                s1_t[pit] = E("tensor", lambda e, pt4=pt4: e.matmul(
                    ps[4 + pit % 2][:, :], self.sh128[:, :], zbf[pt4 % 2][:, :],
                    start=True, stop=True), "stn")
                W("scalar", "stn", s1_t[pit])
                if pit >= 2:
                    self.WD("scalar", u2_dma[pit - 2])
                u_a[pit] = E("scalar", lambda e, pt4=pt4: e.activation(
                    u2ev[pt4 % 2][:, :], ps[4 + pit % 2][:, :], AF.Copy), "sact")
                W("gpsimd", "sact", u_a[pit])
                u2_dma.append(dma("gpsimd", f"ub{pt4 % 2}",
                                  d["u2_d"].ap()[:, po, pt4 * TS:(pt4 + 1) * TS],
                                  u2ev[pt4 % 2][:, :]))

            for t4 in range(4):
                it = o * 4 + t4
                if it >= 2:
                    W("tensor", "svec", z_v[it - 2])
                for k in range(16):
                    E("tensor", lambda e, o=o, t4=t4, k=k: e.matmul(
                        ps[t4 % 2][:, :], wstrb[o % 2][:, k * 128:(k + 1) * 128],
                        self.hA(k)[:, t4 * TS:(t4 + 1) * TS],
                        start=(k == 0), stop=(k == 15)), "stn")
                for k in range(16):
                    gu = E("tensor", lambda e, o=o, t4=t4, k=k: e.matmul(
                        ps[2 + t4 % 2][:, :], wstr2b[o % 2][:, k * 128:(k + 1) * 128],
                        self.hA(k)[:, t4 * TS:(t4 + 1) * TS],
                        start=(k == 0), stop=(k == 15)), "stn")
                W("vector", "stn", gu)
                if it >= 2:
                    W("vector", "sact", z_a_prev[it - 2])
                E("vector", lambda e, t4=t4: e.tensor_tensor(
                    zt1[t4 % 2][:, :], ps[t4 % 2][:, :],
                    self.bc1[:, t4 * TS:(t4 + 1) * TS], ALU.mult), "svec")
                ztv = self.val("svec")
                W("scalar", "svec", ztv)
                if it >= 2:
                    W("scalar", "svec", z_v[it - 2])
                za = E("scalar", lambda e, t4=t4: e.activation(
                    zt2[t4 % 2][:, :], zt1[t4 % 2][:, :], AF.Silu), "sact")
                if it == 0:
                    z_a_prev = {}
                z_a_prev[it] = za
                W("vector", "sact", za)
                if it >= 2:
                    W("vector", "stn", s1_t.get(it - 2, None) or s1_t[it - 2])
                z_v[it] = E("vector", lambda e, t4=t4: e.tensor_tensor(
                    zbf[t4 % 2][:, :], zt2[t4 % 2][:, :], ps[2 + t4 % 2][:, :],
                    ALU.mult), "svec")
                if it >= 1:
                    emit_s1(it - 1)
            mm_done.append(self.val("stn"))
        emit_s1(255)
        self.GU_TOT = [("ub0", self.val("ub0")), ("ub1", self.val("ub1"))]

    def p12_down(self, scope):
        srow_a, srow_b, srow_c, srow_d = (self.srow_a, self.srow_b,
                                          self.srow_c, self.srow_d)
        xstr = self.xstr
        E, W, dma = self.E, self.W, self.dma
        d, ps = self.dram, self.ps
        tds = [self.sb(f"tds{i}", [128, 8192], BF16, scope) for i in range(2)]
        outsb = [self.sb(f"outsb{i}", [128, TS], F32, scope) for i in range(2)]
        mxp = self.sb("mxp12", [128, TS], F32, scope)
        mnp = self.sb("mnp12", [128, TS], F32, scope)
        zq = self.sb("zq12", [128, TS], F32, scope)

        def v2A(g):
            return self.bigA1[:, g * 512:(g + 1) * 512]

        out_dmas = []
        prev_down_done = None
        for tok in self.GU_TOT:
            self.WD("gpsimd", tok)
        for q4 in range(4):
            HB = q4 * TS
            if prev_down_done is not None:
                W("gpsimd", "stn", prev_down_done)
            v2_in = []
            for g in range(64):
                v2_in.append(dma("gpsimd", "vin", v2A(g)[:, :],
                                 d["u2_d"].ap()[2 * g:2 * g + 2, :, HB:HB + TS]
                                 .rearrange("e b t -> (e b) t")))
            self.WD("tensor", v2_in[-1])
            mx_v, ab_a = {}, {}
            for g in range(64):
                bank = 2 + g % 2
                dst = mnp if g % 2 == 0 else zq
                if g >= 2:
                    W("tensor", "sact", ab_a[g - 2])
                pa = E("tensor", lambda e, g=g, bank=bank: e.matmul(
                    ps[bank][:, :], self.si2h64[:, :], v2A(g)[:, :],
                    start=True, stop=True), "stn")
                W("scalar", "stn", pa)
                if g >= 2:
                    W("scalar", "svec", mx_v[g - 2])
                ab_a[g] = E("scalar", lambda e, bank=bank, dst=dst: e.activation(
                    dst[:, :], ps[bank][:, :], AF.Abs), "sact")
                W("vector", "sact", ab_a[g])
                if g == 0:
                    v = E("vector", lambda e, dst=dst: e.tensor_copy(
                        mxp[:, :], dst[:, :]), "svec")
                else:
                    v = E("vector", lambda e, dst=dst: e.tensor_tensor(
                        mxp[:, :], mxp[:, :], dst[:, :], ALU.max), "svec")
                mx_v[g] = v
            mab = mx_v[63]
            self.xpart_max(mxp, TS, mab, srow_c)
            E("vector", lambda e, HB=HB: e.tensor_tensor(
                srow_c[0:1, 0:TS], srow_c[0:1, 0:TS],
                self.srow_su[0:1, HB:HB + TS], ALU.mult))
            E("vector", lambda e: e.tensor_scalar(
                srow_d[0:1, 0:TS], srow_c[0:1, 0:TS], C8192, EPS,
                ALU.mult, ALU.max))
            rcv = E("vector", lambda e: e.reciprocal(srow_a[0:1, 0:TS],
                                                     srow_d[0:1, 0:TS]), "svec")
            W("vector", "svec", rcv)
            E("vector", lambda e: e.tensor_scalar(
                srow_a[0:1, 0:TS], srow_a[0:1, 0:TS], 127.0 * C8192,
                None, ALU.mult))
            E("vector", lambda e, HB=HB: e.tensor_tensor(
                srow_a[0:1, 0:TS], srow_a[0:1, 0:TS],
                self.srow_su[0:1, HB:HB + TS], ALU.mult))
            s2v = E("vector", lambda e: e.tensor_scalar(
                srow_b[0:1, 0:TS], srow_d[0:1, 0:TS],
                self.wsrow[0:1, 10:11], 1.0 / 127.0, ALU.mult, ALU.mult), "svec")
            bcv = self.bcast_rows([(srow_a, self.bc2, TS),
                                   (srow_b, self.bc1, TS)], s2v)
            W("tensor", "svec", bcv)
            rq = {}
            for g in range(64):
                bank = 2 + g % 2
                if g >= 2:
                    W("tensor", "svec", rq[g - 2])
                pb = E("tensor", lambda e, g=g, bank=bank: e.matmul(
                    ps[bank][:, :], self.si2h64[:, :], v2A(g)[:, :],
                    start=True, stop=True), "stn")
                W("vector", "stn", pb)
                E("vector", lambda e, bank=bank: e.tensor_tensor(
                    zq[:, :], ps[bank][:, :], self.bc2[:, 0:TS], ALU.mult))
                rq[g] = E("vector", lambda e, g=g: e.tensor_scalar(
                    v2A(g)[:, :], zq[:, :], MAGIC, -MAGIC, ALU.add, ALU.add), "svec")
            W("tensor", "svec", rq[63])
            td_in, ep_a, ep_v, mmd = [], {}, {}, {}
            for o2 in range(16):
                if o2 >= 2:
                    W("sync", "stn", mmd[o2 - 2])
                if q4 == 0 and o2 == 0:
                    for tok in self.TDQ_TOT:
                        self.WD("sync", tok)
                td_in.append(dma("sync", f"ts{o2 % 2}", tds[o2 % 2][:, :],
                                 d["tdq_d"].ap()[o2, :, :]))
                if o2 >= 2:
                    W("sync", "svec", ep_v[o2 - 2])
                xrd = dma("sync", f"xs{o2 % 2}", xstr[o2 % 2][:, 0:TS],
                          d["xprime"].ap()[o2 * 128:(o2 + 1) * 128, HB:HB + TS])
                self.WD("tensor", td_in[o2])
                bank = 4 + o2 % 2
                if o2 >= 2:
                    W("tensor", "sact", ep_a[o2 - 2])
                for g in range(64):
                    dmm = E("tensor", lambda e, o2=o2, g=g, bank=bank: e.matmul(
                        ps[bank][:, :], tds[o2 % 2][:, g * 128:(g + 1) * 128],
                        v2A(g)[:, :], start=(g == 0), stop=(g == 63)), "stn")
                mmd[o2] = self.val("stn")
                W("scalar", "stn", dmm)
                if o2 >= 1:
                    W("scalar", "svec", ep_v[o2 - 1])
                da = E("scalar", lambda e, o2=o2, bank=bank: e.activation(
                    self.tmp2[:, 0:TS], ps[bank][:, :], AF.Identity,
                    scale=self.gbcols[:, o2:o2 + 1]), "sact")
                ep_a[o2] = da
                W("vector", "sact", da)
                self.WD("vector", xrd)
                if len(out_dmas) >= 2:
                    self.WD("vector", out_dmas[-2])
                E("vector", lambda e: e.tensor_tensor(
                    self.tmp2[:, 0:TS], self.tmp2[:, 0:TS], self.bc1[:, 0:TS],
                    ALU.mult))
                ep_v[o2] = E("vector", lambda e, o2=o2: e.tensor_tensor(
                    outsb[o2 % 2][:, :], self.tmp2[:, 0:TS],
                    xstr[o2 % 2][:, 0:TS], ALU.add), "svec")
                W("sync", "svec", ep_v[o2])
                out_dmas.append(dma("sync", f"ou{o2 % 2}",
                                    d["out_fm"].ap()[o2 * 128:(o2 + 1) * 128,
                                                     HB:HB + TS],
                                    outsb[o2 % 2][:, :]))
            prev_down_done = self.val("stn")
        self.W("sync", "ou0", self.val("ou0"))
        self.W("sync", "ou1", self.val("ou1"))

    def tap(self, key, tile, rows, cols):
        self.dbg_tiles[key] = (tile, rows, cols)

    def emit_taps(self):
        if not self.dbg:
            return
        self.barrier()
        off = 0
        for key, r, c in self.dbg:
            tile, tr, tc = self.dbg_tiles[key]
            assert r == tr and c == tc, (key, r, c, tr, tc)
            self.dma("gpsimd", "dbg", self.dram["dbg_d"].ap()[0:r, off:off + c],
                     tile[0:r, 0:c])
            off += c
        self.W("gpsimd", "dbg", self.val("dbg"))

    def build(self):
        self.declare()
        self.p0_consts()
        with contextlib.ExitStack() as sc:
            self.p1_mod_ws(sc)
            self.barrier()
        if self.upto >= 3:
            with contextlib.ExitStack() as sc:
                xbf = [self.sb(f"p3b{i}", [128, 2048], BF16, sc) for i in range(2)]
                xsq = [self.sb(f"p3q{i}", [128, 2048], BF16, sc) for i in range(2)]
                xln = self.sb("p3xln", [128, T], F32, sc)
                self.alloc_rows(sc)
                self.alloc_xstr(sc)
                self.ln_stats(self.dram["x_fm"], xbf, xsq)
                self.barrier()
                self.p4_h(self.dram["x_fm"], self.onePca, 0, self.hA, xln)
                self.barrier()
            self.tap("h0", self.hA(0), 128, 2048)
            self.tap("h15", self.hA(15), 128, 2048)
        if self.upto >= 5:
            with contextlib.ExitStack() as sc:
                self.p5_kv(sc)
                self.barrier()
                if self.upto >= 6:
                    self.p6_attention(sc)
                    self.barrier()
                for _ki in range(8):
                    self.tap(f"k{_ki}", self.kfm[_ki], 128, 2048)
                self.tap("v0", self.vaug[0], 128, 520)
        if self.upto >= 10:
            self.sc10 = contextlib.ExitStack()
            self.p10_tdq(self.sc10)
        if self.upto >= 7:
            with contextlib.ExitStack() as sc:
                self.alloc_xstr(sc)
                self.p7_outproj(sc)
                self.barrier()
        if self.upto >= 10:
            self.sc10.close()
        if self.upto >= 8:
            with contextlib.ExitStack() as sc:
                xbf = [self.sb(f"p8b{i}", [128, 2048], BF16, sc) for i in range(2)]
                xsq = [self.sb(f"p8q{i}", [128, 2048], BF16, sc) for i in range(2)]
                self.alloc_rows(sc)
                self.alloc_xstr(sc)
                self.ln_stats(self.dram["xprime"], xbf, xsq)
                self.barrier()
        if self.upto >= 9:
            with contextlib.ExitStack() as sc:
                self.alloc_rows(sc)
                self.alloc_xstr(sc)
                self.p9_h2_fwht(sc)
                self.barrier()
        if self.upto >= 11:
            with contextlib.ExitStack() as sc:
                self.p11_gateup(sc)
                self.barrier()
        if self.upto >= 12:
            with contextlib.ExitStack() as sc:
                self.alloc_rows(sc, TS)
                self.alloc_xstr(sc)
                self.p12_down(sc)
        self.emit_taps()
        self.play()
        self.es.close()
        return self.nc



def _blk4(wT):
    K, M = wT.shape
    ko, mo = K // 128, M // 128
    out = np.ascontiguousarray(
        wT.reshape(ko, 128, mo, 128).transpose(2, 1, 0, 3))
    return np.ascontiguousarray(out.reshape(mo, 128, ko * 128))


def host_prep(inputs):
    x = inputs["L0_attn_x"]
    wq, wk, wv, wo = (inputs["L0_attn_wq"], inputs["L0_attn_wk"],
                      inputs["L0_attn_wv"], inputs["L0_attn_wo"])
    wg, wu, wd = (inputs["L0_ffn_w_gate"], inputs["L0_ffn_w_up"],
                  inputs["L0_ffn_w_down"])
    cc_in, wmod = inputs["L0_c"], inputs["L0_wmod"]

    sh = {}
    sh["wq_b"] = _blk4(np.ascontiguousarray(np.asarray(wq).T))
    wkT = np.ascontiguousarray(np.asarray(wk).T)
    wk_dup = np.concatenate([
        np.concatenate([wkT[:, h * 64:(h + 1) * 64]] * 2, axis=1)
        for h in range(8)], axis=1)
    sh["wk_b"] = _blk4(wk_dup)
    sh["wv_t"] = np.ascontiguousarray(np.asarray(wv).T)
    sh["wo_b"] = _blk4(np.ascontiguousarray(np.asarray(wo).T))
    qq = np.arange(128)
    kk16 = np.arange(16)
    frow = (qq[None, :] % 16) * 128 + 8 * kk16[:, None] + qq[None, :] // 16
    og = np.arange(64)
    cg = np.arange(128)
    row_idx = cg[None, None, None, :] * 64 + og[:, None, None, None]
    col_idx = frow.T[None, :, :, None]
    wg_a, wu_a, wd_a = np.asarray(wg), np.asarray(wu), np.asarray(wd)
    tg4 = wg_a[row_idx, col_idx]
    tu4 = wu_a[row_idx, col_idx]
    sh["tg_b"] = np.ascontiguousarray(tg4.reshape(64, 128, 2048).astype(np.float32))
    sh["tu_b"] = np.ascontiguousarray(tu4.reshape(64, 128, 2048).astype(np.float32))
    sh["td_b"] = _blk4(np.ascontiguousarray(wd_a.T))
    sh["ident_f"] = np.eye(128, dtype=np.float32)
    H128 = _hadamard(128)
    sh["h128_b"] = H128.astype(ml_dtypes.bfloat16)
    sh["i8h16_b"] = np.kron(np.eye(8), _hadamard(16)).astype(ml_dtypes.bfloat16)
    sh["i2h64_b"] = np.kron(np.eye(2), _hadamard(64)).astype(ml_dtypes.bfloat16)
    psw = np.zeros((128, 128), np.float32)
    for i in range(128):
        psw[i, i ^ 1] = 1.0
    sh["pswap_b"] = psw.astype(ml_dtypes.bfloat16)
    i_idx = (np.arange(128) % 64) // 2
    omega = 1.0 / 10000.0 ** (np.arange(0, 64, 2, dtype=np.float64) / 64.0)
    ang = omega[i_idx][:, None] * np.arange(T, dtype=np.float64)[None, :]
    cos_t = np.cos(ang)
    sin_t = np.sin(ang)
    sign = np.where(np.arange(128) % 2 == 0, -1.0, 1.0)[:, None]
    sh["cosq"] = (0.125 * cos_t).astype(np.float32)
    sh["sinq"] = (0.125 * sin_t * sign).astype(np.float32)
    sh["cosk"] = cos_t.astype(np.float32)
    sh["sink"] = (sin_t * sign).astype(np.float32)
    masks = np.zeros((4, 128, TS), np.float32)
    rr = np.arange(128)[:, None]
    ss = np.arange(TS)[None, :]
    for dd in range(4):
        masks[dd] = (ss >= rr + 128 * dd).astype(np.float32)
    sh["masks"] = np.ascontiguousarray(
        masks.transpose(1, 0, 2).reshape(128, 4 * TS)).astype(ml_dtypes.bfloat16)
    ccols = np.zeros((128, 8), np.float32)
    ccols[:, 0] = EXPB
    ccols[:, 1] = EPS
    ccols[:, 2] = 1.0
    ccols[:, 3] = 0.25
    sh["ccols"] = ccols
    sh["ones1f"] = np.ones((1, 128), np.float32)
    sh["onescolb"] = np.ones((128, 1), ml_dtypes.bfloat16)
    c_np = np.asarray(cc_in)
    c_t = np.zeros((128, 128), np.float32)
    for ch in range(16):
        for b in range(8):
            c_t[:, ch * 8 + b] = c_np[b, ch * 128:(ch + 1) * 128]
    sh["c_t"] = c_t

    wmod_np = np.asarray(wmod)
    x_np = np.asarray(x)
    per_core = []
    for i in range(NCORES):
        dd = dict(sh)
        dd["x_fm"] = np.ascontiguousarray(x_np[i].T.astype(np.float32))
        oh = np.zeros((8, 1), np.float32)
        oh[i, 0] = 1.0
        dd["onehot"] = oh
        dd["wmod_sl"] = np.ascontiguousarray(wmod_np[:, i * 1536:(i + 1) * 1536])
        dd["wgs"] = np.ascontiguousarray(sh["tg_b"][8 * i:8 * i + 8])
        dd["wus"] = np.ascontiguousarray(sh["tu_b"][8 * i:8 * i + 8])
        dd["wds"] = np.ascontiguousarray(
            sh["td_b"][2 * i:2 * i + 2].reshape(2, 128, 4, 2048)
            .transpose(0, 2, 1, 3).reshape(8, 128, 2048))
        per_core.append(dd)
    return per_core


def kernel(**inputs):
    in_maps = host_prep(inputs)
    nc = KB().build()
    res = run_bass_kernel_spmd(nc, in_maps, core_ids=list(range(NCORES)))
    out = np.stack([np.asarray(res.results[i]["out_fm"]).T for i in range(NCORES)])
    return np.ascontiguousarray(out).astype(np.float32)


# version 05 (final)
# speedup vs baseline: 2.3190x; 1.0333x over previous
import contextlib
import numpy as np
import ml_dtypes
import concourse.bass as bass
import concourse.mybir as mybir
from concourse.bass_utils import run_bass_kernel_spmd

F32 = mybir.dt.float32
BF16 = mybir.dt.bfloat16
AF = mybir.ActivationFunctionType
ALU = mybir.AluOpType
AX = mybir.AxisListType

NCORES = 8
T = 2048
D = 2048
NK = 16
NTS = 4
TS = 512
NPAIR = 16
MAGIC = 12582912.0
EXPB = -18.0
EPS = 1e-5
C2048 = float(1.0 / np.sqrt(np.float64(2048.0)))
C8192 = float(1.0 / np.sqrt(np.float64(8192.0)))
NELW = 8192.0 * 2048.0

ENGS = ("tensor", "vector", "scalar", "gpsimd", "sync")
NO_CC = False
DBG_WS = False


def _hadamard(n):
    h = np.array([[1.0]])
    while h.shape[0] < n:
        h = np.block([[h, h], [h, -h]])
    return h


class KB:

    def __init__(self, upto=99, dbg=()):
        self.upto = upto
        self.dbg = list(dbg)
        nc = self.nc = bass.Bass("TRN2", target_bir_lowering=False,
                                 num_devices=NCORES)
        self.q = {e: [] for e in ENGS}
        self.es = contextlib.ExitStack()
        self.sems = {}
        for s in ("cc", "cc2", "stn", "svec", "sact", "sgps"):
            self.sems[s] = [self.es.enter_context(nc.semaphore(s)), 0]
        self.dbg_tiles = {}

    FENCE = True
    _ENG_SEM = {"vector": "svec", "scalar": "sact", "gpsimd": "sgps"}

    def E(self, eng, build, sem=None, k=1, nofence=False):
        fence = self.FENCE and k == 1 and eng in self._ENG_SEM and not nofence
        if fence and sem is None:
            sem = self._ENG_SEM[eng]
        v = None
        if sem is not None:
            s = self.sem_of(sem)
            s[1] += k
            v = s[1]
            h = s[0]

            def go(e, build=build, h=h, k=k):
                build(e).then_inc(h, k)
        else:
            def go(e, build=build):
                build(e)
        self.q[eng].append(go)
        if fence:
            self.W(eng, sem, v)
        return v

    def W(self, eng, sem, v):
        if v is None or v <= 0:
            return
        h = self.sem_of(sem)[0]
        self.q[eng].append(lambda e, h=h, v=v: e.wait_ge(h, v))

    def sem_of(self, name):
        if name not in self.sems:
            self.sems[name] = [self.es.enter_context(self.nc.semaphore(name)), 0]
        return self.sems[name]

    def dma(self, eng, stream, dst, src, **kw):
        self.sem_of(stream)
        v = self.E(eng, lambda e, d=dst, s=src, kw=kw: e.dma_start(d, s, **kw),
                   stream, 16)
        return (stream, v)

    def WD(self, eng, tok):
        if tok is None:
            return
        self.W(eng, tok[0], tok[1])

    def val(self, sem):
        return self.sems[sem][1]

    def barrier(self):
        vals = [(s, sv[1]) for s, sv in list(self.sems.items())]
        for eng in ENGS:
            for s, v in vals:
                self.W(eng, s, v)

    _uid = 0

    def sb(self, name, shape, dtype, scope=None):
        KB._uid += 1
        t = (scope or self.es).enter_context(
            self.nc.sbuf_tensor(f"{name}_u{KB._uid}", shape, dtype))
        return t

    def play(self):
        q = self.q
        with self.nc.Block() as blk:
            @blk.tensor
            def _(e):
                for f in q["tensor"]:
                    f(e)

            @blk.vector
            def _(e):
                for f in q["vector"]:
                    f(e)

            @blk.scalar
            def _(e):
                for f in q["scalar"]:
                    f(e)

            @blk.gpsimd
            def _(e):
                for f in q["gpsimd"]:
                    f(e)

            @blk.sync
            def _(e):
                for f in q["sync"]:
                    f(e)

    def declare(self):
        nc = self.nc
        dram = self.dram = {}
        for name, shape in [
                ("x_fm", [D, T]), ("wq_b", [16, 128, 2048]), ("wk_b", [8, 128, 2048]),
                ("wv_t", [2048, 512]), ("wo_b", [16, 128, 2048]),
                ("tg_b", [64, 128, 2048]), ("tu_b", [64, 128, 2048]),
                ("td_b", [16, 128, 8192]), ("wmod_sl", [2048, 1536]),
                ("c_t", [128, 128]), ("onehot", [8, 1]),
                ("wgs", [8, 128, 2048]), ("wus", [8, 128, 2048]), ("wds", [8, 128, 2048]),
                ("ident_f", [128, 128]), ("cosq", [128, T]), ("sinq", [128, T]),
                ("cosk", [128, T]), ("sink", [128, T]), ("ccols", [128, 8]),
                ("ones1f", [1, 128])]:
            dram[name] = nc.dram_tensor(name, shape, F32, kind="ExternalInput")
        for name, shape in [
                ("h128_b", [128, 128]), ("i8h16_b", [128, 128]), ("i2h64_b", [128, 128]),
                ("pswap_b", [128, 128]), ("masks", [128, 4 * TS]),
                ("onescolb", [128, 1])]:
            dram[name] = nc.dram_tensor(name, shape, BF16, kind="ExternalInput")
        dram["out_fm"] = nc.dram_tensor("out_fm", [D, T], F32, kind="ExternalOutput")
        if self.dbg:
            dram["dbg_d"] = nc.dram_tensor("dbg_d", [128, 65536], F32,
                                           kind="ExternalOutput")
        dram["xprime"] = nc.dram_tensor("xprime", [D, T], F32)
        dram["u1_d"] = nc.dram_tensor("u1_d", [128, 16, T], BF16)
        dram["u2_d"] = nc.dram_tensor("u2_d", [128, 64, T], BF16)
        dram["tdq_d"] = nc.dram_tensor("tdq_d", [16, 128, 8192], BF16)
        dram["opair_d"] = nc.dram_tensor("opair_d", [16, 128, T], BF16)
        dram["ag_in"] = nc.dram_tensor("ag_in", [8, 1536], F32)
        dram["ag_out"] = nc.dram_tensor("ag_out", [64, 1536], F32)
        dram["ar_in"] = nc.dram_tensor("ar_in", [1, 4], F32)
        dram["ar_out"] = nc.dram_tensor("ar_out", [1, 4], F32)

        sb = self.sb
        self.bigA1 = sb("bigA1", [128, 32768], BF16)
        self.sid_f = sb("sid_f", [128, 128], F32)
        self.sh128 = sb("sh128", [128, 128], BF16)
        self.si8h16 = sb("si8h16", [128, 128], BF16)
        self.si2h64 = sb("si2h64", [128, 128], BF16)
        self.spswap = sb("spswap", [128, 128], BF16)
        self.sccols = sb("sccols", [128, 8], F32)
        self.sones1 = sb("sones1", [1, 128], F32)
        self.sonescol = sb("sonescol", [128, 1], BF16)
        self.sonehot = sb("sonehot", [8, 1], F32)
        self.mod96 = sb("mod96", [128, 96], F32)
        self.onePca = sb("onePca", [128, 16], F32)
        self.onePcb = sb("onePcb", [128, 16], F32)
        self.gacols = sb("gacols", [128, 16], F32)
        self.gbcols = sb("gbcols", [128, 16], F32)
        self.wsrow = sb("wsrow", [1, 16], F32)
        self.wsg_i = sb("wsg_i", [128, 1], F32)
        self.wsu_i = sb("wsu_i", [128, 1], F32)
        self.wsd_i = sb("wsd_i", [128, 1], F32)
        self.srow_su = sb("srow_su", [1, T], F32)
        self.bc1 = sb("bc1", [128, T], F32)
        self.bc2 = sb("bc2", [128, T], F32)
        self.tmp1 = sb("tmp1", [128, TS], F32)
        self.tmp2 = sb("tmp2", [128, TS], F32)
        self.mcol = sb("mcol", [128, 16], F32)

        self.ps = [self.es.enter_context(self.nc.psum_tensor(f"ps{i}", [128, TS], F32))
                   for i in range(6)]
        self.ps_row = self.es.enter_context(self.nc.psum_tensor("ps_row", [1, TS], F32))
        self.ps_row2 = self.es.enter_context(self.nc.psum_tensor("ps_row2", [1, TS], F32))

    def hA(self, i):
        return self.bigA1[:, i * 2048:(i + 1) * 2048]

    def alloc_xstr(self, scope):
        self.xstr = [self.sb(f"xstr{i}", [128, T], F32, scope) for i in range(2)]

    def alloc_rows(self, scope, width=T):
        self.srow_a = self.sb("srow_a", [1, width], F32, scope)
        self.srow_b = self.sb("srow_b", [1, width], F32, scope)
        self.srow_c = self.sb("srow_c", [1, width], F32, scope)
        self.srow_d = self.sb("srow_d", [1, width], F32, scope)

    def p0_consts(self):
        d = self.dram
        pairs = [(self.sid_f, d["ident_f"]), (self.sh128, d["h128_b"]),
                 (self.si8h16, d["i8h16_b"]), (self.si2h64, d["i2h64_b"]),
                 (self.spswap, d["pswap_b"]), (self.sccols, d["ccols"]),
                 (self.sonehot, d["onehot"]), (self.sonescol, d["onescolb"]),
                 (self.sones1, d["ones1f"])]
        for dst, src in pairs:
            tok = self.dma("sync", "cst", dst[:, :], src.ap())
        self.CONSTS_D = tok

    def p1_mod_ws(self, scope):
        E, W, dma = self.E, self.W, self.dma
        d, ps = self.dram, self.ps
        siluc = self.sb("siluc", [128, 128], F32, scope)
        siluc_b = self.sb("siluc_b", [128, 128], BF16, scope)
        modsl = self.sb("modsl", [8, 1536], F32, scope)
        exsl = [self.sb(f"exsl{i}", [8, TS], F32, scope) for i in range(2)]
        rowbuf = [self.sb(f"rowbuf{i}", [1, TS], F32, scope) for i in range(2)]
        wsacc = self.sb("wsacc", [128, 4], F32, scope)
        wsred = self.sb("wsred", [128, 1], F32, scope)
        if DBG_WS:
            wsdbg = self.sb("wsdbg", [128, 24], F32, scope)
            self.tap("wsdbg", wsdbg, 128, 24)
        gtmp = self.sb("gtmp", [128, 64], F32, scope)
        wstr = [self.sb(f"p1w{i}", [128, 2048], F32, scope) for i in range(2)]
        wmf = [self.sb(f"p1m{i}", [128, 1536], F32, scope) for i in range(2)]
        wmb = [self.sb(f"p1mb{i}", [128, 1536], BF16, scope) for i in range(2)]

        cdone = self.dma("sync", "cst", siluc[:, :], d["c_t"].ap())
        self.WD("scalar", cdone)
        sa = E("scalar", lambda e: e.activation(siluc[:, :], siluc[:, :], AF.Silu),
               "sact")
        W("vector", "sact", sa)
        sg = E("vector", lambda e: e.tensor_copy(siluc_b[:, :], siluc[:, :]), "svec")

        slab_d, slab_v = [], []
        for t24 in range(24):
            m, i = divmod(t24, 8)
            src = (d["wgs"], d["wus"], d["wds"])[m]
            if t24 >= 2:
                W("sync", "svec", slab_v[t24 - 2])
            slab_d.append(dma("sync", f"sw{t24 % 2}", wstr[t24 % 2][:, :],
                              src.ap()[i, :, :]))
            self.WD("vector", slab_d[t24])
            rdv = E("vector", lambda e, b=t24 % 2: e.tensor_reduce(
                wsred[:, :], wstr[b][:, :], AX.X, ALU.add,
                apply_absolute_value=True), "svec")
            W("vector", "svec", rdv)
            if DBG_WS:
                E("vector", lambda e, t24=t24: e.tensor_copy(
                    wsdbg[:, t24:t24 + 1], wsred[:, :]))
            if i == 0:
                slab_v.append(E("vector", lambda e, m=m: e.tensor_copy(
                    wsacc[:, m:m + 1], wsred[:, :]), "svec"))
            else:
                slab_v.append(E("vector", lambda e, m=m: e.tensor_tensor(
                    wsacc[:, m:m + 1], wsacc[:, m:m + 1], wsred[:, :], ALU.add), "svec"))
        self.WD("tensor", self.CONSTS_D)
        W("tensor", "svec", slab_v[-1])
        wst = E("tensor", lambda e: e.matmul(self.ps_row[0:1, 0:4], self.sccols[:, 2:3],
                                             wsacc[:, :], start=True, stop=True), "stn")
        W("vector", "stn", wst)
        wsv = E("vector", lambda e: e.tensor_copy(self.wsrow[0:1, 0:4],
                                                  self.ps_row[0:1, 0:4]), "svec")

        W("tensor", "svec", sg)
        wm_d, wm_v, wm_t = [], [], []
        for k in range(16):
            if k >= 2:
                W("sync", "svec", wm_v[k - 2])
            wm_d.append(dma("sync", f"wm{k % 2}", wmf[k % 2][:, :],
                            d["wmod_sl"].ap()[k * 128:(k + 1) * 128, :]))
            self.WD("vector", wm_d[k])
            if k >= 2:
                W("vector", "stn", wm_t[k - 2])
            wm_v.append(E("vector", lambda e, b=k % 2: e.tensor_copy(
                wmb[b][:, :], wmf[b][:, :]), "svec"))
            W("tensor", "svec", wm_v[k])
            for n3 in range(3):
                v = E("tensor", lambda e, k=k, n3=n3: e.matmul(
                    ps[n3][0:8, :], siluc_b[:, k * 8:(k + 1) * 8],
                    wmb[k % 2][:, n3 * 512:(n3 + 1) * 512],
                    start=(k == 0), stop=(k == 15)), "stn")
            wm_t.append(v)
        W("vector", "stn", wm_t[-1])
        for n3 in range(3):
            mev = E("vector", lambda e, n3=n3: e.tensor_copy(
                modsl[0:8, n3 * 512:(n3 + 1) * 512], ps[n3][0:8, :]), "svec")

        W("gpsimd", "svec", mev)
        W("gpsimd", "svec", wsv)
        dma("gpsimd", "ccb", d["ag_in"].ap(), modsl[0:8, :])
        dma("gpsimd", "ccb", d["ar_in"].ap()[0:1, :], self.wsrow[0:1, 0:4])
        W("gpsimd", "ccb", self.val("ccb"))
        if NO_CC:
            for rr in range(8):
                self.dma("gpsimd", "ccf", d["ag_out"].ap()[rr * 8:rr * 8 + 8, :],
                         d["ag_in"].ap())
            self.dma("gpsimd", "ccf", d["ar_out"].ap(), d["ar_in"].ap())
        else:
            cch, cc2h = self.sems["cc"][0], self.sems["cc2"][0]
            E("gpsimd", lambda e: e.collective_compute(
                "AllGather", ALU.bypass, replica_groups=[list(range(NCORES))],
                ins=[d["ag_in"].ap().opt()],
                outs=[d["ag_out"].ap().opt()]).then_inc(cch), nofence=True)
            E("gpsimd", lambda e: e.collective_compute(
                "AllReduce", ALU.add, replica_groups=[list(range(NCORES))],
                ins=[d["ar_in"].ap().opt()],
                outs=[d["ar_out"].ap().opt()]).then_inc(cc2h), nofence=True)
            self.sems["cc"][1] = 1
            self.sems["cc2"][1] = 1
            W("gpsimd", "cc", 1)
            W("gpsimd", "cc2", 1)
        if NO_CC:
            W("gpsimd", "ccf", self.val("ccf"))
        ccd = dma("gpsimd", "ccb", self.wsrow[0:1, 4:8], d["ar_out"].ap()[0:1, :])

        self.WD("sync", ccd)
        ex_d, ex_t, ex_v, tp_t = [], [], [], []
        for n in range(24):
            if n >= 2:
                W("sync", "stn", ex_t[n - 2])
            r0, js = n // 3, (n % 3) * 512
            ex_d.append(dma("sync", f"ex{n % 2}", exsl[n % 2][0:8, :],
                            d["ag_out"].ap()[r0 * 8:r0 * 8 + 8, js:js + 512]))
            self.WD("tensor", ex_d[n])
            if n >= 2:
                W("tensor", "svec", ex_v[n - 2])
            ex_t.append(E("tensor", lambda e, n=n: e.matmul(
                ps[n % 2][0:1, :], self.sonehot[0:8, 0:1],
                exsl[n % 2][0:8, :], start=True, stop=True), "stn"))
            W("vector", "stn", ex_t[n])
            if n >= 2:
                W("vector", "stn", tp_t[n - 2])
            ex_v.append(E("vector", lambda e, n=n: e.tensor_copy(
                rowbuf[n % 2][0:1, :], ps[n % 2][0:1, :]), "svec"))
            W("tensor", "svec", ex_v[n])
            for j4 in range(4):
                tp = E("tensor", lambda e, n=n, j4=j4: e.transpose(
                    ps[5][:, n * 4 + j4:n * 4 + j4 + 1],
                    rowbuf[n % 2][0:1, j4 * 128:(j4 + 1) * 128],
                    self.sones1[0:1, 0:1]), "stn")
            tp_t.append(tp)
        W("vector", "stn", tp_t[-1])
        E("vector", lambda e: e.tensor_copy(self.mod96[:, :], ps[5][:, 0:96]))
        E("vector", lambda e: e.tensor_scalar(self.onePca[:, :], self.mod96[:, 16:32],
                                              1.0, None, ALU.add))
        E("vector", lambda e: e.tensor_scalar(self.onePcb[:, :], self.mod96[:, 64:80],
                                              1.0, None, ALU.add))
        for dst, off, lo in ((self.gacols, 32, 0), (self.gbcols, 80, 16)):
            E("vector", lambda e, off=off, lo=lo: e.tensor_scalar(
                gtmp[:, lo:lo + 16], self.mod96[:, off:off + 16], 0.0, None, ALU.is_ge))
            E("vector", lambda e, lo=lo: e.tensor_scalar(
                gtmp[:, lo:lo + 16], gtmp[:, lo:lo + 16], 2.0, -1.0, ALU.mult, ALU.add))
            E("vector", lambda e, off=off, lo=lo: e.tensor_scalar(
                gtmp[:, 32 + lo:48 + lo], self.mod96[:, off:off + 16], -1.0, None,
                ALU.mult))
            E("vector", lambda e, off=off, lo=lo: e.tensor_tensor(
                gtmp[:, 32 + lo:48 + lo], gtmp[:, 32 + lo:48 + lo],
                self.mod96[:, off:off + 16], ALU.max))
            E("vector", lambda e, lo=lo: e.tensor_scalar(
                gtmp[:, 32 + lo:48 + lo], gtmp[:, 32 + lo:48 + lo], 0.25, None, ALU.max))
            E("vector", lambda e, dst=dst, lo=lo: e.tensor_tensor(
                dst[:, :], gtmp[:, 32 + lo:48 + lo], gtmp[:, lo:lo + 16], ALU.mult))
        E("vector", lambda e: e.tensor_scalar(self.wsrow[0:1, 8:11], self.wsrow[0:1, 4:7],
                                              1.0 / NELW, EPS, ALU.mult, ALU.max))
        mv = E("vector", lambda e: e.reciprocal(self.wsrow[0:1, 12:15],
                                                self.wsrow[0:1, 8:11]), "svec")
        W("vector", "svec", mv)
        W("tensor", "svec", mv)
        for i in range(3):
            wb = E("tensor", lambda e, i=i: e.matmul(
                ps[4][:, i:i + 1], self.sones1[0:1, :], self.wsrow[0:1, 12 + i:13 + i],
                start=True, stop=True), "stn")
        W("vector", "stn", wb)
        E("vector", lambda e: e.tensor_copy(self.wsg_i[:, :], ps[4][:, 0:1]))
        E("vector", lambda e: e.tensor_copy(self.wsu_i[:, :], ps[4][:, 1:2]))
        E("vector", lambda e: e.tensor_copy(self.wsd_i[:, :], ps[4][:, 2:3]), "svec")
        self.tap("mod96", self.mod96, 128, 96)
        self.tap("wsrow", self.wsrow, 1, 16)
        self.tap("wsacc", wsacc, 128, 4)

    def ln_stats(self, src_dram, xbf, xsq):
        srow_a, srow_b, srow_c, srow_d = (self.srow_a, self.srow_b,
                                          self.srow_c, self.srow_d)
        xstr = self.xstr
        E, W, dma = self.E, self.W, self.dma
        ps = self.ps
        sq_ps = [ps[4][0:1, :], ps[5][0:1, :], self.ps_row[0:1, :], self.ps_row2[0:1, :]]
        cast_v, sq_a, mm_t = [], [], []
        for c in range(NK):
            if c >= 2:
                W("sync", "svec", cast_v[c - 2])
            xd = dma("sync", f"xs{c % 2}", xstr[c % 2][:, :],
                     src_dram.ap()[c * 128:(c + 1) * 128, :])
            self.WD("vector", xd)
            if c >= 2:
                W("vector", "stn", mm_t[c - 2])
            cast_v.append(E("vector", lambda e, b=c % 2: e.tensor_copy(
                xbf[b][:, :], xstr[b][:, :]), "svec"))
            W("scalar", "svec", cast_v[c])
            if c >= 2:
                W("scalar", "stn", mm_t[c - 2])
            sq_a.append(E("scalar", lambda e, b=c % 2: e.activation(
                xsq[b][:, :], xbf[b][:, :], AF.Square), "sact"))
            W("tensor", "sact", sq_a[c])
            for t4 in range(NTS):
                E("tensor", lambda e, c=c, t4=t4: e.matmul(
                    ps[t4][0:1, :], self.sonescol[:, :],
                    xbf[c % 2][:, t4 * TS:(t4 + 1) * TS],
                    start=(c == 0), stop=(c == NK - 1)))
                mm = E("tensor", lambda e, c=c, t4=t4, tgt=sq_ps[t4]: e.matmul(
                    tgt, self.sonescol[:, :], xsq[c % 2][:, t4 * TS:(t4 + 1) * TS],
                    start=(c == 0), stop=(c == NK - 1)), "stn")
            mm_t.append(mm)
        W("vector", "stn", mm_t[-1])
        for t4 in range(NTS):
            E("vector", lambda e, t4=t4: e.tensor_copy(
                srow_a[0:1, t4 * TS:(t4 + 1) * TS], ps[t4][0:1, :]))
            E("vector", lambda e, t4=t4, s=sq_ps[t4]: e.tensor_copy(
                srow_b[0:1, t4 * TS:(t4 + 1) * TS], s))
        E("vector", lambda e: e.tensor_scalar(srow_a[0:1, :], srow_a[0:1, :],
                                              1.0 / D, None, ALU.mult))
        E("vector", lambda e: e.tensor_tensor(srow_c[0:1, :], srow_a[0:1, :],
                                              srow_a[0:1, :], ALU.mult))
        rv = E("vector", lambda e: e.scalar_tensor_tensor(
            srow_b[0:1, :], srow_b[0:1, :], 1.0 / D, srow_c[0:1, :],
            ALU.mult, ALU.subtract), "svec")
        W("scalar", "svec", rv)
        ra = E("scalar", lambda e: e.activation(
            srow_d[0:1, :], srow_b[0:1, :], AF.Sqrt,
            bias=self.sccols[0:1, 1:2]), "sact")
        W("vector", "sact", ra)
        rv2 = E("vector", lambda e: e.reciprocal(srow_b[0:1, :],
                                                 srow_d[0:1, :]), "svec")
        W("vector", "svec", rv2)
        self.bcast_rows([(srow_a, self.bc1, T), (srow_b, self.bc2, T)], rv2)

    def bcast_rows(self, jobs, ready_v):
        E, W = self.E, self.W
        ps = self.ps
        W("tensor", "svec", ready_v)
        i = 0
        last_ev = None
        for row, dstt, width in jobs:
            for t4 in range(width // TS):
                bank = i % 4
                if last_ev is not None and i >= 4:
                    W("tensor", "svec", evs[i - 4])
                mm = E("tensor", lambda e, row=row, t4=t4, bank=bank: e.matmul(
                    ps[bank][:, :], self.sones1[0:1, :],
                    row[0:1, t4 * TS:(t4 + 1) * TS], start=True, stop=True), "stn")
                W("vector", "stn", mm)
                ev = E("vector", lambda e, dstt=dstt, t4=t4, bank=bank: e.tensor_copy(
                    dstt[:, t4 * TS:(t4 + 1) * TS], ps[bank][:, :]), "svec")
                if i == 0:
                    evs = {}
                evs[i] = ev
                last_ev = ev
                i += 1
        return last_ev

    def p4_h(self, src_dram, scale_tile, bias_off, dst_fn, xln):
        xstr = self.xstr
        E, W, dma = self.E, self.W, self.dma
        sub_v, act_a = [], []
        for c in range(NK):
            if c >= 2:
                W("sync", "svec", sub_v[c - 2])
            xd = dma("sync", f"xs{c % 2}", xstr[c % 2][:, :],
                     src_dram.ap()[c * 128:(c + 1) * 128, :])
            self.WD("vector", xd)
            if c >= 1:
                W("vector", "sact", act_a[c - 1])
            E("vector", lambda e, b=c % 2: e.tensor_tensor(
                xln[:, :], xstr[b][:, :], self.bc1[:, :], ALU.subtract))
            sub_v.append(E("vector", lambda e: e.tensor_tensor(
                xln[:, :], xln[:, :], self.bc2[:, :], ALU.mult), "svec"))
            W("scalar", "svec", sub_v[c])
            act_a.append(E("scalar", lambda e, c=c, st=scale_tile, bo=bias_off:
                           e.activation(dst_fn(c)[:, :], xln[:, :], AF.Identity,
                                        scale=st[:, c:c + 1],
                                        bias=self.mod96[:, bo + c:bo + c + 1]), "sact"))
        return act_a[-1]

    def p5_kv(self, scope):
        E, W, dma = self.E, self.W, self.dma
        d, ps = self.dram, self.ps
        self.kfm = [self.sb(f"kfm{i}", [128, T], BF16, scope) for i in range(8)]
        self.vaug = [self.sb(f"vaug{i}", [128, 520], BF16, scope) for i in range(16)]
        self.qpair = self.sb("qpair", [128, T], BF16, scope)
        self.qraw = self.sb("qraw", [128, TS], BF16, scope)
        wv_all = self.sb("wv_all", [128, 8192], BF16, scope)
        sc5 = contextlib.ExitStack()
        rope_ck = self.bc1
        rope_sk = self.bc2
        wstr = [self.sb("p5w0", [128, 2048], F32, sc5)]
        wstr.append(wstr[0])
        wstrb = [self.sb("p5wb0", [128, 2048], BF16, sc5)]
        wstrb.append(wstrb[0])

        ckd = dma("sync", "cst", rope_ck[:, :], d["cosk"].ap())
        skd = dma("sync", "cst", rope_sk[:, :], d["sink"].ap())

        wk_d, wk_g, mm_last = [], [], []
        rope_done = []
        for o in range(8):
            if o >= 1:
                W("sync", "sact", wk_g[o - 1])
            wk_d.append(dma("sync", "wk", wstr[o % 2][:, :], d["wk_b"].ap()[o, :, :]))
            self.WD("scalar", wk_d[o])
            if o >= 1:
                W("scalar", "stn", mm_last[o - 1])
            wk_g.append(E("scalar", lambda e, b=o % 2: e.copy(
                wstrb[b][:, :], wstr[b][:, :]), "sact"))
            W("tensor", "sact", wk_g[o])
            if o == 0:
                self.WD("tensor", skd)
                self.WD("vector", skd)
                swv_prev = None
            for t4 in range(4):
                if t4 >= 2 or o > 0:
                    W("tensor", "svec", rope_done[-2] if len(rope_done) >= 2 else None)
                for k in range(16):
                    mmv = E("tensor", lambda e, o=o, t4=t4, k=k: e.matmul(
                        ps[t4 % 2][:, :], wstrb[o % 2][:, k * 128:(k + 1) * 128],
                        self.hA(k)[:, t4 * TS:(t4 + 1) * TS],
                        start=(k == 0), stop=(k == 15)), "stn")
                W("scalar", "stn", mmv)
                if swv_prev is not None:
                    W("scalar", "stn", swv_prev)
                arv = E("scalar", lambda e, t4=t4: e.activation(
                    self.qraw[:, :], ps[t4 % 2][:, :], AF.Copy), "sact")
                W("tensor", "sact", arv)
                swv_prev = E("tensor", lambda e, t4=t4: e.matmul(
                    ps[2 + t4 % 2][:, :], self.spswap[:, :], self.qraw[:, :],
                    start=True, stop=True), "stn")
                W("vector", "stn", swv_prev)
                E("vector", lambda e, t4=t4: e.tensor_tensor(
                    self.tmp1[:, 0:TS], ps[t4 % 2][:, :],
                    rope_ck[:, t4 * TS:(t4 + 1) * TS], ALU.mult))
                E("vector", lambda e, t4=t4: e.tensor_tensor(
                    self.tmp2[:, 0:TS], ps[2 + t4 % 2][:, :],
                    rope_sk[:, t4 * TS:(t4 + 1) * TS], ALU.mult))
                rope_done.append(E("vector", lambda e, o=o, t4=t4: e.tensor_tensor(
                    self.kfm[o][:, t4 * TS:(t4 + 1) * TS], self.tmp1[:, 0:TS],
                    self.tmp2[:, 0:TS], ALU.add), "svec"))
            mm_last.append(self.val("stn"))
        self.K_DONE = rope_done[-1]

        for i in range(16):
            for h8 in range(8):
                E("gpsimd", lambda e, i=i, h8=h8: e.memset(
                    self.vaug[i][:, h8 * 65 + 64:h8 * 65 + 65], 1.0))
        vones = E("gpsimd", lambda e, i=15: e.tensor_copy(
            self.vaug[15][:, 519:520], self.vaug[15][:, 519:520]), "sgps")
        W("sync", "sact", wk_g[-1])
        for qv4 in range(4):
            if qv4 >= 1:
                W("sync", "sact", wvg)
            wvd = dma("sync", "wv", wstr[0].ap().rearrange("p (a v) -> p a v", a=4),
                      d["wv_t"].ap()[qv4 * 512:(qv4 + 1) * 512, :]
                      .rearrange("(a p) v -> p a v", p=128))
            self.WD("scalar", wvd)
            wvg = E("scalar", lambda e, qv4=qv4: e.copy(
                wv_all[:, qv4 * 2048:(qv4 + 1) * 2048], wstr[0][:, :]), "sact")
        W("tensor", "sact", wvg)
        v_ev = []
        for i in range(16):
            bank = 4 + i % 2
            if i >= 2:
                W("tensor", "svec", v_ev[i - 2])
            for k in range(16):
                vmm = E("tensor", lambda e, i=i, k=k, bank=bank: e.matmul(
                    ps[bank][:, :], self.hA(k)[:, i * 128:(i + 1) * 128],
                    wv_all[:, k * 512:(k + 1) * 512],
                    start=(k == 0), stop=(k == 15)), "stn")
            W("vector", "stn", vmm)
            if i == 0:
                W("vector", "sgps", vones)
            for h8 in range(8):
                vv = E("vector", lambda e, i=i, h8=h8, bank=bank: e.tensor_copy(
                    self.vaug[i][:, h8 * 65:h8 * 65 + 64],
                    ps[bank][:, h8 * 64:(h8 + 1) * 64]), "svec")
            v_ev.append(vv)
        self.V_DONE = v_ev[-1]
        self.barrier()
        sc5.close()

    def p6_attention(self, scope):
        E, W, dma = self.E, self.W, self.dma
        d, ps = self.dram, self.ps
        ptile = [self.sb(f"ptile{i}", [128, TS], BF16, scope) for i in range(4)]
        opevA = self.sb("opevA", [64, T], BF16, scope)
        opevB = self.sb("opevB", [64, T], BF16, scope)
        recrow = self.sb("recrow", [1, TS], F32, scope)
        recrow2 = self.sb("recrow2", [1, TS], F32, scope)
        bcsbA = self.sb("bcsbA", [64, TS], F32, scope)
        bcsbB = self.sb("bcsbB", [64, TS], F32, scope)
        rope_c = self.bc1
        rope_s = self.bc2
        self.smasks = self.sb("smasks", [128, 4 * TS], BF16, scope)
        wstr = [self.sb("p6w0", [128, 2048], F32, scope)]
        wstr.append(wstr[0])
        wstrb = [self.sb("p6wb0", [128, 2048], BF16, scope)]
        wstrb.append(wstrb[0])

        W("sync", "svec", self.K_DONE)
        dma("sync", "cst", self.smasks[:, :], d["masks"].ap())
        cqd = dma("sync", "cst", rope_c[:, :], d["cosq"].ap())
        sqd = dma("sync", "cst", rope_s[:, :], d["sinq"].ap())
        self.WD("tensor", sqd)
        self.WD("vector", sqd)
        W("tensor", "svec", self.K_DONE)
        W("tensor", "svec", self.V_DONE)

        wq_g, q_mm_done, op_dma = [], [], []
        prev_o_last_smm = None
        prev_ov = None
        swv_prev = None
        prev_pv_last = None
        for o in range(NPAIR):
            if o >= 1:
                W("sync", "svec", wq_g[o - 1])
            wqd = dma("sync", "wq", wstr[o % 2][:, :], d["wq_b"].ap()[o, :, :])
            self.WD("vector", wqd)
            if o >= 1:
                W("vector", "stn", q_mm_done[o - 1])
            wq_g.append(E("vector", lambda e, b=o % 2: e.tensor_copy(
                wstrb[b][:, :], wstr[b][:, :]), "svec"))
            W("tensor", "svec", wq_g[o])
            rope_v = []
            for t4 in range(4):
                if prev_ov is not None and t4 == 0:
                    W("tensor", "svec", prev_ov)
                if t4 >= 2:
                    W("tensor", "svec", rope_v[t4 - 2])
                for k in range(16):
                    mmv = E("tensor", lambda e, o=o, t4=t4, k=k: e.matmul(
                        ps[t4 % 2][:, :], wstrb[o % 2][:, k * 128:(k + 1) * 128],
                        self.hA(k)[:, t4 * TS:(t4 + 1) * TS],
                        start=(k == 0), stop=(k == 15)), "stn")
                W("scalar", "stn", mmv)
                if swv_prev is not None:
                    W("scalar", "stn", swv_prev)
                arv = E("scalar", lambda e, t4=t4: e.activation(
                    self.qraw[:, :], ps[t4 % 2][:, :], AF.Copy), "sact")
                W("tensor", "sact", arv)
                swv_prev = E("tensor", lambda e, t4=t4: e.matmul(
                    ps[2 + t4 % 2][:, :], self.spswap[:, :], self.qraw[:, :],
                    start=True, stop=True), "stn")
                W("vector", "stn", swv_prev)
                if prev_o_last_smm is not None and t4 == 0:
                    W("vector", "stn", prev_o_last_smm)
                E("vector", lambda e, t4=t4: e.tensor_tensor(
                    self.tmp1[:, 0:TS], ps[t4 % 2][:, :],
                    rope_c[:, t4 * TS:(t4 + 1) * TS], ALU.mult))
                E("vector", lambda e, t4=t4: e.tensor_tensor(
                    self.tmp2[:, 0:TS], ps[2 + t4 % 2][:, :],
                    rope_s[:, t4 * TS:(t4 + 1) * TS], ALU.mult))
                rope_v.append(E("vector", lambda e, t4=t4: e.tensor_tensor(
                    self.qpair[:, t4 * TS:(t4 + 1) * TS], self.tmp1[:, 0:TS],
                    self.tmp2[:, 0:TS], ALU.add), "svec"))
            q_mm_done.append(self.val("stn"))
            kvch = o // 2
            vgrp = o // 2
            W("tensor", "svec", rope_v[-1])
            ov = None
            for qc in range(NTS):
                njt = 4 * (qc + 1)
                exp_a, pv_t, mk_v = {}, {}, {}
                if ov is not None:
                    W("tensor", "svec", ov)
                if prev_pv_last is not None:
                    W("scalar", "stn", prev_pv_last)
                def emit_pv(loc, j):
                    pti = 2 * (j % 2) + loc
                    if (loc, j) in mk_v:
                        W("tensor", "svec", mk_v[(loc, j)])
                    else:
                        W("tensor", "sact", exp_a[(loc, j)])
                    pv = E("tensor", lambda e, j=j, loc=loc, pti=pti, vgrp=vgrp,
                           njt=njt: e.matmul(
                               ps[loc][0:65, :],
                               self.vaug[j][:, vgrp * 65:vgrp * 65 + 65],
                               ptile[pti][:, :],
                               start=(j == 0), stop=(j == njt - 1)), "stn")
                    pv_t[(loc, j)] = pv

                for j in range(njt):
                    for loc in range(2):
                        sbank = (2 if j % 2 == 0 else 4) + loc
                        pti = 2 * (j % 2) + loc
                        if (loc, j - 2) in exp_a:
                            W("tensor", "sact", exp_a[(loc, j - 2)])
                        smm = E("tensor", lambda e, j=j, qc=qc, loc=loc, sbank=sbank,
                                kvch=kvch: e.matmul(
                                    ps[sbank][:, :],
                                    self.kfm[kvch][loc * 64:loc * 64 + 64,
                                                   j * 128:(j + 1) * 128],
                                    self.qpair[loc * 64:loc * 64 + 64,
                                               qc * TS:(qc + 1) * TS],
                                    start=True, stop=True), "stn")
                        W("scalar", "stn", smm)
                        if (loc, j - 2) in pv_t:
                            W("scalar", "stn", pv_t[(loc, j - 2)])
                        if (loc, j - 2) in mk_v:
                            W("scalar", "svec", mk_v[(loc, j - 2)])
                        ea = E("scalar", lambda e, sbank=sbank, pti=pti: e.activation(
                            ptile[pti][:, :], ps[sbank][:, :], AF.Exp,
                            bias=self.sccols[:, 0:1]), "sact")
                        exp_a[(loc, j)] = ea
                        dmask = j - 4 * qc
                        if dmask >= 0:
                            W("vector", "sact", ea)
                            mv = E("vector", lambda e, pti=pti, dm=dmask: e.tensor_tensor(
                                ptile[pti][:, :], ptile[pti][:, :],
                                self.smasks[:, dm * TS:(dm + 1) * TS], ALU.mult), "svec")
                            mk_v[(loc, j)] = mv
                    if j >= 1:
                        emit_pv(0, j - 1)
                        emit_pv(1, j - 1)
                emit_pv(0, njt - 1)
                emit_pv(1, njt - 1)
                prev_o_last_smm = self.val("stn")
                prev_pv_last = pv_t[(1, njt - 1)]
                W("vector", "stn", pv_t[(1, njt - 1)])
                E("vector", lambda e: e.reciprocal(recrow[0:1, :], ps[0][64:65, :]))
                rv = E("vector", lambda e: e.reciprocal(recrow2[0:1, :],
                                                        ps[1][64:65, :]), "svec")
                W("tensor", "svec", rv)
                E("tensor", lambda e: e.matmul(
                    ps[2][0:64, :], self.sones1[0:1, 0:64], recrow[0:1, :],
                    start=True, stop=True), "stn")
                b2 = E("tensor", lambda e: e.matmul(
                    ps[3][0:64, :], self.sones1[0:1, 0:64], recrow2[0:1, :],
                    start=True, stop=True), "stn")
                W("vector", "stn", b2)
                if op_dma:
                    self.WD("vector", op_dma[-1])
                E("vector", lambda e: e.tensor_copy(bcsbA[:, :], ps[2][0:64, :]))
                E("vector", lambda e: e.tensor_copy(bcsbB[:, :], ps[3][0:64, :]))
                E("vector", lambda e, qc=qc: e.tensor_tensor(
                    opevA[:, qc * TS:(qc + 1) * TS], ps[0][0:64, :], bcsbA[:, :],
                    ALU.mult))
                ov = E("vector", lambda e, qc=qc: e.tensor_tensor(
                    opevB[:, qc * TS:(qc + 1) * TS], ps[1][0:64, :], bcsbB[:, :],
                    ALU.mult), "svec")
            prev_ov = ov
            W("gpsimd", "svec", ov)
            dma("gpsimd", "opo", d["opair_d"].ap()[o, 0:64, :], opevA[:, :])
            op_dma.append(dma("gpsimd", "opo", d["opair_d"].ap()[o, 64:128, :],
                              opevB[:, :]))
        self.ATT_DMA = op_dma[-1]
        self.tap("qpair", self.qpair, 128, 2048)
        self.tap("pt3", ptile[3], 128, 512)
        self.tap("rec2", recrow2, 1, 512)
        self.tap("bcsbB", bcsbB, 64, 512)
        self.tap("opevA", opevA, 64, 2048)
        self.tap("opevB", opevB, 64, 2048)

    def p7_outproj(self, scope):
        xstr = self.xstr
        E, W, dma = self.E, self.W, self.dma
        d, ps = self.dram, self.ps
        wstr = [self.sb(f"p7w{i}", [128, 2048], F32, scope) for i in range(2)]
        wstrb = [self.sb(f"p7wb{i}", [128, 2048], BF16, scope) for i in range(2)]
        xnout = [self.sb(f"xnout{i}", [128, T], F32, scope) for i in range(2)]
        xln = self.sb("p7xln", [128, T], F32, scope)

        self.WD("sync", self.ATT_DMA)
        o_in = []
        for i in range(16):
            o_in.append(dma("sync", "oin", self.hA(i)[:, :], d["opair_d"].ap()[i, :, :]))
        wo_g, xp_dma = [], []
        mm_done = []
        ep_a, ep_v = {}, {}
        for o2 in range(16):
            if o2 >= 2:
                W("sync", "sact", wo_g[o2 - 2])
            wod = dma("sync", f"wo{o2 % 2}", wstr[o2 % 2][:, :], d["wo_b"].ap()[o2, :, :])
            if o2 >= 2:
                W("sync", "svec", ep_v[(o2 - 2, 3)])
            xrd = dma("sync", f"xs{o2 % 2}", xstr[o2 % 2][:, :],
                      d["x_fm"].ap()[o2 * 128:(o2 + 1) * 128, :])
            self.WD("scalar", wod)
            if o2 >= 2:
                W("scalar", "stn", mm_done[o2 - 2])
            wo_g.append(E("scalar", lambda e, b=o2 % 2: e.copy(
                wstrb[b][:, :], wstr[b][:, :]), "sact"))
            W("tensor", "sact", wo_g[o2])
            if o2 == 0:
                self.WD("tensor", o_in[-1])
            for t4 in range(4):
                if (o2, t4 - 2) in ep_a:
                    W("tensor", "sact", ep_a[(o2, t4 - 2)])
                elif o2 >= 1:
                    W("tensor", "sact", ep_a[(o2 - 1, t4 + 2)])
                for k in range(16):
                    mmv = E("tensor", lambda e, o2=o2, t4=t4, k=k: e.matmul(
                        ps[t4 % 2][:, :], wstrb[o2 % 2][:, k * 128:(k + 1) * 128],
                        self.hA(k)[:, t4 * TS:(t4 + 1) * TS],
                        start=(k == 0), stop=(k == 15)), "stn")
                W("scalar", "stn", mmv)
                if (o2 - 1, t4) in ep_v:
                    W("scalar", "svec", ep_v[(o2 - 1, t4)])
                av = E("scalar", lambda e, o2=o2, t4=t4: e.activation(
                    xln[:, t4 * TS:(t4 + 1) * TS], ps[t4 % 2][:, :],
                    AF.Identity, scale=self.gacols[:, o2:o2 + 1]), "sact")
                ep_a[(o2, t4)] = av
                W("vector", "sact", av)
                if t4 == 0:
                    self.WD("vector", xrd)
                    if o2 >= 2:
                        self.WD("vector", xp_dma[o2 - 2])
                ep_v[(o2, t4)] = E("vector", lambda e, o2=o2, t4=t4: e.tensor_tensor(
                    xnout[o2 % 2][:, t4 * TS:(t4 + 1) * TS],
                    xln[:, t4 * TS:(t4 + 1) * TS],
                    xstr[o2 % 2][:, t4 * TS:(t4 + 1) * TS], ALU.add), "svec")
            mm_done.append(self.val("stn"))
            W("sync", "svec", ep_v[(o2, 3)])
            xp_dma.append(dma("sync", f"xp{o2 % 2}",
                              d["xprime"].ap()[o2 * 128:(o2 + 1) * 128, :],
                              xnout[o2 % 2][:, :]))
        self.XP_DMA = xp_dma[-1]
        self.tap("xn15", xnout[1], 128, 2048)
        self.tap("gacols", self.gacols, 128, 16)

    def p9_h2_fwht(self, scope):
        srow_a, srow_b, srow_c, srow_d = (self.srow_a, self.srow_b,
                                          self.srow_c, self.srow_d)
        xstr = self.xstr
        E, W, dma = self.E, self.W, self.dma
        d, ps = self.dram, self.ps
        xbf = [self.sb(f"p9b{i}", [128, 2048], BF16, scope) for i in range(2)]
        u1ev = [self.sb(f"u1ev{i}", [128, TS], BF16, scope) for i in range(2)]
        xln = self.sb("p9xln", [128, T], F32, scope)
        mxp = self.sb("mxp9", [128, T], F32, scope)
        mnp = self.sb("mnp9", [128, T], F32, scope)
        zq = self.sb("zq9", [128, TS], F32, scope)
        srow_sg = self.sb("srow_sg", [1, T], F32, scope)
        self.srow_sg = srow_sg

        sub_v, act_a, st1_t, u1_dma, uev_a = [], [], {}, [], []
        for c in range(NK):
            if c >= 2:
                W("sync", "svec", sub_v[c - 2])
            xd = dma("sync", f"xs{c % 2}", xstr[c % 2][:, :],
                     d["xprime"].ap()[c * 128:(c + 1) * 128, :])
            self.WD("vector", xd)
            if c >= 1:
                W("vector", "sact", act_a[c - 1])
            E("vector", lambda e, b=c % 2: e.tensor_tensor(
                xln[:, :], xstr[b][:, :], self.bc1[:, :], ALU.subtract))
            sub_v.append(E("vector", lambda e: e.tensor_tensor(
                xln[:, :], xln[:, :], self.bc2[:, :], ALU.mult), "svec"))
            W("scalar", "svec", sub_v[c])
            if c >= 2:
                W("scalar", "stn", st1_t[(c - 2, 3)])
            act_a.append(E("scalar", lambda e, c=c: e.activation(
                xbf[c % 2][:, :], xln[:, :], AF.Identity,
                scale=self.onePcb[:, c:c + 1],
                bias=self.mod96[:, 48 + c:49 + c]), "sact"))
            W("tensor", "sact", act_a[c])
            for t4 in range(4):
                it = c * 4 + t4
                if it >= 2:
                    W("tensor", "sact", uev_a[it - 2])
                stv = E("tensor", lambda e, c=c, t4=t4: e.matmul(
                    ps[t4 % 2][:, :], self.sh128[:, :],
                    xbf[c % 2][:, t4 * TS:(t4 + 1) * TS], start=True, stop=True), "stn")
                st1_t[(c, t4)] = stv
                W("scalar", "stn", stv)
                if it >= 2:
                    self.WD("scalar", u1_dma[it - 2])
                ua = E("scalar", lambda e, t4=t4: e.activation(
                    u1ev[t4 % 2][:, :], ps[t4 % 2][:, :], AF.Copy), "sact")
                uev_a.append(ua)
                W("gpsimd", "sact", ua)
                u1_dma.append(dma("gpsimd", f"ub{t4 % 2}",
                                  d["u1_d"].ap()[:, c, t4 * TS:(t4 + 1) * TS],
                                  u1ev[t4 % 2][:, :]))
        W("gpsimd", "ub0", self.val("ub0"))
        W("gpsimd", "ub1", self.val("ub1"))
        v1_in = []
        for g in range(16):
            v1_in.append(dma("gpsimd", "vin", self.hA(g)[:, :],
                             d["u1_d"].ap()[8 * g:8 * g + 8, :, :]
                             .rearrange("e c t -> (e c) t")))
        self.WD("tensor", v1_in[-1])
        mx_v = {}
        for g in range(16):
            for t4 in range(4):
                it = g * 4 + t4
                bank = 2 + it % 2
                if it >= 2:
                    W("tensor", "svec", mx_v[it - 2])
                pa = E("tensor", lambda e, g=g, t4=t4, bank=bank: e.matmul(
                    ps[bank][:, :], self.si8h16[:, :],
                    self.hA(g)[:, t4 * TS:(t4 + 1) * TS], start=True, stop=True), "stn")
                W("vector", "stn", pa)
                if g == 0:
                    E("vector", lambda e, t4=t4, bank=bank: e.tensor_copy(
                        mxp[:, t4 * TS:(t4 + 1) * TS], ps[bank][:, :]))
                    v = E("vector", lambda e, t4=t4, bank=bank: e.tensor_copy(
                        mnp[:, t4 * TS:(t4 + 1) * TS], ps[bank][:, :]), "svec")
                else:
                    E("vector", lambda e, t4=t4, bank=bank: e.tensor_tensor(
                        mxp[:, t4 * TS:(t4 + 1) * TS], mxp[:, t4 * TS:(t4 + 1) * TS],
                        ps[bank][:, :], ALU.max))
                    v = E("vector", lambda e, t4=t4, bank=bank: e.tensor_tensor(
                        mnp[:, t4 * TS:(t4 + 1) * TS], mnp[:, t4 * TS:(t4 + 1) * TS],
                        ps[bank][:, :], ALU.min), "svec")
                mx_v[it] = v
        E("vector", lambda e: e.tensor_scalar(mnp[:, :], mnp[:, :], -1.0, None, ALU.mult))
        mab = E("vector", lambda e: e.tensor_tensor(mxp[:, :], mxp[:, :], mnp[:, :],
                                                    ALU.max), "svec")
        mrow = self.xpart_max(mxp, T, mab, srow_c)
        E("vector", lambda e: e.tensor_scalar(srow_d[0:1, :], srow_c[0:1, :],
                                              C2048, EPS, ALU.mult, ALU.max))
        rcv = E("vector", lambda e: e.reciprocal(srow_a[0:1, :], srow_d[0:1, :]),
                "svec")
        W("vector", "svec", rcv)
        E("vector", lambda e: e.tensor_scalar(srow_a[0:1, :], srow_a[0:1, :],
                                              127.0 * C2048, None, ALU.mult))
        E("vector", lambda e: e.tensor_scalar(
            srow_sg[0:1, :], srow_d[0:1, :], self.wsrow[0:1, 8:9],
            1.0 / 127.0, ALU.mult, ALU.mult))
        srv = E("vector", lambda e: e.tensor_scalar(
            self.srow_su[0:1, :], srow_d[0:1, :], self.wsrow[0:1, 9:10],
            1.0 / 127.0, ALU.mult, ALU.mult), "svec")
        bcv = self.bcast_rows([(srow_a, self.bc2, T), (srow_sg, self.bc1, T)],
                              srv)
        W("tensor", "svec", bcv)
        rq = {}
        for g in range(16):
            for t4 in range(4):
                it = g * 4 + t4
                bank = 2 + it % 2
                if it >= 2:
                    W("tensor", "svec", rq[it - 2])
                pb = E("tensor", lambda e, g=g, t4=t4, bank=bank: e.matmul(
                    ps[bank][:, :], self.si8h16[:, :],
                    self.hA(g)[:, t4 * TS:(t4 + 1) * TS], start=True, stop=True), "stn")
                W("vector", "stn", pb)
                E("vector", lambda e, t4=t4, bank=bank: e.tensor_tensor(
                    zq[:, :], ps[bank][:, :], self.bc2[:, t4 * TS:(t4 + 1) * TS],
                    ALU.mult))
                rq[it] = E("vector", lambda e, g=g, t4=t4: e.tensor_scalar(
                    self.hA(g)[:, t4 * TS:(t4 + 1) * TS], zq[:, :],
                    MAGIC, -MAGIC, ALU.add, ALU.add), "svec")
        self.R_DONE = rq[63]
        self.tap("R0", self.hA(0), 128, 2048)

    def xpart_max(self, mtile, width, ready_v, out_row):
        E, W = self.E, self.W
        ps = self.ps
        W("tensor", "svec", ready_v)
        ns = width // 128
        tc = []
        for s in range(ns):
            bank = 2 + s % 2
            if s >= 2:
                W("tensor", "svec", tc[s - 2])
            tv = E("tensor", lambda e, s=s, bank=bank: e.transpose(
                ps[bank][:, 0:128], mtile[:, s * 128:(s + 1) * 128],
                self.sid_f[:, :]), "stn")
            W("vector", "stn", tv)
            tc.append(E("vector", lambda e, s=s, bank=bank: e.tensor_reduce(
                self.mcol[:, s:s + 1], ps[bank][:, 0:128], AX.X, ALU.max), "svec"))
        W("tensor", "svec", tc[-1])
        rv = []
        for s in range(ns):
            pr = self.ps_row if s % 2 == 0 else self.ps_row2
            if s >= 2:
                W("tensor", "svec", rv[s - 2])
            tv = E("tensor", lambda e, s=s, pr=pr: e.transpose(
                pr[0:1, 0:128], self.mcol[:, s:s + 1], self.sid_f[:, :]), "stn")
            W("vector", "stn", tv)
            rv.append(E("vector", lambda e, s=s, pr=pr, out_row=out_row: e.tensor_copy(
                out_row[0:1, s * 128:(s + 1) * 128], pr[0:1, 0:128]), "svec"))
        return rv[-1]

    def p10_tdq(self, scope):
        E, W, dma = self.E, self.W, self.dma
        d = self.dram
        wstr = [self.sb(f"pAw{i}", [128, 2048], F32, scope) for i in range(2)]
        wstrb = [self.sb(f"pAwb{i}", [128, 2048], BF16, scope) for i in range(2)]
        qt = self.sb("qt10", [128, 2048], F32, scope)
        tq, tdma = [], []
        for i in range(64):
            o2, kq = divmod(i, 4)
            if i >= 2:
                W("sync", "svec", tq[i - 2])
            dd = dma("sync", f"tf{i % 2}", wstr[i % 2][:, :],
                     d["td_b"].ap()[o2, :, kq * 2048:(kq + 1) * 2048])
            self.WD("vector", dd)
            if i >= 2:
                self.WD("vector", tdma[i - 2])
            E("vector", lambda e, b=i % 2: e.tensor_scalar(
                qt[:, :], wstr[b][:, :], self.wsd_i[:, :], None, ALU.mult))
            E("vector", lambda e: e.tensor_scalar(
                qt[:, :], qt[:, :], 1.0, -1.0, ALU.min, ALU.max))
            tq.append(E("vector", lambda e, b=i % 2: e.tensor_scalar(
                wstrb[b][:, :], qt[:, :], MAGIC, -MAGIC, ALU.add, ALU.add), "svec"))
            W("gpsimd", "svec", tq[-1])
            tdma.append(dma("gpsimd", f"tq{i % 2}",
                            d["tdq_d"].ap()[o2, :, kq * 2048:(kq + 1) * 2048],
                            wstrb[i % 2][:, :]))
        self.TDQ_TOT = [("tq0", self.val("tq0")), ("tq1", self.val("tq1"))]

    def p11_gateup(self, scope):
        E, W, dma = self.E, self.W, self.dma
        d, ps = self.dram, self.ps
        wstr = [self.sb(f"pBg{i}", [128, 2048], F32, scope) for i in range(2)]
        wstrb = [self.sb(f"pBgb{i}", [128, 2048], BF16, scope) for i in range(2)]
        wstr2 = [self.sb(f"pBu{i}", [128, 2048], F32, scope) for i in range(2)]
        wstr2b = [self.sb(f"pBub{i}", [128, 2048], BF16, scope) for i in range(2)]
        qt = self.sb("qt11", [128, 4096], F32, scope)
        zt1 = [self.sb(f"zt1_{i}", [128, TS], F32, scope) for i in range(2)]
        zt2 = [self.sb(f"zt2_{i}", [128, TS], F32, scope) for i in range(2)]
        zbf = [self.sb(f"zbf{i}", [128, TS], BF16, scope) for i in range(2)]
        u2ev = [self.sb(f"u2ev{i}", [128, TS], BF16, scope) for i in range(2)]

        gq, gq_a, u2_dma = [], [], []
        z_v, u_a, s1_t = {}, {}, {}
        mm_done = []
        for o in range(64):
            if o >= 2:
                W("sync", "sact", gq_a[o - 2])
            d1 = dma("sync", f"tg{o % 2}", wstr[o % 2][:, :], d["tg_b"].ap()[o, :, :])
            d2 = dma("sync", f"tu{o % 2}", wstr2[o % 2][:, :], d["tu_b"].ap()[o, :, :])
            self.WD("scalar", d1)
            self.WD("scalar", d2)
            if o >= 2:
                W("scalar", "stn", mm_done[o - 2])
            for qx, (src, dst, wsc) in enumerate(
                    ((wstr[o % 2], wstrb[o % 2], self.wsg_i),
                     (wstr2[o % 2], wstr2b[o % 2], self.wsu_i))):
                av = E("scalar", lambda e, src=src, wsc=wsc, qx=qx: e.activation(
                    qt[:, qx * 2048:(qx + 1) * 2048], src[:, :], AF.Identity,
                    scale=wsc[:, :]), "sact")
                W("vector", "sact", av)
                E("vector", lambda e, qx=qx: e.tensor_scalar(
                    qt[:, qx * 2048:(qx + 1) * 2048], qt[:, qx * 2048:(qx + 1) * 2048],
                    1.0, -1.0, ALU.min, ALU.max))
                qv = E("vector", lambda e, dst=dst, qx=qx: e.tensor_scalar(
                    dst[:, :], qt[:, qx * 2048:(qx + 1) * 2048],
                    MAGIC, -MAGIC, ALU.add, ALU.add), "svec")
            gq.append(qv)
            gq_a.append(av)
            W("tensor", "svec", qv)
            def emit_s1(pit):
                po, pt4 = divmod(pit, 4)
                if pit >= 2:
                    W("tensor", "sact", u_a[pit - 2])
                W("tensor", "svec", z_v[pit])
                s1_t[pit] = E("tensor", lambda e, pt4=pt4: e.matmul(
                    ps[4 + pit % 2][:, :], self.sh128[:, :], zbf[pt4 % 2][:, :],
                    start=True, stop=True), "stn")
                W("scalar", "stn", s1_t[pit])
                if pit >= 2:
                    self.WD("scalar", u2_dma[pit - 2])
                u_a[pit] = E("scalar", lambda e, pt4=pt4: e.activation(
                    u2ev[pt4 % 2][:, :], ps[4 + pit % 2][:, :], AF.Copy), "sact")
                W("gpsimd", "sact", u_a[pit])
                u2_dma.append(dma("gpsimd", f"ub{pt4 % 2}",
                                  d["u2_d"].ap()[:, po, pt4 * TS:(pt4 + 1) * TS],
                                  u2ev[pt4 % 2][:, :]))

            for t4 in range(4):
                it = o * 4 + t4
                if it >= 2:
                    W("tensor", "svec", z_v[it - 2])
                for k in range(16):
                    E("tensor", lambda e, o=o, t4=t4, k=k: e.matmul(
                        ps[t4 % 2][:, :], wstrb[o % 2][:, k * 128:(k + 1) * 128],
                        self.hA(k)[:, t4 * TS:(t4 + 1) * TS],
                        start=(k == 0), stop=(k == 15)), "stn")
                for k in range(16):
                    gu = E("tensor", lambda e, o=o, t4=t4, k=k: e.matmul(
                        ps[2 + t4 % 2][:, :], wstr2b[o % 2][:, k * 128:(k + 1) * 128],
                        self.hA(k)[:, t4 * TS:(t4 + 1) * TS],
                        start=(k == 0), stop=(k == 15)), "stn")
                W("vector", "stn", gu)
                if it >= 2:
                    W("vector", "sact", z_a_prev[it - 2])
                E("vector", lambda e, t4=t4: e.tensor_tensor(
                    zt1[t4 % 2][:, :], ps[t4 % 2][:, :],
                    self.bc1[:, t4 * TS:(t4 + 1) * TS], ALU.mult), "svec")
                ztv = self.val("svec")
                W("scalar", "svec", ztv)
                if it >= 2:
                    W("scalar", "svec", z_v[it - 2])
                za = E("scalar", lambda e, t4=t4: e.activation(
                    zt2[t4 % 2][:, :], zt1[t4 % 2][:, :], AF.Silu), "sact")
                if it == 0:
                    z_a_prev = {}
                z_a_prev[it] = za
                W("vector", "sact", za)
                if it >= 2:
                    W("vector", "stn", s1_t.get(it - 2, None) or s1_t[it - 2])
                z_v[it] = E("vector", lambda e, t4=t4: e.tensor_tensor(
                    zbf[t4 % 2][:, :], zt2[t4 % 2][:, :], ps[2 + t4 % 2][:, :],
                    ALU.mult), "svec")
                if it >= 1:
                    emit_s1(it - 1)
            mm_done.append(self.val("stn"))
        emit_s1(255)
        self.GU_TOT = [("ub0", self.val("ub0")), ("ub1", self.val("ub1"))]

    def p12_down(self, scope):
        srow_a, srow_b, srow_c, srow_d = (self.srow_a, self.srow_b,
                                          self.srow_c, self.srow_d)
        xstr = self.xstr
        E, W, dma = self.E, self.W, self.dma
        d, ps = self.dram, self.ps
        tds = [self.sb(f"tds{i}", [128, 8192], BF16, scope) for i in range(2)]
        outsb = [self.sb(f"outsb{i}", [128, TS], F32, scope) for i in range(2)]
        mxp = self.sb("mxp12", [128, TS], F32, scope)
        mnp = self.sb("mnp12", [128, TS], F32, scope)
        zq = self.sb("zq12", [128, TS], F32, scope)

        def v2A(g):
            return self.bigA1[:, g * 512:(g + 1) * 512]

        out_dmas = []
        prev_down_done = None
        for tok in self.GU_TOT:
            self.WD("gpsimd", tok)
        for q4 in range(4):
            HB = q4 * TS
            if prev_down_done is not None:
                W("gpsimd", "stn", prev_down_done)
            v2_in = []
            for g in range(64):
                v2_in.append(dma("gpsimd", "vin", v2A(g)[:, :],
                                 d["u2_d"].ap()[2 * g:2 * g + 2, :, HB:HB + TS]
                                 .rearrange("e b t -> (e b) t")))
            self.WD("tensor", v2_in[-1])
            mx_v, ab_a = {}, {}
            for g in range(64):
                bank = 2 + g % 2
                dst = mnp if g % 2 == 0 else zq
                if g >= 2:
                    W("tensor", "sact", ab_a[g - 2])
                pa = E("tensor", lambda e, g=g, bank=bank: e.matmul(
                    ps[bank][:, :], self.si2h64[:, :], v2A(g)[:, :],
                    start=True, stop=True), "stn")
                W("scalar", "stn", pa)
                if g >= 2:
                    W("scalar", "svec", mx_v[g - 2])
                ab_a[g] = E("scalar", lambda e, bank=bank, dst=dst: e.activation(
                    dst[:, :], ps[bank][:, :], AF.Abs), "sact")
                W("vector", "sact", ab_a[g])
                if g == 0:
                    v = E("vector", lambda e, dst=dst: e.tensor_copy(
                        mxp[:, :], dst[:, :]), "svec")
                else:
                    v = E("vector", lambda e, dst=dst: e.tensor_tensor(
                        mxp[:, :], mxp[:, :], dst[:, :], ALU.max), "svec")
                mx_v[g] = v
            mab = mx_v[63]
            self.xpart_max(mxp, TS, mab, srow_c)
            E("vector", lambda e, HB=HB: e.tensor_tensor(
                srow_c[0:1, 0:TS], srow_c[0:1, 0:TS],
                self.srow_su[0:1, HB:HB + TS], ALU.mult))
            E("vector", lambda e: e.tensor_scalar(
                srow_d[0:1, 0:TS], srow_c[0:1, 0:TS], C8192, EPS,
                ALU.mult, ALU.max))
            rcv = E("vector", lambda e: e.reciprocal(srow_a[0:1, 0:TS],
                                                     srow_d[0:1, 0:TS]), "svec")
            W("vector", "svec", rcv)
            E("vector", lambda e: e.tensor_scalar(
                srow_a[0:1, 0:TS], srow_a[0:1, 0:TS], 127.0 * C8192,
                None, ALU.mult))
            E("vector", lambda e, HB=HB: e.tensor_tensor(
                srow_a[0:1, 0:TS], srow_a[0:1, 0:TS],
                self.srow_su[0:1, HB:HB + TS], ALU.mult))
            s2v = E("vector", lambda e: e.tensor_scalar(
                srow_b[0:1, 0:TS], srow_d[0:1, 0:TS],
                self.wsrow[0:1, 10:11], 1.0 / 127.0, ALU.mult, ALU.mult), "svec")
            bcv = self.bcast_rows([(srow_a, self.bc2, TS),
                                   (srow_b, self.bc1, TS)], s2v)
            W("tensor", "svec", bcv)
            rq = {}
            for g in range(64):
                bank = 2 + g % 2
                if g >= 2:
                    W("tensor", "svec", rq[g - 2])
                pb = E("tensor", lambda e, g=g, bank=bank: e.matmul(
                    ps[bank][:, :], self.si2h64[:, :], v2A(g)[:, :],
                    start=True, stop=True), "stn")
                W("vector", "stn", pb)
                E("vector", lambda e, bank=bank: e.tensor_tensor(
                    zq[:, :], ps[bank][:, :], self.bc2[:, 0:TS], ALU.mult))
                rq[g] = E("vector", lambda e, g=g: e.tensor_scalar(
                    v2A(g)[:, :], zq[:, :], MAGIC, -MAGIC, ALU.add, ALU.add), "svec")
            W("tensor", "svec", rq[63])
            td_in, ep_a, ep_v, mmd = [], {}, {}, {}
            for o2 in range(16):
                if o2 >= 2:
                    W("sync", "stn", mmd[o2 - 2])
                if q4 == 0 and o2 == 0:
                    for tok in self.TDQ_TOT:
                        self.WD("sync", tok)
                td_in.append(dma("sync", f"ts{o2 % 2}", tds[o2 % 2][:, :],
                                 d["tdq_d"].ap()[o2, :, :]))
                if o2 >= 2:
                    W("sync", "svec", ep_v[o2 - 2])
                xrd = dma("sync", f"xs{o2 % 2}", xstr[o2 % 2][:, 0:TS],
                          d["xprime"].ap()[o2 * 128:(o2 + 1) * 128, HB:HB + TS])
                self.WD("tensor", td_in[o2])
                bank = 4 + o2 % 2
                if o2 >= 2:
                    W("tensor", "sact", ep_a[o2 - 2])
                for g in range(64):
                    dmm = E("tensor", lambda e, o2=o2, g=g, bank=bank: e.matmul(
                        ps[bank][:, :], tds[o2 % 2][:, g * 128:(g + 1) * 128],
                        v2A(g)[:, :], start=(g == 0), stop=(g == 63)), "stn")
                mmd[o2] = self.val("stn")
                W("scalar", "stn", dmm)
                if o2 >= 1:
                    W("scalar", "svec", ep_v[o2 - 1])
                da = E("scalar", lambda e, o2=o2, bank=bank: e.activation(
                    self.tmp2[:, 0:TS], ps[bank][:, :], AF.Identity,
                    scale=self.gbcols[:, o2:o2 + 1]), "sact")
                ep_a[o2] = da
                W("vector", "sact", da)
                self.WD("vector", xrd)
                if len(out_dmas) >= 2:
                    self.WD("vector", out_dmas[-2])
                E("vector", lambda e: e.tensor_tensor(
                    self.tmp2[:, 0:TS], self.tmp2[:, 0:TS], self.bc1[:, 0:TS],
                    ALU.mult))
                ep_v[o2] = E("vector", lambda e, o2=o2: e.tensor_tensor(
                    outsb[o2 % 2][:, :], self.tmp2[:, 0:TS],
                    xstr[o2 % 2][:, 0:TS], ALU.add), "svec")
                W("sync", "svec", ep_v[o2])
                out_dmas.append(dma("sync", f"ou{o2 % 2}",
                                    d["out_fm"].ap()[o2 * 128:(o2 + 1) * 128,
                                                     HB:HB + TS],
                                    outsb[o2 % 2][:, :]))
            prev_down_done = self.val("stn")
        self.W("sync", "ou0", self.val("ou0"))
        self.W("sync", "ou1", self.val("ou1"))

    def tap(self, key, tile, rows, cols):
        self.dbg_tiles[key] = (tile, rows, cols)

    def emit_taps(self):
        if not self.dbg:
            return
        self.barrier()
        off = 0
        for key, r, c in self.dbg:
            tile, tr, tc = self.dbg_tiles[key]
            assert r == tr and c == tc, (key, r, c, tr, tc)
            self.dma("gpsimd", "dbg", self.dram["dbg_d"].ap()[0:r, off:off + c],
                     tile[0:r, 0:c])
            off += c
        self.W("gpsimd", "dbg", self.val("dbg"))

    def build(self):
        self.declare()
        self.p0_consts()
        with contextlib.ExitStack() as sc:
            self.p1_mod_ws(sc)
            self.barrier()
        if self.upto >= 3:
            with contextlib.ExitStack() as sc:
                xbf = [self.sb(f"p3b{i}", [128, 2048], BF16, sc) for i in range(2)]
                xsq = [self.sb(f"p3q{i}", [128, 2048], BF16, sc) for i in range(2)]
                xln = self.sb("p3xln", [128, T], F32, sc)
                self.alloc_rows(sc)
                self.alloc_xstr(sc)
                self.ln_stats(self.dram["x_fm"], xbf, xsq)
                self.barrier()
                self.p4_h(self.dram["x_fm"], self.onePca, 0, self.hA, xln)
                self.barrier()
            self.tap("h0", self.hA(0), 128, 2048)
            self.tap("h15", self.hA(15), 128, 2048)
        if self.upto >= 5:
            with contextlib.ExitStack() as sc:
                self.p5_kv(sc)
                self.barrier()
                if self.upto >= 6:
                    self.p6_attention(sc)
                    self.barrier()
                for _ki in range(8):
                    self.tap(f"k{_ki}", self.kfm[_ki], 128, 2048)
                self.tap("v0", self.vaug[0], 128, 520)
        if self.upto >= 10:
            self.sc10 = contextlib.ExitStack()
            self.p10_tdq(self.sc10)
        if self.upto >= 7:
            with contextlib.ExitStack() as sc:
                self.alloc_xstr(sc)
                self.p7_outproj(sc)
                self.barrier()
        if self.upto >= 10:
            self.sc10.close()
        if self.upto >= 8:
            with contextlib.ExitStack() as sc:
                xbf = [self.sb(f"p8b{i}", [128, 2048], BF16, sc) for i in range(2)]
                xsq = [self.sb(f"p8q{i}", [128, 2048], BF16, sc) for i in range(2)]
                self.alloc_rows(sc)
                self.alloc_xstr(sc)
                self.ln_stats(self.dram["xprime"], xbf, xsq)
                self.barrier()
        if self.upto >= 9:
            with contextlib.ExitStack() as sc:
                self.alloc_rows(sc)
                self.alloc_xstr(sc)
                self.p9_h2_fwht(sc)
                self.barrier()
        if self.upto >= 11:
            with contextlib.ExitStack() as sc:
                self.p11_gateup(sc)
                self.barrier()
        if self.upto >= 12:
            with contextlib.ExitStack() as sc:
                self.alloc_rows(sc, TS)
                self.alloc_xstr(sc)
                self.p12_down(sc)
        self.emit_taps()
        self.play()
        self.es.close()
        return self.nc



def _blk4(wT):
    K, M = wT.shape
    ko, mo = K // 128, M // 128
    out = np.ascontiguousarray(
        wT.reshape(ko, 128, mo, 128).transpose(2, 1, 0, 3))
    return np.ascontiguousarray(out.reshape(mo, 128, ko * 128))


def host_prep(inputs):
    x = inputs["L0_attn_x"]
    wq, wk, wv, wo = (inputs["L0_attn_wq"], inputs["L0_attn_wk"],
                      inputs["L0_attn_wv"], inputs["L0_attn_wo"])
    wg, wu, wd = (inputs["L0_ffn_w_gate"], inputs["L0_ffn_w_up"],
                  inputs["L0_ffn_w_down"])
    cc_in, wmod = inputs["L0_c"], inputs["L0_wmod"]

    sh = {}
    sh["wq_b"] = _blk4(np.ascontiguousarray(np.asarray(wq).T))
    wkT = np.ascontiguousarray(np.asarray(wk).T)
    wk_dup = np.concatenate([
        np.concatenate([wkT[:, h * 64:(h + 1) * 64]] * 2, axis=1)
        for h in range(8)], axis=1)
    sh["wk_b"] = _blk4(wk_dup)
    sh["wv_t"] = np.ascontiguousarray(np.asarray(wv).T)
    sh["wo_b"] = _blk4(np.ascontiguousarray(np.asarray(wo).T))
    qq = np.arange(128)
    kk16 = np.arange(16)
    frow = (qq[None, :] % 16) * 128 + 8 * kk16[:, None] + qq[None, :] // 16
    og = np.arange(64)
    cg = np.arange(128)
    row_idx = cg[None, None, None, :] * 64 + og[:, None, None, None]
    col_idx = frow.T[None, :, :, None]
    wg_a, wu_a, wd_a = np.asarray(wg), np.asarray(wu), np.asarray(wd)
    tg4 = wg_a[row_idx, col_idx]
    tu4 = wu_a[row_idx, col_idx]
    sh["tg_b"] = np.ascontiguousarray(tg4.reshape(64, 128, 2048).astype(np.float32))
    sh["tu_b"] = np.ascontiguousarray(tu4.reshape(64, 128, 2048).astype(np.float32))
    sh["td_b"] = _blk4(np.ascontiguousarray(wd_a.T))
    sh["ident_f"] = np.eye(128, dtype=np.float32)
    H128 = _hadamard(128)
    sh["h128_b"] = H128.astype(ml_dtypes.bfloat16)
    sh["i8h16_b"] = np.kron(np.eye(8), _hadamard(16)).astype(ml_dtypes.bfloat16)
    sh["i2h64_b"] = np.kron(np.eye(2), _hadamard(64)).astype(ml_dtypes.bfloat16)
    psw = np.zeros((128, 128), np.float32)
    for i in range(128):
        psw[i, i ^ 1] = 1.0
    sh["pswap_b"] = psw.astype(ml_dtypes.bfloat16)
    i_idx = (np.arange(128) % 64) // 2
    omega = 1.0 / 10000.0 ** (np.arange(0, 64, 2, dtype=np.float64) / 64.0)
    ang = omega[i_idx][:, None] * np.arange(T, dtype=np.float64)[None, :]
    cos_t = np.cos(ang)
    sin_t = np.sin(ang)
    sign = np.where(np.arange(128) % 2 == 0, -1.0, 1.0)[:, None]
    sh["cosq"] = (0.125 * cos_t).astype(np.float32)
    sh["sinq"] = (0.125 * sin_t * sign).astype(np.float32)
    sh["cosk"] = cos_t.astype(np.float32)
    sh["sink"] = (sin_t * sign).astype(np.float32)
    masks = np.zeros((4, 128, TS), np.float32)
    rr = np.arange(128)[:, None]
    ss = np.arange(TS)[None, :]
    for dd in range(4):
        masks[dd] = (ss >= rr + 128 * dd).astype(np.float32)
    sh["masks"] = np.ascontiguousarray(
        masks.transpose(1, 0, 2).reshape(128, 4 * TS)).astype(ml_dtypes.bfloat16)
    ccols = np.zeros((128, 8), np.float32)
    ccols[:, 0] = EXPB
    ccols[:, 1] = EPS
    ccols[:, 2] = 1.0
    ccols[:, 3] = 0.25
    sh["ccols"] = ccols
    sh["ones1f"] = np.ones((1, 128), np.float32)
    sh["onescolb"] = np.ones((128, 1), ml_dtypes.bfloat16)
    c_np = np.asarray(cc_in)
    c_t = np.zeros((128, 128), np.float32)
    for ch in range(16):
        for b in range(8):
            c_t[:, ch * 8 + b] = c_np[b, ch * 128:(ch + 1) * 128]
    sh["c_t"] = c_t

    wmod_np = np.asarray(wmod)
    x_np = np.asarray(x)
    per_core = []
    for i in range(NCORES):
        dd = dict(sh)
        dd["x_fm"] = np.ascontiguousarray(x_np[i].T.astype(np.float32))
        oh = np.zeros((8, 1), np.float32)
        oh[i, 0] = 1.0
        dd["onehot"] = oh
        dd["wmod_sl"] = np.ascontiguousarray(wmod_np[:, i * 1536:(i + 1) * 1536])
        dd["wgs"] = np.ascontiguousarray(sh["tg_b"][8 * i:8 * i + 8])
        dd["wus"] = np.ascontiguousarray(sh["tu_b"][8 * i:8 * i + 8])
        dd["wds"] = np.ascontiguousarray(
            sh["td_b"][2 * i:2 * i + 2].reshape(2, 128, 4, 2048)
            .transpose(0, 2, 1, 3).reshape(8, 128, 2048))
        per_core.append(dd)
    return per_core


def kernel(**inputs):
    in_maps = host_prep(inputs)
    nc = KB().build()
    res = run_bass_kernel_spmd(nc, in_maps, core_ids=list(range(NCORES)))
    out = np.stack([np.asarray(res.results[i]["out_fm"]).T for i in range(NCORES)])
    return np.ascontiguousarray(out).astype(np.float32)
